# version 36
# baseline (speedup 1.0000x reference)
import numpy as np
import ml_dtypes
from contextlib import ExitStack

import concourse.bass as bass
import concourse.mybir as mybir
from concourse.bass_utils import run_bass_kernel_spmd

F32 = mybir.dt.float32
BF16 = mybir.dt.bfloat16
AF = mybir.ActivationFunctionType
ALU = mybir.AluOpType

NCORES = 8
T = 2048
D = 1024
S = 16384
FF = 2816
NT = T // 128
ND = D // 128
NF = FF // 128
NG = 32
EPS = 1e-5
SM_SCALE = 1.0 / 32.0
SM_SHIFT = -32.0

ENGINES = ("pe", "act", "dve", "gp", "sync")
SEM_OF = {"pe": "s_pe", "act": "s_act", "dve": "s_dve", "gp": "s_gp", "sync": "s_sy"}
DMA_SEMS = ("s_ld", "s_gd", "s_st")
SEM_NAMES = (["s_pe", "s_act", "s_dve", "s_gp", "s_sy", "s_cc",
              "sid", "swq", "sbq", "sqf0", "sqf1", "sstf",
              "sqt0", "sqt1", "sqt2", "sqt3", "sod", "soi0", "soi1",
              "sgk0", "sgk1", "sgv0", "sgv1", "sgv2", "sod2",
              "swc1", "swc3", "swc2",
              "ssp0", "ssp1", "ssp2", "ssp3", "so0", "so1"]
             + [f"sxt{j}" for j in range(8)] + [f"sxi{j}" for j in range(2)]
             + [f"sxw{j}" for j in range(2)] + [f"skb{j}" for j in range(2)]
             + [f"svb{j}" for j in range(3)] + [f"sw1{j}" for j in range(2)]
             + [f"sw3{j}" for j in range(2)] + [f"sw2{j}" for j in range(4)])


class Emitter:

    def __init__(self):
        self.cnt = {s: 0 for s in SEM_NAMES}
        self.blocks = []
        self.prog = None
        self.last_w = {}
        self.readers = {}
        self.pending_reads = {e: [] for e in ENGINES}
        self.pending_writes = {e: [] for e in ENGINES}
        self.barrier = {}

    def new_block(self):
        self.prog = {e: [] for e in ENGINES}
        self.blocks.append(self.prog)

    def set_barrier(self, exclude=()):
        self.barrier = {s: v for s, v in self.cnt.items() if v and s not in exclude}

    def op(self, eng, fn, reads=(), writes=(), sem=None, dma=False, inc=True):
        reads = list(reads)
        writes = list(writes)
        waits = {}

        def merge(m):
            if m is not None:
                waits[m[0]] = max(waits.get(m[0], 0), m[1])

        for b in reads:
            merge(self.last_w.get(b))
        for b in writes:
            for m in self.readers.get(b, ()):
                merge(m)
            merge(self.last_w.get(b))

        def scoped(b):
            k = b[0] if isinstance(b, tuple) else b
            return not (k.startswith("g:") or k.startswith("d:"))

        if any(scoped(b) for b in reads + writes):
            for s, v in self.barrier.items():
                waits[s] = max(waits.get(s, 0), v)
        mile = None
        amt = 16 if dma else 1
        if inc:
            s = sem or SEM_OF[eng]
            self.cnt[s] += amt
            mile = (s, self.cnt[s])
        self.prog[eng].append((waits, fn, mile, amt))
        if mile is not None:
            for b in self.pending_reads[eng]:
                self.readers.setdefault(b, []).append(mile)
            self.pending_reads[eng] = []
            for b in self.pending_writes[eng]:
                self.last_w[b] = mile
                self.readers[b] = []
            self.pending_writes[eng] = []
            for b in reads:
                self.readers.setdefault(b, []).append(mile)
            for b in writes:
                self.last_w[b] = mile
                self.readers[b] = []
        else:
            self.pending_reads[eng].extend(reads)
            self.pending_writes[eng].extend(writes)
        return mile


def build_program():
    nc = bass.Bass("TRN2", target_bir_lowering=False)

    x_h = nc.dram_tensor("x", [T, D], F32, kind="ExternalInput")
    wq_h = nc.dram_tensor("wq", [D, D], F32, kind="ExternalInput")
    bq_h = nc.dram_tensor("bq", [128, ND], F32, kind="ExternalInput")
    w1_h = nc.dram_tensor("w1", [D, FF], F32, kind="ExternalInput")
    w3_h = nc.dram_tensor("w3", [D, FF], F32, kind="ExternalInput")
    w2_h = nc.dram_tensor("w2", [FF, D], F32, kind="ExternalInput")
    id_h = nc.dram_tensor("ident", [128, 128], BF16, kind="ExternalInput")
    out_h = nc.dram_tensor("out", [T, D], F32, kind="ExternalOutput")

    qq_loc = [nc.dram_tensor(f"qq_loc{k}", [256, 4096], BF16) for k in range(4)]
    g_q = [nc.dram_tensor(f"g_q{k}", [NCORES * 256, 4096], BF16,
                          addr_space="Shared") for k in range(4)]
    xsp = nc.dram_tensor("xsp", [T, D], F32)
    odr = nc.dram_tensor("odr", [16 * 128, D], F32)
    w1b = nc.dram_tensor("w1b", [128, ND * FF], BF16)
    w3b = nc.dram_tensor("w3b", [128, ND * FF], BF16)
    w2c = nc.dram_tensor("w2c", [128, NF * D], BF16)

    em = Emitter()

    with ExitStack() as G:
        eg = G.enter_context
        sems = {s: eg(nc.semaphore(s)) for s in SEM_NAMES}

        ident = eg(nc.sbuf_tensor("ident_sb", [128, 128], BF16))
        ones = eg(nc.sbuf_tensor("ones", [128, 1], BF16))
        onesf = eg(nc.sbuf_tensor("onesf", [1, 1], F32))
        bqs = eg(nc.sbuf_tensor("bqs", [128, ND], F32))
        lrcp = eg(nc.sbuf_tensor("lrcp", [128, 16], F32))
        epsb = eg(nc.sbuf_tensor("epsb", [128, 1], F32))
        shb = eg(nc.sbuf_tensor("shb", [128, 1], F32))
        ln2T = eg(nc.sbuf_tensor("ln2T", [128, ND * T], BF16))
        w1s = [eg(nc.sbuf_tensor(f"w1s{j}", [128, ND * 128], BF16)) for j in range(2)]
        w3s = [eg(nc.sbuf_tensor(f"w3s{j}", [128, ND * 128], BF16)) for j in range(2)]

        def emit_w13_cast():
            for (dst, src, key, sem) in ((w1b, w1_h, "d:w1b", "swc1"),
                                         (w3b, w3_h, "d:w3b", "swc3")):
                def f(g, dst=dst, src=src):
                    i3 = src.ap().rearrange("(hh p) F -> p hh F", p=128)
                    o3 = dst.ap().rearrange("p (hh F) -> p hh F", hh=ND)
                    return g.dma_start(out=o3, in_=i3)
                em.op("gp", f, writes=[key], sem=sem, dma=True)

            def f2(g):
                i3 = w2_h.ap().rearrange("(ff p) d -> p ff d", p=128)
                o3 = w2c.ap().rearrange("p (ff d) -> p ff d", ff=NF)
                return g.dma_start(out=o3, in_=i3)
            em.op("gp", f2, writes=["d:w2c"], sem="swc2", dma=True)

        def emit_w13_load(gidx):
            ff = gidx % NF
            j = gidx % 2
            for (dst, src, key, sem) in ((w1s, w1b, "d:w1b", "sw1"),
                                         (w3s, w3b, "d:w3b", "sw3")):
                def f(g, dst=dst, src=src, ff=ff, j=j):
                    i3 = src.ap().rearrange("p (hh F) -> p hh F", hh=ND)[
                        :, :, ff * 128:(ff + 1) * 128]
                    o3 = dst[j][:, :].rearrange("p (hh f) -> p hh f", hh=ND)
                    return g.dma_start(out=o3, in_=i3)
                em.op("sync", f, reads=[key],
                      writes=[("g:w1s" if sem == "sw1" else "g:w3s", j)],
                      sem=sem + str(j), dma=True)

        P01 = ExitStack()
        qfm = P01.enter_context(nc.sbuf_tensor("qfm", [128, ND * T], BF16))
        kb = [P01.enter_context(nc.sbuf_tensor(f"kb{j}", [128, ND * 512], BF16))
              for j in range(2)]
        vb = [P01.enter_context(nc.sbuf_tensor(f"vb{j}", [128, 4 * D], BF16))
              for j in range(3)]

        em.new_block()
        with ExitStack() as P0:
            e0 = P0.enter_context
            xt = [e0(nc.sbuf_tensor(f"xt{j}", [128, D], F32)) for j in range(6)]
            bns = [e0(nc.sbuf_tensor(f"bns{j}", [128, 12], F32)) for j in range(4)]
            mv = [e0(nc.sbuf_tensor(f"mv{j}", [128, 2], F32)) for j in range(4)]
            sd = [e0(nc.sbuf_tensor(f"sd{j}", [128, 1], F32)) for j in range(4)]
            rs = [e0(nc.sbuf_tensor(f"rs{j}", [128, 1], F32)) for j in range(4)]
            ln1 = [e0(nc.sbuf_tensor(f"ln1_{j}", [128, D], BF16)) for j in range(4)]
            ln1T = e0(nc.sbuf_tensor("ln1T", [128, ND * T], BF16))
            wqb = e0(nc.sbuf_tensor("wqb", [128, ND * D], BF16))
            qtm = [e0(nc.sbuf_tensor(f"qtm{j}", [128, D], BF16)) for j in range(4)]
            ptr = [e0(nc.psum_tensor(f"ptr{j}", [128, D], BF16)) for j in range(2)]
            psq = [e0(nc.psum_tensor(f"psq{j}", [128, 512], F32)) for j in range(2)]

            em.op("sync", lambda s: s.dma_start(out=ident[:, :], in_=id_h[:, :]),
                  writes=["g:ident"], sem="sid", dma=True)
            em.op("dve", lambda v: v.memset(ones[:, :], 1.0), writes=["g:ones"])
            em.op("dve", lambda v: v.memset(onesf[:, :], 1.0), writes=["g:onesf"])
            em.op("dve", lambda v: v.memset(epsb[:, :], EPS), writes=["g:epsb"])
            em.op("dve", lambda v: v.memset(shb[:, :], SM_SHIFT), writes=["g:shb"])
            em.op("gp", lambda g: g.dma_start(out=bqs[:, :], in_=bq_h[:, :]),
                  writes=["g:bq"], sem="sbq", dma=True)

            def f_wq(g):
                i3 = wq_h.ap().rearrange("(hh p) d -> p hh d", p=128)
                o3 = wqb[:, :].rearrange("p (hh d) -> p hh d", hh=ND)
                return g.dma_start(out=o3, in_=i3)
            em.op("gp", f_wq, writes=["wqb"], sem="swq", dma=True)

            def emit_ag(tq, extra=()):
                def f_ag(g, tq=tq):
                    return g.collective_compute(
                        "AllGather", ALU.bypass,
                        replica_groups=[list(range(NCORES))],
                        ins=[qq_loc[tq].ap().opt()], outs=[g_q[tq].ap().opt()])
                em.op("gp", f_ag,
                      reads=[("d:qtml", i_) for i_ in range(tq * 4, tq * 4 + 4)]
                      + [("d:qfml", tq)] + list(extra),
                      writes=[("d:gq", tq)], sem="s_cc")

            for half in range(2):
                for w in (0, 4):
                    ws = [half * 8 + w + k for k in range(4)]
                    for i in ws:
                        j8 = i % 6

                        def f_load(s, i=i, j8=j8):
                            return s.dma_start(out=xt[j8][:, :], in_=x_h[i * 128:(i + 1) * 128, :])
                        em.op("sync" if i % 2 == 0 else "act", f_load,
                              writes=[("xt", j8)], sem=f"sxt{j8}", dma=True)
                    for i in ws:
                        j = i % 4
                        j8 = i % 6
                        em.op("dve", lambda v, j=j, j8=j8: v.bn_stats(
                            out=bns[j][:, 0:6], in_=xt[j8][:, 0:512]),
                            reads=[("xt", j8)], writes=[("bns", j)], inc=False)
                        em.op("dve", lambda v, j=j, j8=j8: v.bn_stats(
                            out=bns[j][:, 6:12], in_=xt[j8][:, 512:1024]),
                            reads=[("xt", j8)], writes=[("bns", j)])
                    for i in ws:
                        j = i % 4
                        em.op("dve", lambda v, j=j: v.bn_aggr(out=mv[j][:, :], in_=bns[j][:, :]),
                              reads=[("bns", j)], writes=[("mv", j)])
                    for i in ws:
                        j = i % 4
                        em.op("act", lambda s, j=j: s.activation(out=sd[j][:, :], in_=mv[j][:, 1:2],
                                                                 func=AF.Sqrt, bias=epsb[:, :]),
                              reads=[("mv", j), "g:epsb"], writes=[("sd", j)])
                    for i in ws:
                        j = i % 4
                        em.op("dve", lambda v, j=j: v.reciprocal(out=rs[j][:, :], in_=sd[j][:, :]),
                              reads=[("sd", j)], writes=[("rs", j)])
                    for i in ws:
                        j = i % 4
                        j8 = i % 6
                        em.op("dve", lambda v, j=j, j8=j8: v.tensor_scalar(
                            out=ln1[j][:, :], in0=xt[j8][:, :], scalar1=mv[j][:, 0:1],
                            scalar2=rs[j][:, :], op0=ALU.subtract, op1=ALU.mult),
                            reads=[("xt", j8), ("mv", j), ("rs", j)], writes=[("ln1", j)])
                    for i in ws:
                        j = i % 4
                        jp = i % 2

                        def f_tr(t, j=j, jp=jp):
                            r = None
                            for hh in range(ND):
                                r = t.transpose(out=ptr[jp][:, hh * 128:(hh + 1) * 128],
                                                in_=ln1[j][:, hh * 128:(hh + 1) * 128],
                                                identity=ident[:, :])
                            return r
                        em.op("pe", f_tr, reads=[("ln1", j), "g:ident"], writes=[("ptr", jp)])

                        def f_cp(v, i=i, jp=jp):
                            o3 = ln1T[:, :].rearrange("p (hh t) -> p hh t", hh=ND)[:, :, i * 128:(i + 1) * 128]
                            i3 = ptr[jp][:, :].rearrange("p (hh t) -> p hh t", hh=ND)
                            return v.tensor_copy(out=o3, in_=i3)
                        em.op("dve", f_cp, reads=[("ptr", jp)],
                              writes=[("ln1T", hh, i) for hh in range(ND)])

                for tq in (2 * half, 2 * half + 1):
                    for dd in range(ND):
                        jq = (tq * ND + dd) % 2

                        def f_mm(t, dd=dd, tq=tq, jq=jq):
                            r = None
                            for hh in range(ND):
                                r = t.matmul(out=psq[jq][:, :],
                                             lhsT=wqb[:, hh * D + dd * 128: hh * D + (dd + 1) * 128],
                                             rhs=ln1T[:, hh * T + tq * 512: hh * T + (tq + 1) * 512],
                                             start=(hh == 0), stop=(hh == ND - 1))
                            return r
                        em.op("pe", f_mm,
                              reads=[("ln1T", hh, i) for hh in range(ND)
                                     for i in range(tq * 4, tq * 4 + 4)] + ["wqb"],
                              writes=[("psq", jq)])

                        def f_bias(s, dd=dd, tq=tq, jq=jq):
                            return s.activation(
                                out=qfm[:, dd * T + tq * 512: dd * T + (tq + 1) * 512],
                                in_=psq[jq][:, :], func=AF.Identity,
                                bias=bqs[:, dd:dd + 1], scale=1.0)
                        em.op("act", f_bias, reads=[("psq", jq), "g:bq"],
                              writes=[("qfm", dd, tq)])

                    for i in range(tq * 4, tq * 4 + 4):
                        j = i % 2
                        js = i % 4

                        def f_trq(t, i=i, j=j):
                            r = None
                            for dd in range(ND):
                                r = t.transpose(out=ptr[j][:, dd * 128:(dd + 1) * 128],
                                                in_=qfm[:, dd * T + i * 128: dd * T + (i + 1) * 128],
                                                identity=ident[:, :])
                            return r
                        em.op("pe", f_trq,
                              reads=[("qfm", dd, i // 4) for dd in range(ND)] + ["g:ident"],
                              writes=[("ptr", j)])
                        em.op("dve", lambda v, j=j, js=js: v.tensor_copy(
                            out=qtm[js][:, :], in_=ptr[j][:, :]),
                            reads=[("ptr", j)], writes=[("qtm", js)])

                        def f_st(g, i=i, js=js):
                            return g.dma_start(
                                out=qq_loc[i // 4][0:128, (i % 4) * 1024:(i % 4 + 1) * 1024],
                                in_=qtm[js][:, :])
                        em.op("gp", f_st, reads=[("qtm", js)], writes=[("d:qtml", i)],
                              sem=f"sqt{js}", dma=True)

                    for kt in range(4):
                        def f_stfm(g, tq=tq, kt=kt):
                            o3 = qq_loc[tq].ap()[128:256,
                                                 kt * 1024:(kt + 1) * 1024].rearrange(
                                "p (hh c) -> p hh c", hh=ND)
                            tok0 = tq * 512 + kt * 128
                            i3 = qfm[:, :].rearrange(
                                "p (dd t) -> p dd t", dd=ND)[:, :, tok0:tok0 + 128]
                            return g.dma_start(out=o3, in_=i3)
                        em.op("gp", f_stfm,
                              reads=[("qfm", dd, tq) for dd in range(ND)],
                              writes=[("d:qfml", tq)],
                              sem="sstf", dma=True)

                    if tq == 0:
                        for jx in range(6):
                            nld = len([i_ for i_ in range(16) if i_ % 6 == jx])
                            em.last_w[("d:xall", jx)] = (f"sxt{jx}", 16 * nld)
                        emit_ag(0, [("d:xall", jx) for jx in range(6)])

            for jw, rw in ((0, 0), (1, 1)):
                em.op("gp", lambda g, jw=jw, rw=rw: g.dma_start(
                    out=kb[jw][:, :],
                    in_=g_q[0].ap()[rw * 256 + 128:rw * 256 + 256, :]),
                    reads=[("d:gq", 0)], writes=[("kb", jw)], sem=f"sgk{jw}", dma=True)
            for jw, rw in ((0, 0), (1, 1), (2, 2)):
                em.op("gp", lambda g, jw=jw, rw=rw: g.dma_start(
                    out=vb[jw][:, :],
                    in_=g_q[0].ap()[rw * 256:rw * 256 + 128, :]),
                    reads=[("d:gq", 0)], writes=[("vb", jw)], sem=f"sgv{jw}", dma=True)
            for s_, v_ in (("sgk0", 16), ("sgk1", 16),
                           ("sgv0", 16), ("sgv1", 16), ("sgv2", 16)):
                em.last_w[("d:p1warm", s_)] = (s_, v_)

            all_staged = ([("d:qtml", i_) for i_ in range(16)]
                          + [("d:qfml", q_) for q_ in range(4)])
            emit_ag(1, all_staged + [("d:p1warm", s_) for s_ in
                                     ("sgk0", "sgk1", "sgv0", "sgv1", "sgv2")])
            for tq_ in (2, 3):
                emit_ag(tq_, all_staged)

            em.set_barrier(exclude=("s_cc", "sstf", "sqt0", "sqt1", "sqt2", "sqt3"))

        em.new_block()
        with ExitStack() as P1:
            e1 = P1.enter_context
            pT = e1(nc.sbuf_tensor("pT", [128, 8 * 1024], BF16))
            lacc = e1(nc.sbuf_tensor("lacc", [1, 1024], F32))
            osb = [e1(nc.sbuf_tensor(f"osb{q}", [128, D], F32)) for q in range(8)]
            oin = [e1(nc.sbuf_tensor(f"oin{j}", [128, D], F32)) for j in range(2)]
            at1 = [e1(nc.sbuf_tensor(f"at1_{j}", [128, D], F32)) for j in range(2)]
            xin = [e1(nc.sbuf_tensor(f"xin{j}", [128, D], F32)) for j in range(2)]
            xnu = [e1(nc.sbuf_tensor(f"xnu{j}", [128, D], F32)) for j in range(3)]
            ln2b = [e1(nc.sbuf_tensor(f"ln2b{j}", [128, D], BF16)) for j in range(2)]
            bn2 = [e1(nc.sbuf_tensor(f"bn2_{j}", [128, 12], F32)) for j in range(8)]
            mv2 = [e1(nc.sbuf_tensor(f"mv2_{j}", [128, 2], F32)) for j in range(8)]
            sd2 = [e1(nc.sbuf_tensor(f"sd2_{j}", [128, 1], F32)) for j in range(8)]
            rs2 = [e1(nc.sbuf_tensor(f"rs2_{j}", [128, 1], F32)) for j in range(8)]
            nm2 = [e1(nc.sbuf_tensor(f"nm2_{j}", [128, 1], F32)) for j in range(8)]
            pss = [e1(nc.psum_tensor(f"pss{j}", [128, 512], F32)) for j in range(2)]
            pso = [e1(nc.psum_tensor(f"pso{j}", [128, D], F32)) for j in range(2)]
            psl = e1(nc.psum_tensor("psl", [128, 8], F32))
            ptr2 = e1(nc.psum_tensor("ptr2", [128, D], BF16))

            QSEQ = [4 * r + qr for qr in range(4) for r in range(NCORES)]

            def emit_kb_load(qpos):
                m = QSEQ[qpos % 32]
                jb = qpos % 2
                r = m // 4
                qr = m % 4

                def f(s, r=r, qr=qr, jb=jb):
                    return s.dma_start(
                        out=kb[jb][:, :],
                        in_=g_q[qr].ap()[r * 256 + 128:r * 256 + 256, :])
                em.op("sync", f, reads=[("d:gq", qr)], writes=[("kb", jb)],
                      sem=f"skb{jb}", dma=True)

            def emit_vb_load(qpos):
                m = QSEQ[qpos % 32]
                jb = qpos % 3
                r = m // 4
                qr = m % 4

                def f(s, r=r, qr=qr, jb=jb):
                    return s.dma_start(
                        out=vb[jb][:, :],
                        in_=g_q[qr].ap()[r * 256:r * 256 + 128, :])
                em.op("sync", f, reads=[("d:gq", qr)], writes=[("vb", jb)],
                      sem=f"svb{jb}", dma=True)

            def emit_xin_load(qq):
                j = qq % 2

                def f(g, qq=qq, j=j):
                    return g.dma_start(out=xin[j][:, :], in_=x_h[qq * 128:(qq + 1) * 128, :])
                em.op("gp", f, writes=[("xin", j)], sem=f"sxi{j}", dma=True)

            def emit_oin_load(k):
                em.op("sync", lambda s, k=k: s.dma_start(
                    out=oin[k % 2][:, :], in_=odr[k * 128:(k + 1) * 128, :]),
                    reads=[("d:osp", q) for q in range(8)], writes=[("oin", k % 2)],
                    sem=f"soi{k % 2}", dma=True)

            def emit_epi_l(hf, qs):
                qq = hf * 8 + qs
                em.op("dve", lambda v, qs=qs, qq=qq: v.reciprocal(
                    out=lrcp[:, qq:qq + 1], in_=psl[:, qs:qs + 1]),
                    reads=[("psl", qs)], writes=[("g:lrcpq", qq)])

            def emit_c0_s1(qs):
                em.op("act", lambda s, qs=qs: s.activation(
                    out=at1[qs % 2][:, :], in_=oin[qs % 2][:, :], func=AF.Identity,
                    scale=lrcp[:, qs:qs + 1]),
                    reads=[("oin", qs % 2), ("g:lrcpq", qs)], writes=[("at1", qs % 2)])
                em.op("dve", lambda v, qs=qs: v.bn_stats(
                    out=bn2[qs][:, 0:6], in_=at1[qs % 2][:, 0:512]),
                    reads=[("at1", qs % 2)], writes=[("bn2", qs)], inc=False)
                em.op("dve", lambda v, qs=qs: v.bn_stats(
                    out=bn2[qs][:, 6:12], in_=at1[qs % 2][:, 512:1024]),
                    reads=[("at1", qs % 2)], writes=[("bn2", qs)])
                em.op("dve", lambda v, qs=qs: v.bn_aggr(out=mv2[qs][:, :], in_=bn2[qs][:, :]),
                      reads=[("bn2", qs)], writes=[("mv2", qs)])

            def emit_c0_s2(qs):
                em.op("act", lambda s, qs=qs: s.activation(
                    out=sd2[qs][:, :], in_=mv2[qs][:, 1:2], func=AF.Sqrt, bias=epsb[:, :]),
                    reads=[("mv2", qs), "g:epsb"], writes=[("sd2", qs)])
                em.op("dve", lambda v, qs=qs: v.reciprocal(out=rs2[qs][:, :], in_=sd2[qs][:, :]),
                      reads=[("sd2", qs)], writes=[("rs2", qs)])
                em.op("dve", lambda v, qs=qs: v.tensor_scalar(
                    out=nm2[qs][:, :], in0=mv2[qs][:, 0:1], scalar1=rs2[qs][:, :],
                    scalar2=-1.0, op0=ALU.mult, op1=ALU.mult),
                    reads=[("mv2", qs), ("rs2", qs)], writes=[("nm2", qs)])

            def emit_c0_s3(qs):
                em.op("act", lambda s, qs=qs: s.activation(
                    out=xnu[qs % 3][:, :], in_=at1[qs % 2][:, :], func=AF.Identity,
                    bias=nm2[qs][:, :], scale=rs2[qs][:, :]),
                    reads=[("at1", qs % 2), ("nm2", qs), ("rs2", qs)],
                    writes=[("xnu", qs % 3)])
                em.op("gp", lambda g_, qs=qs: g_.tensor_add(
                    out=xnu[qs % 3][:, :], in0=xnu[qs % 3][:, :], in1=xin[qs % 2][:, :]),
                    reads=[("xnu", qs % 3), ("xin", qs % 2)], writes=[("xnu", qs % 3)])
                em.op("gp", lambda g_, qs=qs: g_.dma_start(
                    out=xsp[qs * 128:(qs + 1) * 128, :], in_=xnu[qs % 3][:, :]),
                    reads=[("xnu", qs % 3)], writes=[("d:xsp", qs)],
                    sem=f"ssp{qs % 4}", dma=True)
                if qs + 2 < 8:
                    emit_xin_load(qs + 2)

            def emit_c0_s4(qs):
                em.op("dve", lambda v, qs=qs: v.bn_stats(
                    out=bn2[qs][:, 0:6], in_=xnu[qs % 3][:, 0:512]),
                    reads=[("xnu", qs % 3)], writes=[("bn2", qs)], inc=False)
                em.op("dve", lambda v, qs=qs: v.bn_stats(
                    out=bn2[qs][:, 6:12], in_=xnu[qs % 3][:, 512:1024]),
                    reads=[("xnu", qs % 3)], writes=[("bn2", qs)])
                em.op("dve", lambda v, qs=qs: v.bn_aggr(out=mv2[qs][:, :], in_=bn2[qs][:, :]),
                      reads=[("bn2", qs)], writes=[("mv2", qs)])

            def emit_c0_s5(qs):
                em.op("act", lambda s, qs=qs: s.activation(
                    out=sd2[qs][:, :], in_=mv2[qs][:, 1:2], func=AF.Sqrt, bias=epsb[:, :]),
                    reads=[("mv2", qs), "g:epsb"], writes=[("sd2", qs)])
                em.op("dve", lambda v, qs=qs: v.reciprocal(out=rs2[qs][:, :], in_=sd2[qs][:, :]),
                      reads=[("sd2", qs)], writes=[("rs2", qs)])
                em.op("dve", lambda v, qs=qs: v.tensor_scalar(
                    out=ln2b[qs % 2][:, :], in0=xnu[qs % 3][:, :], scalar1=mv2[qs][:, 0:1],
                    scalar2=rs2[qs][:, :], op0=ALU.subtract, op1=ALU.mult),
                    reads=[("xnu", qs % 3), ("mv2", qs), ("rs2", qs)],
                    writes=[("ln2b", qs % 2)])

            def emit_epi_tail(hf, qs):
                qq = hf * 8 + qs
                jl = qq % 2

                def f_tr2(t, jl=jl):
                    r = None
                    for hh in range(ND):
                        r = t.transpose(out=ptr2[:, hh * 128:(hh + 1) * 128],
                                        in_=ln2b[jl][:, hh * 128:(hh + 1) * 128],
                                        identity=ident[:, :])
                    return r
                em.op("pe", f_tr2, reads=[("ln2b", jl), "g:ident"], writes=["ptr2"])

                def f_cp2(v, qq=qq):
                    o3 = ln2T[:, :].rearrange("p (hh t) -> p hh t", hh=ND)[:, :, qq * 128:(qq + 1) * 128]
                    i3 = ptr2[:, :].rearrange("p (hh t) -> p hh t", hh=ND)
                    return v.tensor_copy(out=o3, in_=i3)
                em.op("dve", f_cp2, reads=["ptr2"],
                      writes=[("g:ln2T", hh, qq) for hh in range(ND)])

            emit_xin_load(0)
            emit_xin_load(1)
            emit_w13_cast()

            for gg in range(NG):
                hf = gg // 16
                pos = gg % 16

                qp0 = hf * 32 + 2 * pos
                for t in range(8):
                    kt = t % 4
                    jb = (qp0 + t // 4) % 2
                    for qc in range(2):
                        jss = (t * 2 + qc) % 2

                        def f_sc(tt, jb=jb, kt=kt, hf=hf, qc=qc, jss=jss):
                            r = None
                            for hh in range(ND):
                                r = tt.matmul(
                                    out=pss[jss][:, :],
                                    lhsT=kb[jb][:, kt * 1024 + hh * 128: kt * 1024 + (hh + 1) * 128],
                                    rhs=qfm[:, hh * T + hf * 1024 + qc * 512:
                                            hh * T + hf * 1024 + (qc + 1) * 512],
                                    start=(hh == 0), stop=(hh == ND - 1))
                            return r
                        em.op("pe", f_sc,
                              reads=[("kb", jb)]
                              + [("qfm", d_, 2 * hf + qc) for d_ in range(ND)],
                              writes=[("pss", jss)])

                        def f_exp(s, t=t, qc=qc, jss=jss):
                            return s.activation(
                                out=pT[:, t * 1024 + qc * 512: t * 1024 + (qc + 1) * 512],
                                in_=pss[jss][:, :], func=AF.Exp,
                                bias=shb[:, :], scale=SM_SCALE)
                        em.op("act", f_exp, reads=[("pss", jss), "g:shb"], writes=[("pT", t, qc)])

                for mm_ in (qp0 + 2, qp0 + 3):
                    if mm_ < 64:
                        emit_kb_load(mm_)

                jb4a = qp0 % 3
                jb4b = (qp0 + 1) % 3
                for qs in range(8):
                    jo = (gg * 8 + qs) % 2

                    def f_pv(tt, jb4a=jb4a, jb4b=jb4b, qs=qs, jo=jo):
                        r = None
                        for t in range(8):
                            kt = t % 4
                            jb4 = jb4a if t < 4 else jb4b
                            lhsT = pT[:, t * 1024 + qs * 128: t * 1024 + (qs + 1) * 128]
                            tt.matmul(out=pso[jo][:, 0:512], lhsT=lhsT,
                                      rhs=vb[jb4][:, kt * D: kt * D + 512],
                                      start=(t == 0), stop=(t == 7))
                            r = tt.matmul(out=pso[jo][:, 512:1024], lhsT=lhsT,
                                          rhs=vb[jb4][:, kt * D + 512: (kt + 1) * D],
                                          start=(t == 0), stop=(t == 7))
                        return r
                    em.op("pe", f_pv,
                          reads=[("pT", t, qs // 4) for t in range(8)]
                          + [("vb", jb4a), ("vb", jb4b)],
                          writes=[("pso", jo)])

                    if qs == 0:
                        def f_lsum(tt):
                            r = None
                            for t in range(8):
                                for qc in range(2):
                                    r = tt.matmul(
                                        out=pss[qc][0:1, :],
                                        lhsT=ones[:, 0:1],
                                        rhs=pT[:, t * 1024 + qc * 512:
                                               t * 1024 + (qc + 1) * 512],
                                        start=(t == 0), stop=(t == 7))
                            return r
                        em.op("pe", f_lsum,
                              reads=[("pT", t, qc) for t in range(8)
                                     for qc in range(2)] + ["g:ones"],
                              writes=[("pss", 0), ("pss", 1)])
                        for qc in range(2):
                            c0 = qc * 512
                            if pos == 0:
                                em.op("dve", lambda v, qc=qc, c0=c0: v.tensor_copy(
                                    out=lacc[0:1, c0:c0 + 512], in_=pss[qc][0:1, :]),
                                    reads=[("pss", qc)], writes=["g:lacc"])
                            else:
                                em.op("dve", lambda v, qc=qc, c0=c0: v.tensor_add(
                                    out=lacc[0:1, c0:c0 + 512],
                                    in0=lacc[0:1, c0:c0 + 512], in1=pss[qc][0:1, :]),
                                    reads=[("pss", qc), "g:lacc"],
                                    writes=["g:lacc"])

                    if pos == 0:
                        em.op("dve", lambda v, qs=qs, jo=jo: v.tensor_copy(
                            out=osb[qs][:, :], in_=pso[jo][:, :]),
                            reads=[("pso", jo)]
                            + ([("d:osp", q) for q in range(8)] if hf == 1 else []),
                            writes=[("osb", qs)])
                    else:
                        em.op("dve", lambda v, qs=qs, jo=jo: v.tensor_add(
                            out=osb[qs][:, :], in0=osb[qs][:, :], in1=pso[jo][:, :]),
                            reads=[("pso", jo), ("osb", qs)], writes=[("osb", qs)])
                    if pos == 15:
                        if hf == 0:
                            em.op("gp", lambda g_, qs=qs: g_.dma_start(
                                out=odr[qs * 128:(qs + 1) * 128, :], in_=osb[qs][:, :]),
                                reads=[("osb", qs)], writes=[("d:osp", qs)],
                                sem="sod", dma=True)
                        else:
                            em.op("sync", lambda s, qs=qs: s.dma_start(
                                out=odr[(8 + qs) * 128:(9 + qs) * 128, :],
                                in_=osb[qs][:, :]),
                                reads=[("osb", qs)], writes=[("d:osp", 8 + qs)],
                                sem="sod2", dma=True)

                for mm_ in (qp0 + 3, qp0 + 4):
                    if 2 < mm_ < 64:
                        emit_vb_load(mm_)

                if pos == 15:
                    def f_ltr(tt):
                        r = None
                        for qs in range(8):
                            r = tt.matmul(
                                out=psl[:, qs:qs + 1],
                                lhsT=lacc[0:1, qs * 128:(qs + 1) * 128],
                                rhs=onesf[0:1, 0:1], start=True, stop=True)
                        return r
                    em.op("pe", f_ltr, reads=["g:lacc", "g:onesf"],
                          writes=[("psl", qs) for qs in range(8)])
                    for qs in range(8):
                        emit_epi_l(hf, qs)

                if hf == 0 and pos == 15:
                    emit_oin_load(0)
                    emit_oin_load(1)
                if hf == 1:
                    if 0 <= pos - 5 < 8:
                        emit_epi_tail(0, pos - 5)
                    if 0 <= pos - 4 < 8:
                        emit_c0_s5(pos - 4)
                    if 0 <= pos - 3 < 8:
                        emit_c0_s4(pos - 3)
                    if 0 <= pos - 2 < 8:
                        emit_c0_s3(pos - 2)
                    if 0 <= pos - 1 < 8:
                        emit_c0_s2(pos - 1)
                    if pos < 8:
                        emit_c0_s1(pos)
                        if pos + 2 < 8:
                            emit_oin_load(pos + 2)

                if gg == 29:
                    emit_w13_load(0)
                    emit_w13_load(1)

            em.set_barrier()

        P01.close()

        em.new_block()
        with ExitStack() as P2:
            e2 = P2.enter_context
            h_q = [e2(nc.sbuf_tensor(f"h_q{j}", [128, NF * 512], BF16)) for j in range(2)]
            w2b = e2(nc.sbuf_tensor("w2b", [128, NF * D], BF16))
            gsl = [e2(nc.sbuf_tensor(f"gsl{j}", [128, 512], F32)) for j in range(2)]
            xnw = [e2(nc.sbuf_tensor(f"xnw{j}", [128, D], F32)) for j in range(2)]
            otp = [e2(nc.sbuf_tensor(f"otp{j}", [128, D], F32)) for j in range(2)]
            bn3 = [e2(nc.sbuf_tensor(f"bn3_{j}", [128, 12], F32)) for j in range(2)]
            mv3 = [e2(nc.sbuf_tensor(f"mv3_{j}", [128, 2], F32)) for j in range(2)]
            sd3 = [e2(nc.sbuf_tensor(f"sd3_{j}", [128, 1], F32)) for j in range(2)]
            rs3 = [e2(nc.sbuf_tensor(f"rs3_{j}", [128, 1], F32)) for j in range(2)]
            nm3 = [e2(nc.sbuf_tensor(f"nm3_{j}", [128, 1], F32)) for j in range(2)]
            oin2 = [e2(nc.sbuf_tensor(f"oin2_{j}", [128, D], F32)) for j in range(2)]
            at2 = [e2(nc.sbuf_tensor(f"at2_{j}", [128, D], F32)) for j in range(4)]
            xnu2 = [e2(nc.sbuf_tensor(f"xnu2_{j}", [128, D], F32)) for j in range(4)]
            xin2 = [e2(nc.sbuf_tensor(f"xin2_{j}", [128, D], F32)) for j in range(2)]
            ln2c = [e2(nc.sbuf_tensor(f"ln2c{j}", [128, D], BF16)) for j in range(2)]
            bn4 = [e2(nc.sbuf_tensor(f"bn4_{k}", [128, 12], F32)) for k in range(8)]
            mv4 = [e2(nc.sbuf_tensor(f"mv4_{k}", [128, 2], F32)) for k in range(8)]
            sd4 = [e2(nc.sbuf_tensor(f"sd4_{k}", [128, 1], F32)) for k in range(8)]
            rs4 = [e2(nc.sbuf_tensor(f"rs4_{k}", [128, 1], F32)) for k in range(8)]
            nm4 = [e2(nc.sbuf_tensor(f"nm4_{k}", [128, 1], F32)) for k in range(8)]
            psg = e2(nc.psum_tensor("psg", [128, 512], F32))
            psu = [e2(nc.psum_tensor(f"psu{j}", [128, 512], F32)) for j in range(2)]
            psd = [e2(nc.psum_tensor(f"psd{j}", [128, D], F32)) for j in range(2)]
            ptr3 = e2(nc.psum_tensor("ptr3", [128, D], BF16))

            def emit_w2_load(part):
                ranges = [(0, 6), (6, 12), (12, 18), (18, 22)]
                f0, f1 = ranges[part]

                def f_w2(g, f0=f0, f1=f1):
                    return g.dma_start(out=w2b[:, f0 * D:f1 * D],
                                       in_=w2c.ap()[:, f0 * D:f1 * D])
                em.op("sync", f_w2, reads=["d:w2c"], writes=[("w2b", part)],
                      sem=f"sw2{part}", dma=True)

            def emit_oin2_load(k):
                em.op("sync", lambda s, k=k: s.dma_start(
                    out=oin2[k % 2][:, :], in_=odr[(8 + k) * 128:(9 + k) * 128, :]),
                    reads=[("d:osp", 8 + q) for q in range(8)], writes=[("oin2", k % 2)],
                    sem=f"soi{k % 2}", dma=True)

            def emit_xin2_load(k):
                em.op("gp", lambda g, k=k: g.dma_start(
                    out=xin2[k % 2][:, :], in_=x_h[(8 + k) * 128:(9 + k) * 128, :]),
                    writes=[("xin2", k % 2)], sem=f"sxi{k % 2}", dma=True)

            def emit_c1_s1(k):
                qq = 8 + k
                em.op("act", lambda s, k=k, qq=qq: s.activation(
                    out=at2[k % 4][:, :], in_=oin2[k % 2][:, :], func=AF.Identity,
                    scale=lrcp[:, qq:qq + 1]),
                    reads=[("oin2", k % 2), ("g:lrcpq", qq)], writes=[("at2", k % 4)])
                em.op("dve", lambda v, k=k: v.bn_stats(
                    out=bn4[k][:, 0:6], in_=at2[k % 4][:, 0:512]),
                    reads=[("at2", k % 4)], writes=[("bn4", k)], inc=False)
                em.op("dve", lambda v, k=k: v.bn_stats(
                    out=bn4[k][:, 6:12], in_=at2[k % 4][:, 512:1024]),
                    reads=[("at2", k % 4)], writes=[("bn4", k)])
                em.op("dve", lambda v, k=k: v.bn_aggr(out=mv4[k][:, :], in_=bn4[k][:, :]),
                      reads=[("bn4", k)], writes=[("mv4", k)])

            def emit_c1_s2(k):
                em.op("act", lambda s, k=k: s.activation(
                    out=sd4[k][:, :], in_=mv4[k][:, 1:2], func=AF.Sqrt, bias=epsb[:, :]),
                    reads=[("mv4", k), "g:epsb"], writes=[("sd4", k)])
                em.op("dve", lambda v, k=k: v.reciprocal(out=rs4[k][:, :], in_=sd4[k][:, :]),
                      reads=[("sd4", k)], writes=[("rs4", k)])
                em.op("dve", lambda v, k=k: v.tensor_scalar(
                    out=nm4[k][:, :], in0=mv4[k][:, 0:1], scalar1=rs4[k][:, :],
                    scalar2=-1.0, op0=ALU.mult, op1=ALU.mult),
                    reads=[("mv4", k), ("rs4", k)], writes=[("nm4", k)])

            def emit_c1_s3(k):
                qq = 8 + k
                em.op("act", lambda s, k=k: s.activation(
                    out=xnu2[k % 4][:, :], in_=at2[k % 4][:, :], func=AF.Identity,
                    bias=nm4[k][:, :], scale=rs4[k][:, :]),
                    reads=[("at2", k % 4), ("nm4", k), ("rs4", k)],
                    writes=[("xnu2", k % 4)])
                em.op("gp", lambda g_, k=k: g_.tensor_add(
                    out=xnu2[k % 4][:, :], in0=xnu2[k % 4][:, :], in1=xin2[k % 2][:, :]),
                    reads=[("xnu2", k % 4), ("xin2", k % 2)], writes=[("xnu2", k % 4)])
                em.op("gp", lambda g_, k=k, qq=qq: g_.dma_start(
                    out=xsp[qq * 128:(qq + 1) * 128, :], in_=xnu2[k % 4][:, :]),
                    reads=[("xnu2", k % 4)], writes=[("d:xsp", qq)],
                    sem=f"ssp{k % 4}", dma=True)
                if k + 2 < 8:
                    emit_xin2_load(k + 2)

            def emit_c1_s4(k):
                em.op("dve", lambda v, k=k: v.bn_stats(
                    out=bn4[k][:, 0:6], in_=xnu2[k % 4][:, 0:512]),
                    reads=[("xnu2", k % 4)], writes=[("bn4", k)], inc=False)
                em.op("dve", lambda v, k=k: v.bn_stats(
                    out=bn4[k][:, 6:12], in_=xnu2[k % 4][:, 512:1024]),
                    reads=[("xnu2", k % 4)], writes=[("bn4", k)])
                em.op("dve", lambda v, k=k: v.bn_aggr(out=mv4[k][:, :], in_=bn4[k][:, :]),
                      reads=[("bn4", k)], writes=[("mv4", k)])

            def emit_c1_s5(k):
                em.op("act", lambda s, k=k: s.activation(
                    out=sd4[k][:, :], in_=mv4[k][:, 1:2], func=AF.Sqrt, bias=epsb[:, :]),
                    reads=[("mv4", k), "g:epsb"], writes=[("sd4", k)])
                em.op("dve", lambda v, k=k: v.reciprocal(out=rs4[k][:, :], in_=sd4[k][:, :]),
                      reads=[("sd4", k)], writes=[("rs4", k)])
                em.op("dve", lambda v, k=k: v.tensor_scalar(
                    out=ln2c[k % 2][:, :], in0=xnu2[k % 4][:, :], scalar1=mv4[k][:, 0:1],
                    scalar2=rs4[k][:, :], op0=ALU.subtract, op1=ALU.mult),
                    reads=[("xnu2", k % 4), ("mv4", k), ("rs4", k)],
                    writes=[("ln2c", k % 2)])

            def emit_c1_s6(k):
                qq = 8 + k

                def f_tr(t, k=k):
                    r = None
                    for hh in range(ND):
                        r = t.transpose(out=ptr3[:, hh * 128:(hh + 1) * 128],
                                        in_=ln2c[k % 2][:, hh * 128:(hh + 1) * 128],
                                        identity=ident[:, :])
                    return r
                em.op("pe", f_tr, reads=[("ln2c", k % 2), "g:ident"], writes=["ptr3"])

                def f_cp(v, qq=qq):
                    o3 = ln2T[:, :].rearrange("p (hh t) -> p hh t", hh=ND)[
                        :, :, qq * 128:(qq + 1) * 128]
                    i3 = ptr3[:, :].rearrange("p (hh t) -> p hh t", hh=ND)
                    return v.tensor_copy(out=o3, in_=i3)
                em.op("dve", f_cp, reads=["ptr3"],
                      writes=[("g:ln2T", hh, qq) for hh in range(ND)])

            C1_SEGS = (emit_c1_s1, emit_c1_s2, emit_c1_s3, emit_c1_s4,
                       emit_c1_s5, emit_c1_s6)
            seg_sched = {}
            for k_ in range(8):
                for jseg in range(6):
                    seg_sched.setdefault(2 * k_ + 3 * jseg, []).append((k_, jseg))

            emit_oin2_load(0)
            emit_oin2_load(1)
            emit_xin2_load(0)
            emit_xin2_load(1)

            def emit_xnw_load(tt):
                j = tt % 2

                def f(g, tt=tt, j=j):
                    return g.dma_start(out=xnw[j][:, :], in_=xsp[tt * 128:(tt + 1) * 128, :])
                em.op("gp", f, reads=[("d:xsp", tt)], writes=[("xnw", j)],
                      sem=f"sxw{j}", dma=True)

            def emit_g_group(q, ff, gidx):
                j = gidx % 2
                jg = gidx % 2

                def f_gate(t, j=j, q=q):
                    r = None
                    for hh in range(ND):
                        r = t.matmul(out=psg[:, :],
                                     lhsT=w1s[j][:, hh * 128:(hh + 1) * 128],
                                     rhs=ln2T[:, hh * T + q * 512: hh * T + (q + 1) * 512],
                                     start=(hh == 0), stop=(hh == ND - 1))
                    return r
                em.op("pe", f_gate,
                      reads=[("g:w1s", j)] + [("g:ln2T", hh, i) for hh in range(ND)
                                              for i in range(q * 4, q * 4 + 4)],
                      writes=["psg"])

                em.op("act", lambda s, jg=jg: s.activation(
                    out=gsl[jg][:, :], in_=psg[:, :], func=AF.Silu),
                    reads=["psg"], writes=[("gsl", jg)])

                def f_up(t, j=j, q=q, jg=jg):
                    r = None
                    for hh in range(ND):
                        r = t.matmul(out=psu[jg][:, :],
                                     lhsT=w3s[j][:, hh * 128:(hh + 1) * 128],
                                     rhs=ln2T[:, hh * T + q * 512: hh * T + (q + 1) * 512],
                                     start=(hh == 0), stop=(hh == ND - 1))
                    return r
                em.op("pe", f_up,
                      reads=[("g:w3s", j)] + [("g:ln2T", hh, i) for hh in range(ND)
                                              for i in range(q * 4, q * 4 + 4)],
                      writes=[("psu", jg)])

                def f_h(v, ff=ff, q=q, jg=jg):
                    return v.tensor_mul(
                        out=h_q[q % 2][:, ff * 512:(ff + 1) * 512],
                        in0=gsl[jg][:, :], in1=psu[jg][:, :])
                em.op("dve", f_h, reads=[("gsl", jg), ("psu", jg)],
                      writes=[("h", q, ff)])

                if gidx + 2 < 4 * NF:
                    emit_w13_load(gidx + 2)
                if gidx in (3, 7, 11, 15):
                    emit_w2_load((gidx - 3) // 4)
                for (k_, jseg) in seg_sched.get(gidx, ()):
                    C1_SEGS[jseg](k_)
                    if jseg == 0 and k_ + 2 < 8:
                        emit_oin2_load(k_ + 2)

            def emit_d_group(tt):
                q = tt // 4
                jd = tt % 2
                j = tt % 2

                def f_down(t, tt=tt, q=q, jd=jd):
                    r = None
                    for ff in range(NF):
                        lhsT = h_q[q % 2][:, ff * 512 + (tt % 4) * 128:
                                          ff * 512 + (tt % 4 + 1) * 128]
                        t.matmul(out=psd[jd][:, 0:512], lhsT=lhsT,
                                 rhs=w2b[:, ff * D: ff * D + 512],
                                 start=(ff == 0), stop=(ff == NF - 1))
                        r = t.matmul(out=psd[jd][:, 512:1024], lhsT=lhsT,
                                     rhs=w2b[:, ff * D + 512: (ff + 1) * D],
                                     start=(ff == 0), stop=(ff == NF - 1))
                    return r
                em.op("pe", f_down,
                      reads=[("h", q, ff) for ff in range(NF)]
                      + [("w2b", p) for p in range(4)],
                      writes=[("psd", jd)])

                em.op("dve", lambda v, jd=jd, j=j: v.bn_stats(out=bn3[j][:, 0:6], in_=psd[jd][:, 0:512]),
                      reads=[("psd", jd)], writes=[("bn3", j)], inc=False)
                em.op("dve", lambda v, jd=jd, j=j: v.bn_stats(out=bn3[j][:, 6:12], in_=psd[jd][:, 512:1024]),
                      reads=[("psd", jd)], writes=[("bn3", j)])
                em.op("dve", lambda v, j=j: v.bn_aggr(out=mv3[j][:, :], in_=bn3[j][:, :]),
                      reads=[("bn3", j)], writes=[("mv3", j)])
                em.op("act", lambda s, j=j: s.activation(out=sd3[j][:, :], in_=mv3[j][:, 1:2],
                                                         func=AF.Sqrt, bias=epsb[:, :]),
                      reads=[("mv3", j), "g:epsb"], writes=[("sd3", j)])
                em.op("dve", lambda v, j=j: v.reciprocal(out=rs3[j][:, :], in_=sd3[j][:, :]),
                      reads=[("sd3", j)], writes=[("rs3", j)])
                em.op("dve", lambda v, j=j: v.tensor_scalar(
                    out=nm3[j][:, :], in0=mv3[j][:, 0:1], scalar1=rs3[j][:, :],
                    scalar2=-1.0, op0=ALU.mult, op1=ALU.mult),
                    reads=[("mv3", j), ("rs3", j)], writes=[("nm3", j)])
                em.op("act", lambda s, jd=jd, j=j: s.activation(
                    out=otp[j][:, :], in_=psd[jd][:, :], func=AF.Identity,
                    bias=nm3[j][:, :], scale=rs3[j][:, :]),
                    reads=[("psd", jd), ("nm3", j), ("rs3", j)], writes=[("otp", j)])
                em.op("dve", lambda v, j=j: v.tensor_add(
                    out=otp[j][:, :], in0=otp[j][:, :], in1=xnw[j][:, :]),
                    reads=[("otp", j), ("xnw", j)], writes=[("otp", j)])
                em.op("sync", lambda s, tt=tt, j=j: s.dma_start(
                    out=out_h[tt * 128:(tt + 1) * 128, :], in_=otp[j][:, :]),
                    reads=[("otp", j)], writes=[("d:out", tt)], sem=f"so{j}", dma=True)
                if tt + 2 < NT:
                    emit_xnw_load(tt + 2)

            emit_xnw_load(0)
            emit_xnw_load(1)

            gidx = 0
            for q in (0, 1):
                for ff in range(NF):
                    emit_g_group(q, ff, gidx)
                    gidx += 1
            for tt in range(0, 4):
                emit_d_group(tt)
            for q in (2,):
                for ff in range(NF):
                    emit_g_group(q, ff, gidx)
                    gidx += 1
            for tt in range(4, 8):
                emit_d_group(tt)
            for q in (3,):
                for ff in range(NF):
                    emit_g_group(q, ff, gidx)
                    gidx += 1
            for tt in range(8, 16):
                emit_d_group(tt)

        waited = {e: {} for e in ENGINES}
        attr_of = {"pe": "tensor", "act": "scalar", "dve": "vector",
                   "gp": "gpsimd", "sync": "sync"}

        def emit_block(prog):
            with nc.Block() as block:
                for eng_name in ENGINES:
                    ops = prog[eng_name]
                    if not ops:
                        continue

                    def section(eng, ops=ops, eng_name=eng_name):
                        w = waited[eng_name]
                        for waits, fn, mile, amt in ops:
                            for s_name, v in sorted(waits.items()):
                                if w.get(s_name, 0) < v:
                                    eng.wait_ge(sems[s_name], v)
                                    w[s_name] = v
                            inst = fn(eng)
                            if mile is not None:
                                inst.then_inc(sems[mile[0]], amt)

                    getattr(block, attr_of[eng_name])(section)

        for prog in em.blocks:
            emit_block(prog)

        f0, f1 = em.cnt["so0"], em.cnt["so1"]
        with nc.Block() as block:
            @block.sync
            def _(sync):
                sync.wait_ge(sems["so0"], f0)
                sync.wait_ge(sems["so1"], f1)

    return nc


_NC_CACHE = None


def kernel(L0_attn_inputs, L0_attn_Wq, L0_attn_bq, L0_ffn_w1, L0_ffn_w3, L0_ffn_w2):
    global _NC_CACHE
    x = np.ascontiguousarray(np.asarray(L0_attn_inputs, dtype=np.float32).reshape(S, D))
    wq = np.ascontiguousarray(np.asarray(L0_attn_Wq, dtype=np.float32))
    bq = np.ascontiguousarray(np.asarray(L0_attn_bq, dtype=np.float32).reshape(ND, 128).T)
    w1 = np.asarray(L0_ffn_w1, dtype=np.float32)
    w3 = np.asarray(L0_ffn_w3, dtype=np.float32)
    w2 = np.asarray(L0_ffn_w2, dtype=np.float32)
    ident = np.eye(128, dtype=ml_dtypes.bfloat16)

    if _NC_CACHE is None:
        _NC_CACHE = build_program()
    nc = _NC_CACHE

    in_maps = []
    for c in range(NCORES):
        in_maps.append({
            "x": np.ascontiguousarray(x[c * T:(c + 1) * T]),
            "wq": wq,
            "bq": bq,
            "w1": np.ascontiguousarray(w1[c]),
            "w3": np.ascontiguousarray(w3[c]),
            "w2": np.ascontiguousarray(w2[c]),
            "ident": ident,
        })
    res = run_bass_kernel_spmd(nc, in_maps, core_ids=list(range(NCORES)))
    out = np.concatenate([np.asarray(res.results[c]["out"]) for c in range(NCORES)], axis=0)
    return out.reshape(1, S, D).astype(np.float32)


# version 37
# speedup vs baseline: 1.0148x; 1.0148x over previous
import numpy as np
import ml_dtypes
from contextlib import ExitStack

import concourse.bass as bass
import concourse.mybir as mybir
from concourse.bass_utils import run_bass_kernel_spmd

F32 = mybir.dt.float32
BF16 = mybir.dt.bfloat16
AF = mybir.ActivationFunctionType
ALU = mybir.AluOpType

NCORES = 8
T = 2048
D = 1024
S = 16384
FF = 2816
NT = T // 128
ND = D // 128
NF = FF // 128
NG = 32
EPS = 1e-5
SM_SCALE = 1.0 / 32.0
SM_SHIFT = -32.0

ENGINES = ("pe", "act", "dve", "gp", "sync")
SEM_OF = {"pe": "s_pe", "act": "s_act", "dve": "s_dve", "gp": "s_gp", "sync": "s_sy"}
DMA_SEMS = ("s_ld", "s_gd", "s_st")
SEM_NAMES = (["s_pe", "s_act", "s_dve", "s_gp", "s_sy", "s_cc",
              "sid", "swq", "sbq", "sqf0", "sqf1", "sstf",
              "sqt0", "sqt1", "sqt2", "sqt3", "sod", "soi0", "soi1",
              "sgk0", "sgk1", "sgv0", "sgv1", "sgv2", "sod2",
              "swc1", "swc3", "swc2",
              "ssp0", "ssp1", "ssp2", "ssp3", "so0", "so1"]
             + [f"sxt{j}" for j in range(8)] + [f"sxi{j}" for j in range(2)]
             + [f"sxw{j}" for j in range(2)] + [f"skb{j}" for j in range(2)]
             + [f"svb{j}" for j in range(3)] + [f"sw1{j}" for j in range(2)]
             + [f"sw3{j}" for j in range(2)] + [f"sw2{j}" for j in range(4)])


class Emitter:

    def __init__(self):
        self.cnt = {s: 0 for s in SEM_NAMES}
        self.blocks = []
        self.prog = None
        self.last_w = {}
        self.readers = {}
        self.pending_reads = {e: [] for e in ENGINES}
        self.pending_writes = {e: [] for e in ENGINES}
        self.barrier = {}

    def new_block(self):
        self.prog = {e: [] for e in ENGINES}
        self.blocks.append(self.prog)

    def set_barrier(self, exclude=()):
        self.barrier = {s: v for s, v in self.cnt.items() if v and s not in exclude}

    def op(self, eng, fn, reads=(), writes=(), sem=None, dma=False, inc=True):
        reads = list(reads)
        writes = list(writes)
        waits = {}

        def merge(m):
            if m is not None:
                waits[m[0]] = max(waits.get(m[0], 0), m[1])

        for b in reads:
            merge(self.last_w.get(b))
        for b in writes:
            for m in self.readers.get(b, ()):
                merge(m)
            merge(self.last_w.get(b))

        def scoped(b):
            k = b[0] if isinstance(b, tuple) else b
            return not (k.startswith("g:") or k.startswith("d:"))

        if any(scoped(b) for b in reads + writes):
            for s, v in self.barrier.items():
                waits[s] = max(waits.get(s, 0), v)
        mile = None
        amt = 16 if dma else 1
        if inc:
            s = sem or SEM_OF[eng]
            self.cnt[s] += amt
            mile = (s, self.cnt[s])
        self.prog[eng].append((waits, fn, mile, amt))
        if mile is not None:
            for b in self.pending_reads[eng]:
                self.readers.setdefault(b, []).append(mile)
            self.pending_reads[eng] = []
            for b in self.pending_writes[eng]:
                self.last_w[b] = mile
                self.readers[b] = []
            self.pending_writes[eng] = []
            for b in reads:
                self.readers.setdefault(b, []).append(mile)
            for b in writes:
                self.last_w[b] = mile
                self.readers[b] = []
        else:
            self.pending_reads[eng].extend(reads)
            self.pending_writes[eng].extend(writes)
        return mile


def build_program():
    nc = bass.Bass("TRN2", target_bir_lowering=False)

    x_h = nc.dram_tensor("x", [T, D], F32, kind="ExternalInput")
    wq_h = nc.dram_tensor("wq", [D, D], F32, kind="ExternalInput")
    bq_h = nc.dram_tensor("bq", [128, ND], F32, kind="ExternalInput")
    w1_h = nc.dram_tensor("w1", [D, FF], F32, kind="ExternalInput")
    w3_h = nc.dram_tensor("w3", [D, FF], F32, kind="ExternalInput")
    w2_h = nc.dram_tensor("w2", [FF, D], F32, kind="ExternalInput")
    id_h = nc.dram_tensor("ident", [128, 128], BF16, kind="ExternalInput")
    out_h = nc.dram_tensor("out", [T, D], F32, kind="ExternalOutput")

    qq_loc = [nc.dram_tensor(f"qq_loc{k}", [256, 4096], BF16) for k in range(4)]
    g_q = [nc.dram_tensor(f"g_q{k}", [NCORES * 256, 4096], BF16,
                          addr_space="Shared") for k in range(4)]
    xsp = nc.dram_tensor("xsp", [T, D], F32)
    odr = nc.dram_tensor("odr", [16 * 128, D], F32)
    w1b = nc.dram_tensor("w1b", [128, ND * FF], BF16)
    w3b = nc.dram_tensor("w3b", [128, ND * FF], BF16)
    w2c = nc.dram_tensor("w2c", [128, NF * D], BF16)

    em = Emitter()

    with ExitStack() as G:
        eg = G.enter_context
        sems = {s: eg(nc.semaphore(s)) for s in SEM_NAMES}

        ident = eg(nc.sbuf_tensor("ident_sb", [128, 128], BF16))
        ones = eg(nc.sbuf_tensor("ones", [128, 1], BF16))
        bqs = eg(nc.sbuf_tensor("bqs", [128, ND], F32))
        lsb = eg(nc.sbuf_tensor("lsb", [128, 16], F32))
        lrcp = eg(nc.sbuf_tensor("lrcp", [128, 16], F32))
        epsb = eg(nc.sbuf_tensor("epsb", [128, 1], F32))
        shb = eg(nc.sbuf_tensor("shb", [128, 1], F32))
        ln2T = eg(nc.sbuf_tensor("ln2T", [128, ND * T], BF16))
        w1s = [eg(nc.sbuf_tensor(f"w1s{j}", [128, ND * 128], BF16)) for j in range(2)]
        w3s = [eg(nc.sbuf_tensor(f"w3s{j}", [128, ND * 128], BF16)) for j in range(2)]

        def emit_w13_cast():
            for (dst, src, key, sem) in ((w1b, w1_h, "d:w1b", "swc1"),
                                         (w3b, w3_h, "d:w3b", "swc3")):
                def f(g, dst=dst, src=src):
                    i3 = src.ap().rearrange("(hh p) F -> p hh F", p=128)
                    o3 = dst.ap().rearrange("p (hh F) -> p hh F", hh=ND)
                    return g.dma_start(out=o3, in_=i3)
                em.op("gp", f, writes=[key], sem=sem, dma=True)

            def f2(g):
                i3 = w2_h.ap().rearrange("(ff p) d -> p ff d", p=128)
                o3 = w2c.ap().rearrange("p (ff d) -> p ff d", ff=NF)
                return g.dma_start(out=o3, in_=i3)
            em.op("gp", f2, writes=["d:w2c"], sem="swc2", dma=True)

        def emit_w13_load(gidx):
            ff = gidx % NF
            j = gidx % 2
            for (dst, src, key, sem) in ((w1s, w1b, "d:w1b", "sw1"),
                                         (w3s, w3b, "d:w3b", "sw3")):
                def f(g, dst=dst, src=src, ff=ff, j=j):
                    i3 = src.ap().rearrange("p (hh F) -> p hh F", hh=ND)[
                        :, :, ff * 128:(ff + 1) * 128]
                    o3 = dst[j][:, :].rearrange("p (hh f) -> p hh f", hh=ND)
                    return g.dma_start(out=o3, in_=i3)
                em.op("sync", f, reads=[key],
                      writes=[("g:w1s" if sem == "sw1" else "g:w3s", j)],
                      sem=sem + str(j), dma=True)

        P01 = ExitStack()
        qfm = P01.enter_context(nc.sbuf_tensor("qfm", [128, ND * T], BF16))
        kb = [P01.enter_context(nc.sbuf_tensor(f"kb{j}", [128, ND * 512], BF16))
              for j in range(2)]
        vb = [P01.enter_context(nc.sbuf_tensor(f"vb{j}", [128, 4 * D], BF16))
              for j in range(3)]

        em.new_block()
        with ExitStack() as P0:
            e0 = P0.enter_context
            xt = [e0(nc.sbuf_tensor(f"xt{j}", [128, D], F32)) for j in range(6)]
            bns = [e0(nc.sbuf_tensor(f"bns{j}", [128, 12], F32)) for j in range(4)]
            mv = [e0(nc.sbuf_tensor(f"mv{j}", [128, 2], F32)) for j in range(4)]
            sd = [e0(nc.sbuf_tensor(f"sd{j}", [128, 1], F32)) for j in range(4)]
            rs = [e0(nc.sbuf_tensor(f"rs{j}", [128, 1], F32)) for j in range(4)]
            ln1 = [e0(nc.sbuf_tensor(f"ln1_{j}", [128, D], BF16)) for j in range(4)]
            ln1T = e0(nc.sbuf_tensor("ln1T", [128, ND * T], BF16))
            wqb = e0(nc.sbuf_tensor("wqb", [128, ND * D], BF16))
            qtm = [e0(nc.sbuf_tensor(f"qtm{j}", [128, D], BF16)) for j in range(4)]
            ptr = [e0(nc.psum_tensor(f"ptr{j}", [128, D], BF16)) for j in range(2)]
            psq = [e0(nc.psum_tensor(f"psq{j}", [128, 512], F32)) for j in range(2)]

            em.op("sync", lambda s: s.dma_start(out=ident[:, :], in_=id_h[:, :]),
                  writes=["g:ident"], sem="sid", dma=True)
            em.op("dve", lambda v: v.memset(ones[:, :], 1.0), writes=["g:ones"])
            em.op("dve", lambda v: v.memset(epsb[:, :], EPS), writes=["g:epsb"])
            em.op("dve", lambda v: v.memset(shb[:, :], SM_SHIFT), writes=["g:shb"])
            em.op("gp", lambda g: g.dma_start(out=bqs[:, :], in_=bq_h[:, :]),
                  writes=["g:bq"], sem="sbq", dma=True)

            def f_wq(g):
                i3 = wq_h.ap().rearrange("(hh p) d -> p hh d", p=128)
                o3 = wqb[:, :].rearrange("p (hh d) -> p hh d", hh=ND)
                return g.dma_start(out=o3, in_=i3)
            em.op("gp", f_wq, writes=["wqb"], sem="swq", dma=True)

            def emit_ag(tq, extra=()):
                def f_ag(g, tq=tq):
                    return g.collective_compute(
                        "AllGather", ALU.bypass,
                        replica_groups=[list(range(NCORES))],
                        ins=[qq_loc[tq].ap().opt()], outs=[g_q[tq].ap().opt()])
                em.op("gp", f_ag,
                      reads=[("d:qtml", i_) for i_ in range(tq * 4, tq * 4 + 4)]
                      + [("d:qfml", tq)] + list(extra),
                      writes=[("d:gq", tq)], sem="s_cc")

            def emit_x_load(i):
                j8 = i % 6

                def f_load(s, i=i, j8=j8):
                    return s.dma_start(out=xt[j8][:, :],
                                       in_=x_h[i * 128:(i + 1) * 128, :])
                em.op("sync" if i % 2 == 0 else "act", f_load,
                      writes=[("xt", j8)], sem=f"sxt{j8}", dma=True)

            for i in range(6):
                emit_x_load(i)
            for tq in range(4):
                ws = [tq * 4 + k for k in range(4)]
                for i in ws:
                    j = i % 4
                    j8 = i % 6
                    em.op("dve", lambda v, j=j, j8=j8: v.bn_stats(
                        out=bns[j][:, 0:6], in_=xt[j8][:, 0:512]),
                        reads=[("xt", j8)], writes=[("bns", j)], inc=False)
                    em.op("dve", lambda v, j=j, j8=j8: v.bn_stats(
                        out=bns[j][:, 6:12], in_=xt[j8][:, 512:1024]),
                        reads=[("xt", j8)], writes=[("bns", j)])
                for i in ws:
                    j = i % 4
                    em.op("dve", lambda v, j=j: v.bn_aggr(out=mv[j][:, :], in_=bns[j][:, :]),
                          reads=[("bns", j)], writes=[("mv", j)])
                for i in ws:
                    j = i % 4
                    em.op("act", lambda s, j=j: s.activation(out=sd[j][:, :], in_=mv[j][:, 1:2],
                                                             func=AF.Sqrt, bias=epsb[:, :]),
                          reads=[("mv", j), "g:epsb"], writes=[("sd", j)])
                for i in ws:
                    j = i % 4
                    em.op("dve", lambda v, j=j: v.reciprocal(out=rs[j][:, :], in_=sd[j][:, :]),
                          reads=[("sd", j)], writes=[("rs", j)])
                for i in ws:
                    j = i % 4
                    j8 = i % 6
                    em.op("dve", lambda v, j=j, j8=j8: v.tensor_scalar(
                        out=ln1[j][:, :], in0=xt[j8][:, :], scalar1=mv[j][:, 0:1],
                        scalar2=rs[j][:, :], op0=ALU.subtract, op1=ALU.mult),
                        reads=[("xt", j8), ("mv", j), ("rs", j)], writes=[("ln1", j)])
                for i in ws:
                    j = i % 4
                    jp = i % 2

                    def f_tr(t, j=j, jp=jp):
                        r = None
                        for hh in range(ND):
                            r = t.transpose(out=ptr[jp][:, hh * 128:(hh + 1) * 128],
                                            in_=ln1[j][:, hh * 128:(hh + 1) * 128],
                                            identity=ident[:, :])
                        return r
                    em.op("pe", f_tr, reads=[("ln1", j), "g:ident"], writes=[("ptr", jp)])

                    def f_cp(v, i=i, jp=jp):
                        o3 = ln1T[:, :].rearrange("p (hh t) -> p hh t", hh=ND)[:, :, i * 128:(i + 1) * 128]
                        i3 = ptr[jp][:, :].rearrange("p (hh t) -> p hh t", hh=ND)
                        return v.tensor_copy(out=o3, in_=i3)
                    em.op("dve", f_cp, reads=[("ptr", jp)],
                          writes=[("ln1T", hh, i) for hh in range(ND)])

                for i in range(4 * tq + 6, 4 * tq + 10):
                    if i < 16:
                        emit_x_load(i)

                for dd in range(ND):
                    jq = (tq * ND + dd) % 2

                    def f_mm(t, dd=dd, tq=tq, jq=jq):
                        r = None
                        for hh in range(ND):
                            r = t.matmul(out=psq[jq][:, :],
                                         lhsT=wqb[:, hh * D + dd * 128: hh * D + (dd + 1) * 128],
                                         rhs=ln1T[:, hh * T + tq * 512: hh * T + (tq + 1) * 512],
                                         start=(hh == 0), stop=(hh == ND - 1))
                        return r
                    em.op("pe", f_mm,
                          reads=[("ln1T", hh, i) for hh in range(ND)
                                 for i in range(tq * 4, tq * 4 + 4)] + ["wqb"],
                          writes=[("psq", jq)])

                    def f_bias(s, dd=dd, tq=tq, jq=jq):
                        return s.activation(
                            out=qfm[:, dd * T + tq * 512: dd * T + (tq + 1) * 512],
                            in_=psq[jq][:, :], func=AF.Identity,
                            bias=bqs[:, dd:dd + 1], scale=1.0)
                    em.op("act", f_bias, reads=[("psq", jq), "g:bq"],
                          writes=[("qfm", dd, tq)])

                for i in range(tq * 4, tq * 4 + 4):
                    j = i % 2
                    js = i % 4

                    def f_trq(t, i=i, j=j):
                        r = None
                        for dd in range(ND):
                            r = t.transpose(out=ptr[j][:, dd * 128:(dd + 1) * 128],
                                            in_=qfm[:, dd * T + i * 128: dd * T + (i + 1) * 128],
                                            identity=ident[:, :])
                        return r
                    em.op("pe", f_trq,
                          reads=[("qfm", dd, i // 4) for dd in range(ND)] + ["g:ident"],
                          writes=[("ptr", j)])
                    em.op("dve", lambda v, j=j, js=js: v.tensor_copy(
                        out=qtm[js][:, :], in_=ptr[j][:, :]),
                        reads=[("ptr", j)], writes=[("qtm", js)])

                    def f_st(g, i=i, js=js):
                        return g.dma_start(
                            out=qq_loc[i // 4][0:128, (i % 4) * 1024:(i % 4 + 1) * 1024],
                            in_=qtm[js][:, :])
                    em.op("sync", f_st, reads=[("qtm", js)], writes=[("d:qtml", i)],
                          sem=f"sqt{js}", dma=True)

                for kt in range(4):
                    def f_stfm(g, tq=tq, kt=kt):
                        o3 = qq_loc[tq].ap()[128:256,
                                             kt * 1024:(kt + 1) * 1024].rearrange(
                            "p (hh c) -> p hh c", hh=ND)
                        tok0 = tq * 512 + kt * 128
                        i3 = qfm[:, :].rearrange(
                            "p (dd t) -> p dd t", dd=ND)[:, :, tok0:tok0 + 128]
                        return g.dma_start(out=o3, in_=i3)
                    em.op("sync", f_stfm,
                          reads=[("qfm", dd, tq) for dd in range(ND)],
                          writes=[("d:qfml", tq)],
                          sem="sstf", dma=True)

                if tq == 0:
                    for jx in range(6):
                        em.last_w[("d:xall", jx)] = (f"sxt{jx}", em.cnt[f"sxt{jx}"])
                    emit_ag(0, [("d:xall", jx) for jx in range(6)])
                    for jw, rw in ((0, 0), (1, 1)):
                        em.op("gp", lambda g, jw=jw, rw=rw: g.dma_start(
                            out=kb[jw][:, :],
                            in_=g_q[0].ap()[rw * 256 + 128:rw * 256 + 256, :]),
                            reads=[("d:gq", 0)], writes=[("kb", jw)], sem=f"sgk{jw}", dma=True)
                    for jw, rw in ((0, 0), (1, 1), (2, 2)):
                        em.op("gp", lambda g, jw=jw, rw=rw: g.dma_start(
                            out=vb[jw][:, :],
                            in_=g_q[0].ap()[rw * 256:rw * 256 + 128, :]),
                            reads=[("d:gq", 0)], writes=[("vb", jw)], sem=f"sgv{jw}", dma=True)
                    for s_, v_ in (("sgk0", 16), ("sgk1", 16),
                                   ("sgv0", 16), ("sgv1", 16), ("sgv2", 16)):
                        em.last_w[("d:p1warm", s_)] = (s_, v_)
                elif tq == 1:
                    emit_ag(1, [("d:p1warm", s_) for s_ in
                                ("sgk0", "sgk1", "sgv0", "sgv1", "sgv2")])
                else:
                    emit_ag(tq)

            em.set_barrier(exclude=("s_cc", "sstf", "sqt0", "sqt1", "sqt2", "sqt3"))

        em.new_block()
        with ExitStack() as P1:
            e1 = P1.enter_context
            pT = e1(nc.sbuf_tensor("pT", [128, 8 * 1024], BF16))
            osb = [e1(nc.sbuf_tensor(f"osb{q}", [128, D], F32)) for q in range(8)]
            oin = [e1(nc.sbuf_tensor(f"oin{j}", [128, D], F32)) for j in range(2)]
            at1 = [e1(nc.sbuf_tensor(f"at1_{j}", [128, D], F32)) for j in range(2)]
            xin = [e1(nc.sbuf_tensor(f"xin{j}", [128, D], F32)) for j in range(2)]
            xnu = [e1(nc.sbuf_tensor(f"xnu{j}", [128, D], F32)) for j in range(3)]
            ln2b = [e1(nc.sbuf_tensor(f"ln2b{j}", [128, D], BF16)) for j in range(2)]
            bn2 = [e1(nc.sbuf_tensor(f"bn2_{j}", [128, 12], F32)) for j in range(8)]
            mv2 = [e1(nc.sbuf_tensor(f"mv2_{j}", [128, 2], F32)) for j in range(8)]
            sd2 = [e1(nc.sbuf_tensor(f"sd2_{j}", [128, 1], F32)) for j in range(8)]
            rs2 = [e1(nc.sbuf_tensor(f"rs2_{j}", [128, 1], F32)) for j in range(8)]
            nm2 = [e1(nc.sbuf_tensor(f"nm2_{j}", [128, 1], F32)) for j in range(8)]
            pss = [e1(nc.psum_tensor(f"pss{j}", [128, 512], F32)) for j in range(2)]
            pso = [e1(nc.psum_tensor(f"pso{j}", [128, D], F32)) for j in range(2)]
            psl = e1(nc.psum_tensor("psl", [128, 8], F32))
            ptr2 = e1(nc.psum_tensor("ptr2", [128, D], BF16))

            QSEQ = [4 * r + qr for qr in range(4) for r in range(NCORES)]

            def emit_kb_load(qpos):
                m = QSEQ[qpos % 32]
                jb = qpos % 2
                r = m // 4
                qr = m % 4

                def f(s, r=r, qr=qr, jb=jb):
                    return s.dma_start(
                        out=kb[jb][:, :],
                        in_=g_q[qr].ap()[r * 256 + 128:r * 256 + 256, :])
                em.op("sync", f, reads=[("d:gq", qr)], writes=[("kb", jb)],
                      sem=f"skb{jb}", dma=True)

            def emit_vb_load(qpos):
                m = QSEQ[qpos % 32]
                jb = qpos % 3
                r = m // 4
                qr = m % 4

                def f(s, r=r, qr=qr, jb=jb):
                    return s.dma_start(
                        out=vb[jb][:, :],
                        in_=g_q[qr].ap()[r * 256:r * 256 + 128, :])
                em.op("sync", f, reads=[("d:gq", qr)], writes=[("vb", jb)],
                      sem=f"svb{jb}", dma=True)

            def emit_xin_load(qq):
                j = qq % 2

                def f(g, qq=qq, j=j):
                    return g.dma_start(out=xin[j][:, :], in_=x_h[qq * 128:(qq + 1) * 128, :])
                em.op("gp", f, writes=[("xin", j)], sem=f"sxi{j}", dma=True)

            def emit_oin_load(k):
                em.op("sync", lambda s, k=k: s.dma_start(
                    out=oin[k % 2][:, :], in_=odr[k * 128:(k + 1) * 128, :]),
                    reads=[("d:osp", q) for q in range(8)], writes=[("oin", k % 2)],
                    sem=f"soi{k % 2}", dma=True)

            def emit_epi_l(hf, qs):
                qq = hf * 8 + qs
                em.op("dve", lambda v, hf=hf, qs=qs, qq=qq: v.tensor_add(
                    out=lsb[:, qq:qq + 1], in0=lsb[:, qq:qq + 1], in1=psl[:, qs:qs + 1]),
                    reads=[("psl", qs), ("g:lsb", hf)], writes=[("g:lsbq", qq)])
                em.op("dve", lambda v, qq=qq: v.reciprocal(
                    out=lrcp[:, qq:qq + 1], in_=lsb[:, qq:qq + 1]),
                    reads=[("g:lsbq", qq)], writes=[("g:lrcpq", qq)])

            def emit_c0_s1(qs):
                em.op("act", lambda s, qs=qs: s.activation(
                    out=at1[qs % 2][:, :], in_=oin[qs % 2][:, :], func=AF.Identity,
                    scale=lrcp[:, qs:qs + 1]),
                    reads=[("oin", qs % 2), ("g:lrcpq", qs)], writes=[("at1", qs % 2)])
                em.op("dve", lambda v, qs=qs: v.bn_stats(
                    out=bn2[qs][:, 0:6], in_=at1[qs % 2][:, 0:512]),
                    reads=[("at1", qs % 2)], writes=[("bn2", qs)], inc=False)
                em.op("dve", lambda v, qs=qs: v.bn_stats(
                    out=bn2[qs][:, 6:12], in_=at1[qs % 2][:, 512:1024]),
                    reads=[("at1", qs % 2)], writes=[("bn2", qs)])
                em.op("dve", lambda v, qs=qs: v.bn_aggr(out=mv2[qs][:, :], in_=bn2[qs][:, :]),
                      reads=[("bn2", qs)], writes=[("mv2", qs)])

            def emit_c0_s2(qs):
                em.op("act", lambda s, qs=qs: s.activation(
                    out=sd2[qs][:, :], in_=mv2[qs][:, 1:2], func=AF.Sqrt, bias=epsb[:, :]),
                    reads=[("mv2", qs), "g:epsb"], writes=[("sd2", qs)])
                em.op("dve", lambda v, qs=qs: v.reciprocal(out=rs2[qs][:, :], in_=sd2[qs][:, :]),
                      reads=[("sd2", qs)], writes=[("rs2", qs)])
                em.op("dve", lambda v, qs=qs: v.tensor_scalar(
                    out=nm2[qs][:, :], in0=mv2[qs][:, 0:1], scalar1=rs2[qs][:, :],
                    scalar2=-1.0, op0=ALU.mult, op1=ALU.mult),
                    reads=[("mv2", qs), ("rs2", qs)], writes=[("nm2", qs)])

            def emit_c0_s3(qs):
                em.op("act", lambda s, qs=qs: s.activation(
                    out=xnu[qs % 3][:, :], in_=at1[qs % 2][:, :], func=AF.Identity,
                    bias=nm2[qs][:, :], scale=rs2[qs][:, :]),
                    reads=[("at1", qs % 2), ("nm2", qs), ("rs2", qs)],
                    writes=[("xnu", qs % 3)])
                em.op("gp", lambda g_, qs=qs: g_.tensor_add(
                    out=xnu[qs % 3][:, :], in0=xnu[qs % 3][:, :], in1=xin[qs % 2][:, :]),
                    reads=[("xnu", qs % 3), ("xin", qs % 2)], writes=[("xnu", qs % 3)])
                em.op("gp", lambda g_, qs=qs: g_.dma_start(
                    out=xsp[qs * 128:(qs + 1) * 128, :], in_=xnu[qs % 3][:, :]),
                    reads=[("xnu", qs % 3)], writes=[("d:xsp", qs)],
                    sem=f"ssp{qs % 4}", dma=True)
                if qs + 2 < 8:
                    emit_xin_load(qs + 2)

            def emit_c0_s4(qs):
                em.op("dve", lambda v, qs=qs: v.bn_stats(
                    out=bn2[qs][:, 0:6], in_=xnu[qs % 3][:, 0:512]),
                    reads=[("xnu", qs % 3)], writes=[("bn2", qs)], inc=False)
                em.op("dve", lambda v, qs=qs: v.bn_stats(
                    out=bn2[qs][:, 6:12], in_=xnu[qs % 3][:, 512:1024]),
                    reads=[("xnu", qs % 3)], writes=[("bn2", qs)])
                em.op("dve", lambda v, qs=qs: v.bn_aggr(out=mv2[qs][:, :], in_=bn2[qs][:, :]),
                      reads=[("bn2", qs)], writes=[("mv2", qs)])

            def emit_c0_s5(qs):
                em.op("act", lambda s, qs=qs: s.activation(
                    out=sd2[qs][:, :], in_=mv2[qs][:, 1:2], func=AF.Sqrt, bias=epsb[:, :]),
                    reads=[("mv2", qs), "g:epsb"], writes=[("sd2", qs)])
                em.op("dve", lambda v, qs=qs: v.reciprocal(out=rs2[qs][:, :], in_=sd2[qs][:, :]),
                      reads=[("sd2", qs)], writes=[("rs2", qs)])
                em.op("dve", lambda v, qs=qs: v.tensor_scalar(
                    out=ln2b[qs % 2][:, :], in0=xnu[qs % 3][:, :], scalar1=mv2[qs][:, 0:1],
                    scalar2=rs2[qs][:, :], op0=ALU.subtract, op1=ALU.mult),
                    reads=[("xnu", qs % 3), ("mv2", qs), ("rs2", qs)],
                    writes=[("ln2b", qs % 2)])

            def emit_epi_tail(hf, qs):
                qq = hf * 8 + qs
                jl = qq % 2

                def f_tr2(t, jl=jl):
                    r = None
                    for hh in range(ND):
                        r = t.transpose(out=ptr2[:, hh * 128:(hh + 1) * 128],
                                        in_=ln2b[jl][:, hh * 128:(hh + 1) * 128],
                                        identity=ident[:, :])
                    return r
                em.op("pe", f_tr2, reads=[("ln2b", jl), "g:ident"], writes=["ptr2"])

                def f_cp2(v, qq=qq):
                    o3 = ln2T[:, :].rearrange("p (hh t) -> p hh t", hh=ND)[:, :, qq * 128:(qq + 1) * 128]
                    i3 = ptr2[:, :].rearrange("p (hh t) -> p hh t", hh=ND)
                    return v.tensor_copy(out=o3, in_=i3)
                em.op("dve", f_cp2, reads=["ptr2"],
                      writes=[("g:ln2T", hh, qq) for hh in range(ND)])

            emit_xin_load(0)
            emit_xin_load(1)
            emit_w13_cast()

            for gg in range(NG):
                hf = gg // 16
                pos = gg % 16

                qp0 = hf * 32 + 2 * pos
                for t in range(8):
                    kt = t % 4
                    jb = (qp0 + t // 4) % 2
                    for qc in range(2):
                        jss = (t * 2 + qc) % 2

                        def f_sc(tt, jb=jb, kt=kt, hf=hf, qc=qc, jss=jss):
                            r = None
                            for hh in range(ND):
                                r = tt.matmul(
                                    out=pss[jss][:, :],
                                    lhsT=kb[jb][:, kt * 1024 + hh * 128: kt * 1024 + (hh + 1) * 128],
                                    rhs=qfm[:, hh * T + hf * 1024 + qc * 512:
                                            hh * T + hf * 1024 + (qc + 1) * 512],
                                    start=(hh == 0), stop=(hh == ND - 1))
                            return r
                        em.op("pe", f_sc,
                              reads=[("kb", jb)]
                              + [("qfm", d_, 2 * hf + qc) for d_ in range(ND)],
                              writes=[("pss", jss)])

                        def f_exp(s, t=t, qc=qc, jss=jss):
                            return s.activation(
                                out=pT[:, t * 1024 + qc * 512: t * 1024 + (qc + 1) * 512],
                                in_=pss[jss][:, :], func=AF.Exp,
                                bias=shb[:, :], scale=SM_SCALE)
                        em.op("act", f_exp, reads=[("pss", jss), "g:shb"], writes=[("pT", t, qc)])

                for mm_ in (qp0 + 2, qp0 + 3):
                    if mm_ < 64:
                        emit_kb_load(mm_)

                jb4a = qp0 % 3
                jb4b = (qp0 + 1) % 3
                for qs in range(8):
                    jo = (gg * 8 + qs) % 2

                    def f_pv(tt, jb4a=jb4a, jb4b=jb4b, qs=qs, jo=jo):
                        r = None
                        for t in range(8):
                            kt = t % 4
                            jb4 = jb4a if t < 4 else jb4b
                            lhsT = pT[:, t * 1024 + qs * 128: t * 1024 + (qs + 1) * 128]
                            tt.matmul(out=pso[jo][:, 0:512], lhsT=lhsT,
                                      rhs=vb[jb4][:, kt * D: kt * D + 512],
                                      start=(t == 0), stop=(t == 7))
                            tt.matmul(out=pso[jo][:, 512:1024], lhsT=lhsT,
                                      rhs=vb[jb4][:, kt * D + 512: (kt + 1) * D],
                                      start=(t == 0), stop=(t == 7))
                            r = tt.matmul(out=psl[:, qs:qs + 1], lhsT=lhsT,
                                          rhs=ones[:, 0:1],
                                          start=(t == 0), stop=(t == 7))
                        return r
                    em.op("pe", f_pv,
                          reads=[("pT", t, qs // 4) for t in range(8)]
                          + [("vb", jb4a), ("vb", jb4b), "g:ones"],
                          writes=[("pso", jo), ("psl", qs)])

                    if pos == 0:
                        em.op("dve", lambda v, qs=qs, jo=jo: v.tensor_copy(
                            out=osb[qs][:, :], in_=pso[jo][:, :]),
                            reads=[("pso", jo)]
                            + ([("d:osp", q) for q in range(8)] if hf == 1 else []),
                            writes=[("osb", qs)])
                    else:
                        em.op("dve", lambda v, qs=qs, jo=jo: v.tensor_add(
                            out=osb[qs][:, :], in0=osb[qs][:, :], in1=pso[jo][:, :]),
                            reads=[("pso", jo), ("osb", qs)], writes=[("osb", qs)])
                    if pos == 15:
                        emit_epi_l(hf, qs)
                        if hf == 0:
                            em.op("gp", lambda g_, qs=qs: g_.dma_start(
                                out=odr[qs * 128:(qs + 1) * 128, :], in_=osb[qs][:, :]),
                                reads=[("osb", qs)], writes=[("d:osp", qs)],
                                sem="sod", dma=True)
                        else:
                            em.op("sync", lambda s, qs=qs: s.dma_start(
                                out=odr[(8 + qs) * 128:(9 + qs) * 128, :],
                                in_=osb[qs][:, :]),
                                reads=[("osb", qs)], writes=[("d:osp", 8 + qs)],
                                sem="sod2", dma=True)

                for mm_ in (qp0 + 3, qp0 + 4):
                    if 2 < mm_ < 64:
                        emit_vb_load(mm_)

                if pos == 15:
                    pass
                elif pos == 0:
                    em.op("dve", lambda v, hf=hf: v.tensor_copy(
                        out=lsb[:, hf * 8:(hf + 1) * 8], in_=psl[:, :]),
                        reads=[("psl", q) for q in range(8)], writes=[("g:lsb", hf)])
                else:
                    em.op("dve", lambda v, hf=hf: v.tensor_add(
                        out=lsb[:, hf * 8:(hf + 1) * 8],
                        in0=lsb[:, hf * 8:(hf + 1) * 8], in1=psl[:, :]),
                        reads=[("psl", q) for q in range(8)] + [("g:lsb", hf)],
                        writes=[("g:lsb", hf)])

                if hf == 0 and pos == 15:
                    emit_oin_load(0)
                    emit_oin_load(1)
                if hf == 1:
                    if 0 <= pos - 5 < 8:
                        emit_epi_tail(0, pos - 5)
                    if 0 <= pos - 4 < 8:
                        emit_c0_s5(pos - 4)
                    if 0 <= pos - 3 < 8:
                        emit_c0_s4(pos - 3)
                    if 0 <= pos - 2 < 8:
                        emit_c0_s3(pos - 2)
                    if 0 <= pos - 1 < 8:
                        emit_c0_s2(pos - 1)
                    if pos < 8:
                        emit_c0_s1(pos)
                        if pos + 2 < 8:
                            emit_oin_load(pos + 2)

                if gg == 29:
                    emit_w13_load(0)
                    emit_w13_load(1)

            em.set_barrier()

        P01.close()

        em.new_block()
        with ExitStack() as P2:
            e2 = P2.enter_context
            h_q = [e2(nc.sbuf_tensor(f"h_q{j}", [128, NF * 512], BF16)) for j in range(2)]
            w2b = e2(nc.sbuf_tensor("w2b", [128, NF * D], BF16))
            gsl = [e2(nc.sbuf_tensor(f"gsl{j}", [128, 512], F32)) for j in range(2)]
            xnw = [e2(nc.sbuf_tensor(f"xnw{j}", [128, D], F32)) for j in range(2)]
            otp = [e2(nc.sbuf_tensor(f"otp{j}", [128, D], F32)) for j in range(2)]
            bn3 = [e2(nc.sbuf_tensor(f"bn3_{j}", [128, 12], F32)) for j in range(2)]
            mv3 = [e2(nc.sbuf_tensor(f"mv3_{j}", [128, 2], F32)) for j in range(2)]
            sd3 = [e2(nc.sbuf_tensor(f"sd3_{j}", [128, 1], F32)) for j in range(2)]
            rs3 = [e2(nc.sbuf_tensor(f"rs3_{j}", [128, 1], F32)) for j in range(2)]
            nm3 = [e2(nc.sbuf_tensor(f"nm3_{j}", [128, 1], F32)) for j in range(2)]
            oin2 = [e2(nc.sbuf_tensor(f"oin2_{j}", [128, D], F32)) for j in range(2)]
            at2 = [e2(nc.sbuf_tensor(f"at2_{j}", [128, D], F32)) for j in range(4)]
            xnu2 = [e2(nc.sbuf_tensor(f"xnu2_{j}", [128, D], F32)) for j in range(4)]
            xin2 = [e2(nc.sbuf_tensor(f"xin2_{j}", [128, D], F32)) for j in range(2)]
            ln2c = [e2(nc.sbuf_tensor(f"ln2c{j}", [128, D], BF16)) for j in range(2)]
            bn4 = [e2(nc.sbuf_tensor(f"bn4_{k}", [128, 12], F32)) for k in range(8)]
            mv4 = [e2(nc.sbuf_tensor(f"mv4_{k}", [128, 2], F32)) for k in range(8)]
            sd4 = [e2(nc.sbuf_tensor(f"sd4_{k}", [128, 1], F32)) for k in range(8)]
            rs4 = [e2(nc.sbuf_tensor(f"rs4_{k}", [128, 1], F32)) for k in range(8)]
            nm4 = [e2(nc.sbuf_tensor(f"nm4_{k}", [128, 1], F32)) for k in range(8)]
            psg = e2(nc.psum_tensor("psg", [128, 512], F32))
            psu = [e2(nc.psum_tensor(f"psu{j}", [128, 512], F32)) for j in range(2)]
            psd = [e2(nc.psum_tensor(f"psd{j}", [128, D], F32)) for j in range(2)]
            ptr3 = e2(nc.psum_tensor("ptr3", [128, D], BF16))

            def emit_w2_load(part):
                ranges = [(0, 6), (6, 12), (12, 18), (18, 22)]
                f0, f1 = ranges[part]

                def f_w2(g, f0=f0, f1=f1):
                    return g.dma_start(out=w2b[:, f0 * D:f1 * D],
                                       in_=w2c.ap()[:, f0 * D:f1 * D])
                em.op("sync", f_w2, reads=["d:w2c"], writes=[("w2b", part)],
                      sem=f"sw2{part}", dma=True)

            def emit_oin2_load(k):
                em.op("sync", lambda s, k=k: s.dma_start(
                    out=oin2[k % 2][:, :], in_=odr[(8 + k) * 128:(9 + k) * 128, :]),
                    reads=[("d:osp", 8 + q) for q in range(8)], writes=[("oin2", k % 2)],
                    sem=f"soi{k % 2}", dma=True)

            def emit_xin2_load(k):
                em.op("gp", lambda g, k=k: g.dma_start(
                    out=xin2[k % 2][:, :], in_=x_h[(8 + k) * 128:(9 + k) * 128, :]),
                    writes=[("xin2", k % 2)], sem=f"sxi{k % 2}", dma=True)

            def emit_c1_s1(k):
                qq = 8 + k
                em.op("act", lambda s, k=k, qq=qq: s.activation(
                    out=at2[k % 4][:, :], in_=oin2[k % 2][:, :], func=AF.Identity,
                    scale=lrcp[:, qq:qq + 1]),
                    reads=[("oin2", k % 2), ("g:lrcpq", qq)], writes=[("at2", k % 4)])
                em.op("dve", lambda v, k=k: v.bn_stats(
                    out=bn4[k][:, 0:6], in_=at2[k % 4][:, 0:512]),
                    reads=[("at2", k % 4)], writes=[("bn4", k)], inc=False)
                em.op("dve", lambda v, k=k: v.bn_stats(
                    out=bn4[k][:, 6:12], in_=at2[k % 4][:, 512:1024]),
                    reads=[("at2", k % 4)], writes=[("bn4", k)])
                em.op("dve", lambda v, k=k: v.bn_aggr(out=mv4[k][:, :], in_=bn4[k][:, :]),
                      reads=[("bn4", k)], writes=[("mv4", k)])

            def emit_c1_s2(k):
                em.op("act", lambda s, k=k: s.activation(
                    out=sd4[k][:, :], in_=mv4[k][:, 1:2], func=AF.Sqrt, bias=epsb[:, :]),
                    reads=[("mv4", k), "g:epsb"], writes=[("sd4", k)])
                em.op("dve", lambda v, k=k: v.reciprocal(out=rs4[k][:, :], in_=sd4[k][:, :]),
                      reads=[("sd4", k)], writes=[("rs4", k)])
                em.op("dve", lambda v, k=k: v.tensor_scalar(
                    out=nm4[k][:, :], in0=mv4[k][:, 0:1], scalar1=rs4[k][:, :],
                    scalar2=-1.0, op0=ALU.mult, op1=ALU.mult),
                    reads=[("mv4", k), ("rs4", k)], writes=[("nm4", k)])

            def emit_c1_s3(k):
                qq = 8 + k
                em.op("act", lambda s, k=k: s.activation(
                    out=xnu2[k % 4][:, :], in_=at2[k % 4][:, :], func=AF.Identity,
                    bias=nm4[k][:, :], scale=rs4[k][:, :]),
                    reads=[("at2", k % 4), ("nm4", k), ("rs4", k)],
                    writes=[("xnu2", k % 4)])
                em.op("gp", lambda g_, k=k: g_.tensor_add(
                    out=xnu2[k % 4][:, :], in0=xnu2[k % 4][:, :], in1=xin2[k % 2][:, :]),
                    reads=[("xnu2", k % 4), ("xin2", k % 2)], writes=[("xnu2", k % 4)])
                em.op("gp", lambda g_, k=k, qq=qq: g_.dma_start(
                    out=xsp[qq * 128:(qq + 1) * 128, :], in_=xnu2[k % 4][:, :]),
                    reads=[("xnu2", k % 4)], writes=[("d:xsp", qq)],
                    sem=f"ssp{k % 4}", dma=True)
                if k + 2 < 8:
                    emit_xin2_load(k + 2)

            def emit_c1_s4(k):
                em.op("dve", lambda v, k=k: v.bn_stats(
                    out=bn4[k][:, 0:6], in_=xnu2[k % 4][:, 0:512]),
                    reads=[("xnu2", k % 4)], writes=[("bn4", k)], inc=False)
                em.op("dve", lambda v, k=k: v.bn_stats(
                    out=bn4[k][:, 6:12], in_=xnu2[k % 4][:, 512:1024]),
                    reads=[("xnu2", k % 4)], writes=[("bn4", k)])
                em.op("dve", lambda v, k=k: v.bn_aggr(out=mv4[k][:, :], in_=bn4[k][:, :]),
                      reads=[("bn4", k)], writes=[("mv4", k)])

            def emit_c1_s5(k):
                em.op("act", lambda s, k=k: s.activation(
                    out=sd4[k][:, :], in_=mv4[k][:, 1:2], func=AF.Sqrt, bias=epsb[:, :]),
                    reads=[("mv4", k), "g:epsb"], writes=[("sd4", k)])
                em.op("dve", lambda v, k=k: v.reciprocal(out=rs4[k][:, :], in_=sd4[k][:, :]),
                      reads=[("sd4", k)], writes=[("rs4", k)])
                em.op("dve", lambda v, k=k: v.tensor_scalar(
                    out=ln2c[k % 2][:, :], in0=xnu2[k % 4][:, :], scalar1=mv4[k][:, 0:1],
                    scalar2=rs4[k][:, :], op0=ALU.subtract, op1=ALU.mult),
                    reads=[("xnu2", k % 4), ("mv4", k), ("rs4", k)],
                    writes=[("ln2c", k % 2)])

            def emit_c1_s6(k):
                qq = 8 + k

                def f_tr(t, k=k):
                    r = None
                    for hh in range(ND):
                        r = t.transpose(out=ptr3[:, hh * 128:(hh + 1) * 128],
                                        in_=ln2c[k % 2][:, hh * 128:(hh + 1) * 128],
                                        identity=ident[:, :])
                    return r
                em.op("pe", f_tr, reads=[("ln2c", k % 2), "g:ident"], writes=["ptr3"])

                def f_cp(v, qq=qq):
                    o3 = ln2T[:, :].rearrange("p (hh t) -> p hh t", hh=ND)[
                        :, :, qq * 128:(qq + 1) * 128]
                    i3 = ptr3[:, :].rearrange("p (hh t) -> p hh t", hh=ND)
                    return v.tensor_copy(out=o3, in_=i3)
                em.op("dve", f_cp, reads=["ptr3"],
                      writes=[("g:ln2T", hh, qq) for hh in range(ND)])

            C1_SEGS = (emit_c1_s1, emit_c1_s2, emit_c1_s3, emit_c1_s4,
                       emit_c1_s5, emit_c1_s6)
            seg_sched = {}
            for k_ in range(8):
                for jseg in range(6):
                    seg_sched.setdefault(2 * k_ + 3 * jseg, []).append((k_, jseg))

            emit_oin2_load(0)
            emit_oin2_load(1)
            emit_xin2_load(0)
            emit_xin2_load(1)

            def emit_xnw_load(tt):
                j = tt % 2

                def f(g, tt=tt, j=j):
                    return g.dma_start(out=xnw[j][:, :], in_=xsp[tt * 128:(tt + 1) * 128, :])
                em.op("gp", f, reads=[("d:xsp", tt)], writes=[("xnw", j)],
                      sem=f"sxw{j}", dma=True)

            def emit_g_group(q, ff, gidx):
                j = gidx % 2
                jg = gidx % 2

                def f_gate(t, j=j, q=q):
                    r = None
                    for hh in range(ND):
                        r = t.matmul(out=psg[:, :],
                                     lhsT=w1s[j][:, hh * 128:(hh + 1) * 128],
                                     rhs=ln2T[:, hh * T + q * 512: hh * T + (q + 1) * 512],
                                     start=(hh == 0), stop=(hh == ND - 1))
                    return r
                em.op("pe", f_gate,
                      reads=[("g:w1s", j)] + [("g:ln2T", hh, i) for hh in range(ND)
                                              for i in range(q * 4, q * 4 + 4)],
                      writes=["psg"])

                em.op("act", lambda s, jg=jg: s.activation(
                    out=gsl[jg][:, :], in_=psg[:, :], func=AF.Silu),
                    reads=["psg"], writes=[("gsl", jg)])

                def f_up(t, j=j, q=q, jg=jg):
                    r = None
                    for hh in range(ND):
                        r = t.matmul(out=psu[jg][:, :],
                                     lhsT=w3s[j][:, hh * 128:(hh + 1) * 128],
                                     rhs=ln2T[:, hh * T + q * 512: hh * T + (q + 1) * 512],
                                     start=(hh == 0), stop=(hh == ND - 1))
                    return r
                em.op("pe", f_up,
                      reads=[("g:w3s", j)] + [("g:ln2T", hh, i) for hh in range(ND)
                                              for i in range(q * 4, q * 4 + 4)],
                      writes=[("psu", jg)])

                def f_h(v, ff=ff, q=q, jg=jg):
                    return v.tensor_mul(
                        out=h_q[q % 2][:, ff * 512:(ff + 1) * 512],
                        in0=gsl[jg][:, :], in1=psu[jg][:, :])
                em.op("dve", f_h, reads=[("gsl", jg), ("psu", jg)],
                      writes=[("h", q, ff)])

                if gidx + 2 < 4 * NF:
                    emit_w13_load(gidx + 2)
                if gidx in (3, 7, 11, 15):
                    emit_w2_load((gidx - 3) // 4)
                for (k_, jseg) in seg_sched.get(gidx, ()):
                    C1_SEGS[jseg](k_)
                    if jseg == 0 and k_ + 2 < 8:
                        emit_oin2_load(k_ + 2)

            def emit_d_group(tt):
                q = tt // 4
                jd = tt % 2
                j = tt % 2

                def f_down(t, tt=tt, q=q, jd=jd):
                    r = None
                    for ff in range(NF):
                        lhsT = h_q[q % 2][:, ff * 512 + (tt % 4) * 128:
                                          ff * 512 + (tt % 4 + 1) * 128]
                        t.matmul(out=psd[jd][:, 0:512], lhsT=lhsT,
                                 rhs=w2b[:, ff * D: ff * D + 512],
                                 start=(ff == 0), stop=(ff == NF - 1))
                        r = t.matmul(out=psd[jd][:, 512:1024], lhsT=lhsT,
                                     rhs=w2b[:, ff * D + 512: (ff + 1) * D],
                                     start=(ff == 0), stop=(ff == NF - 1))
                    return r
                em.op("pe", f_down,
                      reads=[("h", q, ff) for ff in range(NF)]
                      + [("w2b", p) for p in range(4)],
                      writes=[("psd", jd)])

                em.op("dve", lambda v, jd=jd, j=j: v.bn_stats(out=bn3[j][:, 0:6], in_=psd[jd][:, 0:512]),
                      reads=[("psd", jd)], writes=[("bn3", j)], inc=False)
                em.op("dve", lambda v, jd=jd, j=j: v.bn_stats(out=bn3[j][:, 6:12], in_=psd[jd][:, 512:1024]),
                      reads=[("psd", jd)], writes=[("bn3", j)])
                em.op("dve", lambda v, j=j: v.bn_aggr(out=mv3[j][:, :], in_=bn3[j][:, :]),
                      reads=[("bn3", j)], writes=[("mv3", j)])
                em.op("act", lambda s, j=j: s.activation(out=sd3[j][:, :], in_=mv3[j][:, 1:2],
                                                         func=AF.Sqrt, bias=epsb[:, :]),
                      reads=[("mv3", j), "g:epsb"], writes=[("sd3", j)])
                em.op("dve", lambda v, j=j: v.reciprocal(out=rs3[j][:, :], in_=sd3[j][:, :]),
                      reads=[("sd3", j)], writes=[("rs3", j)])
                em.op("dve", lambda v, j=j: v.tensor_scalar(
                    out=nm3[j][:, :], in0=mv3[j][:, 0:1], scalar1=rs3[j][:, :],
                    scalar2=-1.0, op0=ALU.mult, op1=ALU.mult),
                    reads=[("mv3", j), ("rs3", j)], writes=[("nm3", j)])
                em.op("act", lambda s, jd=jd, j=j: s.activation(
                    out=otp[j][:, :], in_=psd[jd][:, :], func=AF.Identity,
                    bias=nm3[j][:, :], scale=rs3[j][:, :]),
                    reads=[("psd", jd), ("nm3", j), ("rs3", j)], writes=[("otp", j)])
                em.op("dve", lambda v, j=j: v.tensor_add(
                    out=otp[j][:, :], in0=otp[j][:, :], in1=xnw[j][:, :]),
                    reads=[("otp", j), ("xnw", j)], writes=[("otp", j)])
                em.op("sync", lambda s, tt=tt, j=j: s.dma_start(
                    out=out_h[tt * 128:(tt + 1) * 128, :], in_=otp[j][:, :]),
                    reads=[("otp", j)], writes=[("d:out", tt)], sem=f"so{j}", dma=True)
                if tt + 2 < NT:
                    emit_xnw_load(tt + 2)

            emit_xnw_load(0)
            emit_xnw_load(1)

            gidx = 0
            for q in (0, 1):
                for ff in range(NF):
                    emit_g_group(q, ff, gidx)
                    gidx += 1
            for tt in range(0, 4):
                emit_d_group(tt)
            for q in (2,):
                for ff in range(NF):
                    emit_g_group(q, ff, gidx)
                    gidx += 1
            for tt in range(4, 8):
                emit_d_group(tt)
            for q in (3,):
                for ff in range(NF):
                    emit_g_group(q, ff, gidx)
                    gidx += 1
            for tt in range(8, 16):
                emit_d_group(tt)

        waited = {e: {} for e in ENGINES}
        attr_of = {"pe": "tensor", "act": "scalar", "dve": "vector",
                   "gp": "gpsimd", "sync": "sync"}

        def emit_block(prog):
            with nc.Block() as block:
                for eng_name in ENGINES:
                    ops = prog[eng_name]
                    if not ops:
                        continue

                    def section(eng, ops=ops, eng_name=eng_name):
                        w = waited[eng_name]
                        for waits, fn, mile, amt in ops:
                            for s_name, v in sorted(waits.items()):
                                if w.get(s_name, 0) < v:
                                    eng.wait_ge(sems[s_name], v)
                                    w[s_name] = v
                            inst = fn(eng)
                            if mile is not None:
                                inst.then_inc(sems[mile[0]], amt)

                    getattr(block, attr_of[eng_name])(section)

        for prog in em.blocks:
            emit_block(prog)

        f0, f1 = em.cnt["so0"], em.cnt["so1"]
        with nc.Block() as block:
            @block.sync
            def _(sync):
                sync.wait_ge(sems["so0"], f0)
                sync.wait_ge(sems["so1"], f1)

    return nc


_NC_CACHE = None


def kernel(L0_attn_inputs, L0_attn_Wq, L0_attn_bq, L0_ffn_w1, L0_ffn_w3, L0_ffn_w2):
    global _NC_CACHE
    x = np.ascontiguousarray(np.asarray(L0_attn_inputs, dtype=np.float32).reshape(S, D))
    wq = np.ascontiguousarray(np.asarray(L0_attn_Wq, dtype=np.float32))
    bq = np.ascontiguousarray(np.asarray(L0_attn_bq, dtype=np.float32).reshape(ND, 128).T)
    w1 = np.asarray(L0_ffn_w1, dtype=np.float32)
    w3 = np.asarray(L0_ffn_w3, dtype=np.float32)
    w2 = np.asarray(L0_ffn_w2, dtype=np.float32)
    ident = np.eye(128, dtype=ml_dtypes.bfloat16)

    if _NC_CACHE is None:
        _NC_CACHE = build_program()
    nc = _NC_CACHE

    in_maps = []
    for c in range(NCORES):
        in_maps.append({
            "x": np.ascontiguousarray(x[c * T:(c + 1) * T]),
            "wq": wq,
            "bq": bq,
            "w1": np.ascontiguousarray(w1[c]),
            "w3": np.ascontiguousarray(w3[c]),
            "w2": np.ascontiguousarray(w2[c]),
            "ident": ident,
        })
    res = run_bass_kernel_spmd(nc, in_maps, core_ids=list(range(NCORES)))
    out = np.concatenate([np.asarray(res.results[c]["out"]) for c in range(NCORES)], axis=0)
    return out.reshape(1, S, D).astype(np.float32)


# version 38
# speedup vs baseline: 1.0295x; 1.0145x over previous
import numpy as np
import ml_dtypes
from contextlib import ExitStack

import concourse.bass as bass
import concourse.mybir as mybir
from concourse.bass_utils import run_bass_kernel_spmd

F32 = mybir.dt.float32
BF16 = mybir.dt.bfloat16
AF = mybir.ActivationFunctionType
ALU = mybir.AluOpType

NCORES = 8
T = 2048
D = 1024
S = 16384
FF = 2816
NT = T // 128
ND = D // 128
NF = FF // 128
NG = 32
EPS = 1e-5
SM_SCALE = 1.0 / 32.0
SM_SHIFT = -32.0

ENGINES = ("pe", "act", "dve", "gp", "sync")
SEM_OF = {"pe": "s_pe", "act": "s_act", "dve": "s_dve", "gp": "s_gp", "sync": "s_sy"}
DMA_SEMS = ("s_ld", "s_gd", "s_st")
SEM_NAMES = (["s_pe", "s_act", "s_dve", "s_gp", "s_sy", "s_cc",
              "sid", "swq", "sbq", "sqf0", "sqf1", "sstf",
              "sqt0", "sqt1", "sqt2", "sqt3", "sod", "soi0", "soi1",
              "sgk0", "sgk1", "sgv0", "sgv1", "sgv2", "sod2",
              "swc1", "swc3", "swc2",
              "ssp0", "ssp1", "ssp2", "ssp3", "so0", "so1"]
             + [f"sxt{j}" for j in range(8)] + [f"sxi{j}" for j in range(2)]
             + [f"sxw{j}" for j in range(2)] + [f"skb{j}" for j in range(2)]
             + [f"svb{j}" for j in range(3)] + [f"sw1{j}" for j in range(2)]
             + [f"sw3{j}" for j in range(2)] + [f"sw2{j}" for j in range(4)])


class Emitter:

    def __init__(self):
        self.cnt = {s: 0 for s in SEM_NAMES}
        self.blocks = []
        self.prog = None
        self.last_w = {}
        self.readers = {}
        self.pending_reads = {e: [] for e in ENGINES}
        self.pending_writes = {e: [] for e in ENGINES}
        self.barrier = {}

    def new_block(self):
        self.prog = {e: [] for e in ENGINES}
        self.blocks.append(self.prog)

    def set_barrier(self, exclude=()):
        self.barrier = {s: v for s, v in self.cnt.items() if v and s not in exclude}

    def op(self, eng, fn, reads=(), writes=(), sem=None, dma=False, inc=True):
        reads = list(reads)
        writes = list(writes)
        waits = {}

        def merge(m):
            if m is not None:
                waits[m[0]] = max(waits.get(m[0], 0), m[1])

        for b in reads:
            merge(self.last_w.get(b))
        for b in writes:
            for m in self.readers.get(b, ()):
                merge(m)
            merge(self.last_w.get(b))

        def scoped(b):
            k = b[0] if isinstance(b, tuple) else b
            return not (k.startswith("g:") or k.startswith("d:"))

        if any(scoped(b) for b in reads + writes):
            for s, v in self.barrier.items():
                waits[s] = max(waits.get(s, 0), v)
        mile = None
        amt = 16 if dma else 1
        if inc:
            s = sem or SEM_OF[eng]
            self.cnt[s] += amt
            mile = (s, self.cnt[s])
        self.prog[eng].append((waits, fn, mile, amt))
        if mile is not None:
            for b in self.pending_reads[eng]:
                self.readers.setdefault(b, []).append(mile)
            self.pending_reads[eng] = []
            for b in self.pending_writes[eng]:
                self.last_w[b] = mile
                self.readers[b] = []
            self.pending_writes[eng] = []
            for b in reads:
                self.readers.setdefault(b, []).append(mile)
            for b in writes:
                self.last_w[b] = mile
                self.readers[b] = []
        else:
            self.pending_reads[eng].extend(reads)
            self.pending_writes[eng].extend(writes)
        return mile


def build_program():
    nc = bass.Bass("TRN2", target_bir_lowering=False)

    x_h = nc.dram_tensor("x", [T, D], F32, kind="ExternalInput")
    wq_h = nc.dram_tensor("wq", [D, D], F32, kind="ExternalInput")
    bq_h = nc.dram_tensor("bq", [128, ND], F32, kind="ExternalInput")
    w1_h = nc.dram_tensor("w1", [D, FF], F32, kind="ExternalInput")
    w3_h = nc.dram_tensor("w3", [D, FF], F32, kind="ExternalInput")
    w2_h = nc.dram_tensor("w2", [FF, D], F32, kind="ExternalInput")
    id_h = nc.dram_tensor("ident", [128, 128], BF16, kind="ExternalInput")
    out_h = nc.dram_tensor("out", [T, D], F32, kind="ExternalOutput")

    qq_k = [nc.dram_tensor(f"qq_k{k}", [128, 4096], BF16) for k in range(4)]
    qq_v = [nc.dram_tensor(f"qq_v{k}", [128, 4096], BF16) for k in range(4)]
    g_k = [nc.dram_tensor(f"g_k{k}", [NCORES * 128, 4096], BF16,
                          addr_space="Shared") for k in range(4)]
    g_v = [nc.dram_tensor(f"g_v{k}", [NCORES * 128, 4096], BF16,
                          addr_space="Shared") for k in range(4)]
    xsp = nc.dram_tensor("xsp", [T, D], F32)
    odr = nc.dram_tensor("odr", [16 * 128, D], F32)
    w1b = nc.dram_tensor("w1b", [128, ND * FF], BF16)
    w3b = nc.dram_tensor("w3b", [128, ND * FF], BF16)
    w2c = nc.dram_tensor("w2c", [128, NF * D], BF16)

    em = Emitter()

    with ExitStack() as G:
        eg = G.enter_context
        sems = {s: eg(nc.semaphore(s)) for s in SEM_NAMES}

        ident = eg(nc.sbuf_tensor("ident_sb", [128, 128], BF16))
        ones = eg(nc.sbuf_tensor("ones", [128, 1], BF16))
        bqs = eg(nc.sbuf_tensor("bqs", [128, ND], F32))
        lsb = eg(nc.sbuf_tensor("lsb", [128, 16], F32))
        lrcp = eg(nc.sbuf_tensor("lrcp", [128, 16], F32))
        epsb = eg(nc.sbuf_tensor("epsb", [128, 1], F32))
        shb = eg(nc.sbuf_tensor("shb", [128, 1], F32))
        ln2T = eg(nc.sbuf_tensor("ln2T", [128, ND * T], BF16))
        w1s = [eg(nc.sbuf_tensor(f"w1s{j}", [128, ND * 128], BF16)) for j in range(2)]
        w3s = [eg(nc.sbuf_tensor(f"w3s{j}", [128, ND * 128], BF16)) for j in range(2)]

        def emit_w13_cast():
            for (dst, src, key, sem) in ((w1b, w1_h, "d:w1b", "swc1"),
                                         (w3b, w3_h, "d:w3b", "swc3")):
                def f(g, dst=dst, src=src):
                    i3 = src.ap().rearrange("(hh p) F -> p hh F", p=128)
                    o3 = dst.ap().rearrange("p (hh F) -> p hh F", hh=ND)
                    return g.dma_start(out=o3, in_=i3)
                em.op("gp", f, writes=[key], sem=sem, dma=True)

            def f2(g):
                i3 = w2_h.ap().rearrange("(ff p) d -> p ff d", p=128)
                o3 = w2c.ap().rearrange("p (ff d) -> p ff d", ff=NF)
                return g.dma_start(out=o3, in_=i3)
            em.op("gp", f2, writes=["d:w2c"], sem="swc2", dma=True)

        def emit_w13_load(gidx):
            ff = gidx % NF
            j = gidx % 2
            for (dst, src, key, sem) in ((w1s, w1b, "d:w1b", "sw1"),
                                         (w3s, w3b, "d:w3b", "sw3")):
                def f(g, dst=dst, src=src, ff=ff, j=j):
                    i3 = src.ap().rearrange("p (hh F) -> p hh F", hh=ND)[
                        :, :, ff * 128:(ff + 1) * 128]
                    o3 = dst[j][:, :].rearrange("p (hh f) -> p hh f", hh=ND)
                    return g.dma_start(out=o3, in_=i3)
                em.op("sync", f, reads=[key],
                      writes=[("g:w1s" if sem == "sw1" else "g:w3s", j)],
                      sem=sem + str(j), dma=True)

        P01 = ExitStack()
        qfm = P01.enter_context(nc.sbuf_tensor("qfm", [128, ND * T], BF16))
        kb = [P01.enter_context(nc.sbuf_tensor(f"kb{j}", [128, ND * 512], BF16))
              for j in range(2)]
        vb = [P01.enter_context(nc.sbuf_tensor(f"vb{j}", [128, 4 * D], BF16))
              for j in range(3)]

        em.new_block()
        with ExitStack() as P0:
            e0 = P0.enter_context
            xt = [e0(nc.sbuf_tensor(f"xt{j}", [128, D], F32)) for j in range(6)]
            bns = [e0(nc.sbuf_tensor(f"bns{j}", [128, 12], F32)) for j in range(4)]
            mv = [e0(nc.sbuf_tensor(f"mv{j}", [128, 2], F32)) for j in range(4)]
            sd = [e0(nc.sbuf_tensor(f"sd{j}", [128, 1], F32)) for j in range(4)]
            rs = [e0(nc.sbuf_tensor(f"rs{j}", [128, 1], F32)) for j in range(4)]
            ln1 = [e0(nc.sbuf_tensor(f"ln1_{j}", [128, D], BF16)) for j in range(4)]
            ln1T = e0(nc.sbuf_tensor("ln1T", [128, ND * T], BF16))
            wqb = e0(nc.sbuf_tensor("wqb", [128, ND * D], BF16))
            qtm = [e0(nc.sbuf_tensor(f"qtm{j}", [128, D], BF16)) for j in range(4)]
            ptr = [e0(nc.psum_tensor(f"ptr{j}", [128, D], BF16)) for j in range(2)]
            psq = [e0(nc.psum_tensor(f"psq{j}", [128, 512], F32)) for j in range(2)]

            em.op("sync", lambda s: s.dma_start(out=ident[:, :], in_=id_h[:, :]),
                  writes=["g:ident"], sem="sid", dma=True)
            em.op("dve", lambda v: v.memset(ones[:, :], 1.0), writes=["g:ones"])
            em.op("dve", lambda v: v.memset(epsb[:, :], EPS), writes=["g:epsb"])
            em.op("dve", lambda v: v.memset(shb[:, :], SM_SHIFT), writes=["g:shb"])
            em.op("gp", lambda g: g.dma_start(out=bqs[:, :], in_=bq_h[:, :]),
                  writes=["g:bq"], sem="sbq", dma=True)

            def f_wq(g):
                i3 = wq_h.ap().rearrange("(hh p) d -> p hh d", p=128)
                o3 = wqb[:, :].rearrange("p (hh d) -> p hh d", hh=ND)
                return g.dma_start(out=o3, in_=i3)
            em.op("gp", f_wq, writes=["wqb"], sem="swq", dma=True)

            def emit_agk(tq, extra=()):
                def f_ag(g, tq=tq):
                    return g.collective_compute(
                        "AllGather", ALU.bypass,
                        replica_groups=[list(range(NCORES))],
                        ins=[qq_k[tq].ap().opt()], outs=[g_k[tq].ap().opt()])
                em.op("gp", f_ag, reads=[("d:qfml", tq)] + list(extra),
                      writes=[("d:gqk", tq)], sem="s_cc")

            def emit_agv(tq, extra=()):
                def f_ag(g, tq=tq):
                    return g.collective_compute(
                        "AllGather", ALU.bypass,
                        replica_groups=[list(range(NCORES))],
                        ins=[qq_v[tq].ap().opt()], outs=[g_v[tq].ap().opt()])
                em.op("gp", f_ag,
                      reads=[("d:qtml", i_) for i_ in range(tq * 4, tq * 4 + 4)]
                      + list(extra),
                      writes=[("d:gqv", tq)], sem="s_cc")

            def emit_x_load(i):
                j8 = i % 6

                def f_load(s, i=i, j8=j8):
                    return s.dma_start(out=xt[j8][:, :],
                                       in_=x_h[i * 128:(i + 1) * 128, :])
                em.op("sync" if i % 2 == 0 else "act", f_load,
                      writes=[("xt", j8)], sem=f"sxt{j8}", dma=True)

            for i in range(6):
                emit_x_load(i)
            for tq in range(4):
                ws = [tq * 4 + k for k in range(4)]
                for i in ws:
                    j = i % 4
                    j8 = i % 6
                    em.op("dve", lambda v, j=j, j8=j8: v.bn_stats(
                        out=bns[j][:, 0:6], in_=xt[j8][:, 0:512]),
                        reads=[("xt", j8)], writes=[("bns", j)], inc=False)
                    em.op("dve", lambda v, j=j, j8=j8: v.bn_stats(
                        out=bns[j][:, 6:12], in_=xt[j8][:, 512:1024]),
                        reads=[("xt", j8)], writes=[("bns", j)])
                for i in ws:
                    j = i % 4
                    em.op("dve", lambda v, j=j: v.bn_aggr(out=mv[j][:, :], in_=bns[j][:, :]),
                          reads=[("bns", j)], writes=[("mv", j)])
                for i in ws:
                    j = i % 4
                    em.op("act", lambda s, j=j: s.activation(out=sd[j][:, :], in_=mv[j][:, 1:2],
                                                             func=AF.Sqrt, bias=epsb[:, :]),
                          reads=[("mv", j), "g:epsb"], writes=[("sd", j)])
                for i in ws:
                    j = i % 4
                    em.op("dve", lambda v, j=j: v.reciprocal(out=rs[j][:, :], in_=sd[j][:, :]),
                          reads=[("sd", j)], writes=[("rs", j)])
                for i in ws:
                    j = i % 4
                    j8 = i % 6
                    em.op("dve", lambda v, j=j, j8=j8: v.tensor_scalar(
                        out=ln1[j][:, :], in0=xt[j8][:, :], scalar1=mv[j][:, 0:1],
                        scalar2=rs[j][:, :], op0=ALU.subtract, op1=ALU.mult),
                        reads=[("xt", j8), ("mv", j), ("rs", j)], writes=[("ln1", j)])
                for i in ws:
                    j = i % 4
                    jp = i % 2

                    def f_tr(t, j=j, jp=jp):
                        r = None
                        for hh in range(ND):
                            r = t.transpose(out=ptr[jp][:, hh * 128:(hh + 1) * 128],
                                            in_=ln1[j][:, hh * 128:(hh + 1) * 128],
                                            identity=ident[:, :])
                        return r
                    em.op("pe", f_tr, reads=[("ln1", j), "g:ident"], writes=[("ptr", jp)])

                    def f_cp(v, i=i, jp=jp):
                        o3 = ln1T[:, :].rearrange("p (hh t) -> p hh t", hh=ND)[:, :, i * 128:(i + 1) * 128]
                        i3 = ptr[jp][:, :].rearrange("p (hh t) -> p hh t", hh=ND)
                        return v.tensor_copy(out=o3, in_=i3)
                    em.op("dve", f_cp, reads=[("ptr", jp)],
                          writes=[("ln1T", hh, i) for hh in range(ND)])

                for i in range(4 * tq + 6, 4 * tq + 10):
                    if i < 16:
                        emit_x_load(i)

                for dd in range(ND):
                    jq = (tq * ND + dd) % 2

                    def f_mm(t, dd=dd, tq=tq, jq=jq):
                        r = None
                        for hh in range(ND):
                            r = t.matmul(out=psq[jq][:, :],
                                         lhsT=wqb[:, hh * D + dd * 128: hh * D + (dd + 1) * 128],
                                         rhs=ln1T[:, hh * T + tq * 512: hh * T + (tq + 1) * 512],
                                         start=(hh == 0), stop=(hh == ND - 1))
                        return r
                    em.op("pe", f_mm,
                          reads=[("ln1T", hh, i) for hh in range(ND)
                                 for i in range(tq * 4, tq * 4 + 4)] + ["wqb"],
                          writes=[("psq", jq)])

                    def f_bias(s, dd=dd, tq=tq, jq=jq):
                        return s.activation(
                            out=qfm[:, dd * T + tq * 512: dd * T + (tq + 1) * 512],
                            in_=psq[jq][:, :], func=AF.Identity,
                            bias=bqs[:, dd:dd + 1], scale=1.0)
                    em.op("act", f_bias, reads=[("psq", jq), "g:bq"],
                          writes=[("qfm", dd, tq)])

                for i in range(tq * 4, tq * 4 + 4):
                    j = i % 2
                    js = i % 4

                    def f_trq(t, i=i, j=j):
                        r = None
                        for dd in range(ND):
                            r = t.transpose(out=ptr[j][:, dd * 128:(dd + 1) * 128],
                                            in_=qfm[:, dd * T + i * 128: dd * T + (i + 1) * 128],
                                            identity=ident[:, :])
                        return r
                    em.op("pe", f_trq,
                          reads=[("qfm", dd, i // 4) for dd in range(ND)] + ["g:ident"],
                          writes=[("ptr", j)])
                    em.op("dve", lambda v, j=j, js=js: v.tensor_copy(
                        out=qtm[js][:, :], in_=ptr[j][:, :]),
                        reads=[("ptr", j)], writes=[("qtm", js)])

                    def f_st(g, i=i, js=js):
                        return g.dma_start(
                            out=qq_v[i // 4][:, (i % 4) * 1024:(i % 4 + 1) * 1024],
                            in_=qtm[js][:, :])
                    em.op("sync", f_st, reads=[("qtm", js)], writes=[("d:qtml", i)],
                          sem=f"sqt{js}", dma=True)

                for kt in range(4):
                    def f_stfm(g, tq=tq, kt=kt):
                        o3 = qq_k[tq].ap()[:, kt * 1024:(kt + 1) * 1024].rearrange(
                            "p (hh c) -> p hh c", hh=ND)
                        tok0 = tq * 512 + kt * 128
                        i3 = qfm[:, :].rearrange(
                            "p (dd t) -> p dd t", dd=ND)[:, :, tok0:tok0 + 128]
                        return g.dma_start(out=o3, in_=i3)
                    em.op("sync", f_stfm,
                          reads=[("qfm", dd, tq) for dd in range(ND)],
                          writes=[("d:qfml", tq)],
                          sem="sstf", dma=True)

                if tq == 0:
                    for jx in range(6):
                        em.last_w[("d:xall", jx)] = (f"sxt{jx}", em.cnt[f"sxt{jx}"])
                    emit_agk(0, [("d:xall", jx) for jx in range(6)])
                    for jw, rw in ((0, 0), (1, 1)):
                        em.op("gp", lambda g, jw=jw, rw=rw: g.dma_start(
                            out=kb[jw][:, :],
                            in_=g_k[0].ap()[rw * 128:(rw + 1) * 128, :]),
                            reads=[("d:gqk", 0)], writes=[("kb", jw)], sem=f"sgk{jw}", dma=True)
                    for s_ in ("sgk0", "sgk1"):
                        em.last_w[("d:kwarm", s_)] = (s_, 16)
                    emit_agv(0, [("d:kwarm", s_) for s_ in ("sgk0", "sgk1")])
                    for jw, rw in ((0, 0), (1, 1), (2, 2)):
                        em.op("gp", lambda g, jw=jw, rw=rw: g.dma_start(
                            out=vb[jw][:, :],
                            in_=g_v[0].ap()[rw * 128:(rw + 1) * 128, :]),
                            reads=[("d:gqv", 0)], writes=[("vb", jw)], sem=f"sgv{jw}", dma=True)
                    for s_, v_ in (("sgv0", 16), ("sgv1", 16), ("sgv2", 16)):
                        em.last_w[("d:p1warm", s_)] = (s_, v_)
                elif tq == 1:
                    emit_agk(1, [("d:p1warm", s_) for s_ in
                                 ("sgv0", "sgv1", "sgv2")])
                    emit_agv(1)
                else:
                    emit_agk(tq)
                    emit_agv(tq)

            em.set_barrier(exclude=("s_cc", "sstf", "sqt0", "sqt1", "sqt2", "sqt3"))

        em.new_block()
        with ExitStack() as P1:
            e1 = P1.enter_context
            pT = e1(nc.sbuf_tensor("pT", [128, 8 * 1024], BF16))
            osb = [e1(nc.sbuf_tensor(f"osb{q}", [128, D], F32)) for q in range(8)]
            oin = [e1(nc.sbuf_tensor(f"oin{j}", [128, D], F32)) for j in range(2)]
            at1 = [e1(nc.sbuf_tensor(f"at1_{j}", [128, D], F32)) for j in range(2)]
            xin = [e1(nc.sbuf_tensor(f"xin{j}", [128, D], F32)) for j in range(2)]
            xnu = [e1(nc.sbuf_tensor(f"xnu{j}", [128, D], F32)) for j in range(3)]
            ln2b = [e1(nc.sbuf_tensor(f"ln2b{j}", [128, D], BF16)) for j in range(2)]
            bn2 = [e1(nc.sbuf_tensor(f"bn2_{j}", [128, 12], F32)) for j in range(8)]
            mv2 = [e1(nc.sbuf_tensor(f"mv2_{j}", [128, 2], F32)) for j in range(8)]
            sd2 = [e1(nc.sbuf_tensor(f"sd2_{j}", [128, 1], F32)) for j in range(8)]
            rs2 = [e1(nc.sbuf_tensor(f"rs2_{j}", [128, 1], F32)) for j in range(8)]
            nm2 = [e1(nc.sbuf_tensor(f"nm2_{j}", [128, 1], F32)) for j in range(8)]
            pss = [e1(nc.psum_tensor(f"pss{j}", [128, 512], F32)) for j in range(2)]
            pso = [e1(nc.psum_tensor(f"pso{j}", [128, D], F32)) for j in range(2)]
            psl = e1(nc.psum_tensor("psl", [128, 8], F32))
            ptr2 = e1(nc.psum_tensor("ptr2", [128, D], BF16))

            QSEQ = [4 * r + qr for qr in range(4) for r in range(NCORES)]

            def emit_kb_load(qpos):
                m = QSEQ[qpos % 32]
                jb = qpos % 2
                r = m // 4
                qr = m % 4

                def f(s, r=r, qr=qr, jb=jb):
                    return s.dma_start(
                        out=kb[jb][:, :],
                        in_=g_k[qr].ap()[r * 128:(r + 1) * 128, :])
                em.op("sync", f, reads=[("d:gqk", qr)], writes=[("kb", jb)],
                      sem=f"skb{jb}", dma=True)

            def emit_vb_load(qpos):
                m = QSEQ[qpos % 32]
                jb = qpos % 3
                r = m // 4
                qr = m % 4

                def f(s, r=r, qr=qr, jb=jb):
                    return s.dma_start(
                        out=vb[jb][:, :],
                        in_=g_v[qr].ap()[r * 128:(r + 1) * 128, :])
                em.op("sync", f, reads=[("d:gqv", qr)], writes=[("vb", jb)],
                      sem=f"svb{jb}", dma=True)

            def emit_xin_load(qq):
                j = qq % 2

                def f(g, qq=qq, j=j):
                    return g.dma_start(out=xin[j][:, :], in_=x_h[qq * 128:(qq + 1) * 128, :])
                em.op("gp", f, writes=[("xin", j)], sem=f"sxi{j}", dma=True)

            def emit_oin_load(k):
                em.op("sync", lambda s, k=k: s.dma_start(
                    out=oin[k % 2][:, :], in_=odr[k * 128:(k + 1) * 128, :]),
                    reads=[("d:osp", q) for q in range(8)], writes=[("oin", k % 2)],
                    sem=f"soi{k % 2}", dma=True)

            def emit_epi_l(hf, qs):
                qq = hf * 8 + qs
                em.op("dve", lambda v, hf=hf, qs=qs, qq=qq: v.tensor_add(
                    out=lsb[:, qq:qq + 1], in0=lsb[:, qq:qq + 1], in1=psl[:, qs:qs + 1]),
                    reads=[("psl", qs), ("g:lsb", hf)], writes=[("g:lsbq", qq)])
                em.op("dve", lambda v, qq=qq: v.reciprocal(
                    out=lrcp[:, qq:qq + 1], in_=lsb[:, qq:qq + 1]),
                    reads=[("g:lsbq", qq)], writes=[("g:lrcpq", qq)])

            def emit_c0_s1(qs):
                em.op("act", lambda s, qs=qs: s.activation(
                    out=at1[qs % 2][:, :], in_=oin[qs % 2][:, :], func=AF.Identity,
                    scale=lrcp[:, qs:qs + 1]),
                    reads=[("oin", qs % 2), ("g:lrcpq", qs)], writes=[("at1", qs % 2)])
                em.op("dve", lambda v, qs=qs: v.bn_stats(
                    out=bn2[qs][:, 0:6], in_=at1[qs % 2][:, 0:512]),
                    reads=[("at1", qs % 2)], writes=[("bn2", qs)], inc=False)
                em.op("dve", lambda v, qs=qs: v.bn_stats(
                    out=bn2[qs][:, 6:12], in_=at1[qs % 2][:, 512:1024]),
                    reads=[("at1", qs % 2)], writes=[("bn2", qs)])
                em.op("dve", lambda v, qs=qs: v.bn_aggr(out=mv2[qs][:, :], in_=bn2[qs][:, :]),
                      reads=[("bn2", qs)], writes=[("mv2", qs)])

            def emit_c0_s2(qs):
                em.op("act", lambda s, qs=qs: s.activation(
                    out=sd2[qs][:, :], in_=mv2[qs][:, 1:2], func=AF.Sqrt, bias=epsb[:, :]),
                    reads=[("mv2", qs), "g:epsb"], writes=[("sd2", qs)])
                em.op("dve", lambda v, qs=qs: v.reciprocal(out=rs2[qs][:, :], in_=sd2[qs][:, :]),
                      reads=[("sd2", qs)], writes=[("rs2", qs)])
                em.op("dve", lambda v, qs=qs: v.tensor_scalar(
                    out=nm2[qs][:, :], in0=mv2[qs][:, 0:1], scalar1=rs2[qs][:, :],
                    scalar2=-1.0, op0=ALU.mult, op1=ALU.mult),
                    reads=[("mv2", qs), ("rs2", qs)], writes=[("nm2", qs)])

            def emit_c0_s3(qs):
                em.op("act", lambda s, qs=qs: s.activation(
                    out=xnu[qs % 3][:, :], in_=at1[qs % 2][:, :], func=AF.Identity,
                    bias=nm2[qs][:, :], scale=rs2[qs][:, :]),
                    reads=[("at1", qs % 2), ("nm2", qs), ("rs2", qs)],
                    writes=[("xnu", qs % 3)])
                em.op("gp", lambda g_, qs=qs: g_.tensor_add(
                    out=xnu[qs % 3][:, :], in0=xnu[qs % 3][:, :], in1=xin[qs % 2][:, :]),
                    reads=[("xnu", qs % 3), ("xin", qs % 2)], writes=[("xnu", qs % 3)])
                em.op("gp", lambda g_, qs=qs: g_.dma_start(
                    out=xsp[qs * 128:(qs + 1) * 128, :], in_=xnu[qs % 3][:, :]),
                    reads=[("xnu", qs % 3)], writes=[("d:xsp", qs)],
                    sem=f"ssp{qs % 4}", dma=True)
                if qs + 2 < 8:
                    emit_xin_load(qs + 2)

            def emit_c0_s4(qs):
                em.op("dve", lambda v, qs=qs: v.bn_stats(
                    out=bn2[qs][:, 0:6], in_=xnu[qs % 3][:, 0:512]),
                    reads=[("xnu", qs % 3)], writes=[("bn2", qs)], inc=False)
                em.op("dve", lambda v, qs=qs: v.bn_stats(
                    out=bn2[qs][:, 6:12], in_=xnu[qs % 3][:, 512:1024]),
                    reads=[("xnu", qs % 3)], writes=[("bn2", qs)])
                em.op("dve", lambda v, qs=qs: v.bn_aggr(out=mv2[qs][:, :], in_=bn2[qs][:, :]),
                      reads=[("bn2", qs)], writes=[("mv2", qs)])

            def emit_c0_s5(qs):
                em.op("act", lambda s, qs=qs: s.activation(
                    out=sd2[qs][:, :], in_=mv2[qs][:, 1:2], func=AF.Sqrt, bias=epsb[:, :]),
                    reads=[("mv2", qs), "g:epsb"], writes=[("sd2", qs)])
                em.op("dve", lambda v, qs=qs: v.reciprocal(out=rs2[qs][:, :], in_=sd2[qs][:, :]),
                      reads=[("sd2", qs)], writes=[("rs2", qs)])
                em.op("dve", lambda v, qs=qs: v.tensor_scalar(
                    out=ln2b[qs % 2][:, :], in0=xnu[qs % 3][:, :], scalar1=mv2[qs][:, 0:1],
                    scalar2=rs2[qs][:, :], op0=ALU.subtract, op1=ALU.mult),
                    reads=[("xnu", qs % 3), ("mv2", qs), ("rs2", qs)],
                    writes=[("ln2b", qs % 2)])

            def emit_epi_tail(hf, qs):
                qq = hf * 8 + qs
                jl = qq % 2

                def f_tr2(t, jl=jl):
                    r = None
                    for hh in range(ND):
                        r = t.transpose(out=ptr2[:, hh * 128:(hh + 1) * 128],
                                        in_=ln2b[jl][:, hh * 128:(hh + 1) * 128],
                                        identity=ident[:, :])
                    return r
                em.op("pe", f_tr2, reads=[("ln2b", jl), "g:ident"], writes=["ptr2"])

                def f_cp2(v, qq=qq):
                    o3 = ln2T[:, :].rearrange("p (hh t) -> p hh t", hh=ND)[:, :, qq * 128:(qq + 1) * 128]
                    i3 = ptr2[:, :].rearrange("p (hh t) -> p hh t", hh=ND)
                    return v.tensor_copy(out=o3, in_=i3)
                em.op("dve", f_cp2, reads=["ptr2"],
                      writes=[("g:ln2T", hh, qq) for hh in range(ND)])

            emit_xin_load(0)
            emit_xin_load(1)
            emit_w13_cast()

            for gg in range(NG):
                hf = gg // 16
                pos = gg % 16

                qp0 = hf * 32 + 2 * pos
                for t in range(8):
                    kt = t % 4
                    jb = (qp0 + t // 4) % 2
                    for qc in range(2):
                        jss = (t * 2 + qc) % 2

                        def f_sc(tt, jb=jb, kt=kt, hf=hf, qc=qc, jss=jss):
                            r = None
                            for hh in range(ND):
                                r = tt.matmul(
                                    out=pss[jss][:, :],
                                    lhsT=kb[jb][:, kt * 1024 + hh * 128: kt * 1024 + (hh + 1) * 128],
                                    rhs=qfm[:, hh * T + hf * 1024 + qc * 512:
                                            hh * T + hf * 1024 + (qc + 1) * 512],
                                    start=(hh == 0), stop=(hh == ND - 1))
                            return r
                        em.op("pe", f_sc,
                              reads=[("kb", jb)]
                              + [("qfm", d_, 2 * hf + qc) for d_ in range(ND)],
                              writes=[("pss", jss)])

                        def f_exp(s, t=t, qc=qc, jss=jss):
                            return s.activation(
                                out=pT[:, t * 1024 + qc * 512: t * 1024 + (qc + 1) * 512],
                                in_=pss[jss][:, :], func=AF.Exp,
                                bias=shb[:, :], scale=SM_SCALE)
                        em.op("act", f_exp, reads=[("pss", jss), "g:shb"], writes=[("pT", t, qc)])

                for mm_ in (qp0 + 2, qp0 + 3):
                    if mm_ < 64:
                        emit_kb_load(mm_)

                jb4a = qp0 % 3
                jb4b = (qp0 + 1) % 3
                for qs in range(8):
                    jo = (gg * 8 + qs) % 2

                    def f_pv(tt, jb4a=jb4a, jb4b=jb4b, qs=qs, jo=jo):
                        r = None
                        for t in range(8):
                            kt = t % 4
                            jb4 = jb4a if t < 4 else jb4b
                            lhsT = pT[:, t * 1024 + qs * 128: t * 1024 + (qs + 1) * 128]
                            tt.matmul(out=pso[jo][:, 0:512], lhsT=lhsT,
                                      rhs=vb[jb4][:, kt * D: kt * D + 512],
                                      start=(t == 0), stop=(t == 7))
                            tt.matmul(out=pso[jo][:, 512:1024], lhsT=lhsT,
                                      rhs=vb[jb4][:, kt * D + 512: (kt + 1) * D],
                                      start=(t == 0), stop=(t == 7))
                            r = tt.matmul(out=psl[:, qs:qs + 1], lhsT=lhsT,
                                          rhs=ones[:, 0:1],
                                          start=(t == 0), stop=(t == 7))
                        return r
                    em.op("pe", f_pv,
                          reads=[("pT", t, qs // 4) for t in range(8)]
                          + [("vb", jb4a), ("vb", jb4b), "g:ones"],
                          writes=[("pso", jo), ("psl", qs)])

                    if pos == 0:
                        em.op("dve", lambda v, qs=qs, jo=jo: v.tensor_copy(
                            out=osb[qs][:, :], in_=pso[jo][:, :]),
                            reads=[("pso", jo)]
                            + ([("d:osp", q) for q in range(8)] if hf == 1 else []),
                            writes=[("osb", qs)])
                    else:
                        em.op("dve", lambda v, qs=qs, jo=jo: v.tensor_add(
                            out=osb[qs][:, :], in0=osb[qs][:, :], in1=pso[jo][:, :]),
                            reads=[("pso", jo), ("osb", qs)], writes=[("osb", qs)])
                    if pos == 15:
                        emit_epi_l(hf, qs)
                        if hf == 0:
                            em.op("gp", lambda g_, qs=qs: g_.dma_start(
                                out=odr[qs * 128:(qs + 1) * 128, :], in_=osb[qs][:, :]),
                                reads=[("osb", qs)], writes=[("d:osp", qs)],
                                sem="sod", dma=True)
                        else:
                            em.op("sync", lambda s, qs=qs: s.dma_start(
                                out=odr[(8 + qs) * 128:(9 + qs) * 128, :],
                                in_=osb[qs][:, :]),
                                reads=[("osb", qs)], writes=[("d:osp", 8 + qs)],
                                sem="sod2", dma=True)

                for mm_ in (qp0 + 3, qp0 + 4):
                    if 2 < mm_ < 64:
                        emit_vb_load(mm_)

                if pos == 15:
                    pass
                elif pos == 0:
                    em.op("dve", lambda v, hf=hf: v.tensor_copy(
                        out=lsb[:, hf * 8:(hf + 1) * 8], in_=psl[:, :]),
                        reads=[("psl", q) for q in range(8)], writes=[("g:lsb", hf)])
                else:
                    em.op("dve", lambda v, hf=hf: v.tensor_add(
                        out=lsb[:, hf * 8:(hf + 1) * 8],
                        in0=lsb[:, hf * 8:(hf + 1) * 8], in1=psl[:, :]),
                        reads=[("psl", q) for q in range(8)] + [("g:lsb", hf)],
                        writes=[("g:lsb", hf)])

                if hf == 0 and pos == 15:
                    emit_oin_load(0)
                    emit_oin_load(1)
                if hf == 1:
                    if 0 <= pos - 5 < 8:
                        emit_epi_tail(0, pos - 5)
                    if 0 <= pos - 4 < 8:
                        emit_c0_s5(pos - 4)
                    if 0 <= pos - 3 < 8:
                        emit_c0_s4(pos - 3)
                    if 0 <= pos - 2 < 8:
                        emit_c0_s3(pos - 2)
                    if 0 <= pos - 1 < 8:
                        emit_c0_s2(pos - 1)
                    if pos < 8:
                        emit_c0_s1(pos)
                        if pos + 2 < 8:
                            emit_oin_load(pos + 2)

                if gg == 29:
                    emit_w13_load(0)
                    emit_w13_load(1)

            em.set_barrier()

        P01.close()

        em.new_block()
        with ExitStack() as P2:
            e2 = P2.enter_context
            h_q = [e2(nc.sbuf_tensor(f"h_q{j}", [128, NF * 512], BF16)) for j in range(2)]
            w2b = e2(nc.sbuf_tensor("w2b", [128, NF * D], BF16))
            gsl = [e2(nc.sbuf_tensor(f"gsl{j}", [128, 512], F32)) for j in range(2)]
            xnw = [e2(nc.sbuf_tensor(f"xnw{j}", [128, D], F32)) for j in range(2)]
            otp = [e2(nc.sbuf_tensor(f"otp{j}", [128, D], F32)) for j in range(2)]
            bn3 = [e2(nc.sbuf_tensor(f"bn3_{j}", [128, 12], F32)) for j in range(2)]
            mv3 = [e2(nc.sbuf_tensor(f"mv3_{j}", [128, 2], F32)) for j in range(2)]
            sd3 = [e2(nc.sbuf_tensor(f"sd3_{j}", [128, 1], F32)) for j in range(2)]
            rs3 = [e2(nc.sbuf_tensor(f"rs3_{j}", [128, 1], F32)) for j in range(2)]
            nm3 = [e2(nc.sbuf_tensor(f"nm3_{j}", [128, 1], F32)) for j in range(2)]
            oin2 = [e2(nc.sbuf_tensor(f"oin2_{j}", [128, D], F32)) for j in range(2)]
            at2 = [e2(nc.sbuf_tensor(f"at2_{j}", [128, D], F32)) for j in range(4)]
            xnu2 = [e2(nc.sbuf_tensor(f"xnu2_{j}", [128, D], F32)) for j in range(4)]
            xin2 = [e2(nc.sbuf_tensor(f"xin2_{j}", [128, D], F32)) for j in range(2)]
            ln2c = [e2(nc.sbuf_tensor(f"ln2c{j}", [128, D], BF16)) for j in range(2)]
            bn4 = [e2(nc.sbuf_tensor(f"bn4_{k}", [128, 12], F32)) for k in range(8)]
            mv4 = [e2(nc.sbuf_tensor(f"mv4_{k}", [128, 2], F32)) for k in range(8)]
            sd4 = [e2(nc.sbuf_tensor(f"sd4_{k}", [128, 1], F32)) for k in range(8)]
            rs4 = [e2(nc.sbuf_tensor(f"rs4_{k}", [128, 1], F32)) for k in range(8)]
            nm4 = [e2(nc.sbuf_tensor(f"nm4_{k}", [128, 1], F32)) for k in range(8)]
            psg = e2(nc.psum_tensor("psg", [128, 512], F32))
            psu = [e2(nc.psum_tensor(f"psu{j}", [128, 512], F32)) for j in range(2)]
            psd = [e2(nc.psum_tensor(f"psd{j}", [128, D], F32)) for j in range(2)]
            ptr3 = e2(nc.psum_tensor("ptr3", [128, D], BF16))

            def emit_w2_load(part):
                ranges = [(0, 6), (6, 12), (12, 18), (18, 22)]
                f0, f1 = ranges[part]

                def f_w2(g, f0=f0, f1=f1):
                    return g.dma_start(out=w2b[:, f0 * D:f1 * D],
                                       in_=w2c.ap()[:, f0 * D:f1 * D])
                em.op("gp", f_w2, reads=["d:w2c"], writes=[("w2b", part)],
                      sem=f"sw2{part}", dma=True)

            def emit_oin2_load(k):
                em.op("sync", lambda s, k=k: s.dma_start(
                    out=oin2[k % 2][:, :], in_=odr[(8 + k) * 128:(9 + k) * 128, :]),
                    reads=[("d:osp", 8 + q) for q in range(8)], writes=[("oin2", k % 2)],
                    sem=f"soi{k % 2}", dma=True)

            def emit_xin2_load(k):
                em.op("gp", lambda g, k=k: g.dma_start(
                    out=xin2[k % 2][:, :], in_=x_h[(8 + k) * 128:(9 + k) * 128, :]),
                    writes=[("xin2", k % 2)], sem=f"sxi{k % 2}", dma=True)

            def emit_c1_s1(k):
                qq = 8 + k
                em.op("act", lambda s, k=k, qq=qq: s.activation(
                    out=at2[k % 4][:, :], in_=oin2[k % 2][:, :], func=AF.Identity,
                    scale=lrcp[:, qq:qq + 1]),
                    reads=[("oin2", k % 2), ("g:lrcpq", qq)], writes=[("at2", k % 4)])
                em.op("dve", lambda v, k=k: v.bn_stats(
                    out=bn4[k][:, 0:6], in_=at2[k % 4][:, 0:512]),
                    reads=[("at2", k % 4)], writes=[("bn4", k)], inc=False)
                em.op("dve", lambda v, k=k: v.bn_stats(
                    out=bn4[k][:, 6:12], in_=at2[k % 4][:, 512:1024]),
                    reads=[("at2", k % 4)], writes=[("bn4", k)])
                em.op("dve", lambda v, k=k: v.bn_aggr(out=mv4[k][:, :], in_=bn4[k][:, :]),
                      reads=[("bn4", k)], writes=[("mv4", k)])

            def emit_c1_s2(k):
                em.op("act", lambda s, k=k: s.activation(
                    out=sd4[k][:, :], in_=mv4[k][:, 1:2], func=AF.Sqrt, bias=epsb[:, :]),
                    reads=[("mv4", k), "g:epsb"], writes=[("sd4", k)])
                em.op("dve", lambda v, k=k: v.reciprocal(out=rs4[k][:, :], in_=sd4[k][:, :]),
                      reads=[("sd4", k)], writes=[("rs4", k)])
                em.op("dve", lambda v, k=k: v.tensor_scalar(
                    out=nm4[k][:, :], in0=mv4[k][:, 0:1], scalar1=rs4[k][:, :],
                    scalar2=-1.0, op0=ALU.mult, op1=ALU.mult),
                    reads=[("mv4", k), ("rs4", k)], writes=[("nm4", k)])

            def emit_c1_s3(k):
                qq = 8 + k
                em.op("act", lambda s, k=k: s.activation(
                    out=xnu2[k % 4][:, :], in_=at2[k % 4][:, :], func=AF.Identity,
                    bias=nm4[k][:, :], scale=rs4[k][:, :]),
                    reads=[("at2", k % 4), ("nm4", k), ("rs4", k)],
                    writes=[("xnu2", k % 4)])
                em.op("gp", lambda g_, k=k: g_.tensor_add(
                    out=xnu2[k % 4][:, :], in0=xnu2[k % 4][:, :], in1=xin2[k % 2][:, :]),
                    reads=[("xnu2", k % 4), ("xin2", k % 2)], writes=[("xnu2", k % 4)])
                em.op("gp", lambda g_, k=k, qq=qq: g_.dma_start(
                    out=xsp[qq * 128:(qq + 1) * 128, :], in_=xnu2[k % 4][:, :]),
                    reads=[("xnu2", k % 4)], writes=[("d:xsp", qq)],
                    sem=f"ssp{k % 4}", dma=True)
                if k + 2 < 8:
                    emit_xin2_load(k + 2)

            def emit_c1_s4(k):
                em.op("dve", lambda v, k=k: v.bn_stats(
                    out=bn4[k][:, 0:6], in_=xnu2[k % 4][:, 0:512]),
                    reads=[("xnu2", k % 4)], writes=[("bn4", k)], inc=False)
                em.op("dve", lambda v, k=k: v.bn_stats(
                    out=bn4[k][:, 6:12], in_=xnu2[k % 4][:, 512:1024]),
                    reads=[("xnu2", k % 4)], writes=[("bn4", k)])
                em.op("dve", lambda v, k=k: v.bn_aggr(out=mv4[k][:, :], in_=bn4[k][:, :]),
                      reads=[("bn4", k)], writes=[("mv4", k)])

            def emit_c1_s5(k):
                em.op("act", lambda s, k=k: s.activation(
                    out=sd4[k][:, :], in_=mv4[k][:, 1:2], func=AF.Sqrt, bias=epsb[:, :]),
                    reads=[("mv4", k), "g:epsb"], writes=[("sd4", k)])
                em.op("dve", lambda v, k=k: v.reciprocal(out=rs4[k][:, :], in_=sd4[k][:, :]),
                      reads=[("sd4", k)], writes=[("rs4", k)])
                em.op("dve", lambda v, k=k: v.tensor_scalar(
                    out=ln2c[k % 2][:, :], in0=xnu2[k % 4][:, :], scalar1=mv4[k][:, 0:1],
                    scalar2=rs4[k][:, :], op0=ALU.subtract, op1=ALU.mult),
                    reads=[("xnu2", k % 4), ("mv4", k), ("rs4", k)],
                    writes=[("ln2c", k % 2)])

            def emit_c1_s6(k):
                qq = 8 + k

                def f_tr(t, k=k):
                    r = None
                    for hh in range(ND):
                        r = t.transpose(out=ptr3[:, hh * 128:(hh + 1) * 128],
                                        in_=ln2c[k % 2][:, hh * 128:(hh + 1) * 128],
                                        identity=ident[:, :])
                    return r
                em.op("pe", f_tr, reads=[("ln2c", k % 2), "g:ident"], writes=["ptr3"])

                def f_cp(v, qq=qq):
                    o3 = ln2T[:, :].rearrange("p (hh t) -> p hh t", hh=ND)[
                        :, :, qq * 128:(qq + 1) * 128]
                    i3 = ptr3[:, :].rearrange("p (hh t) -> p hh t", hh=ND)
                    return v.tensor_copy(out=o3, in_=i3)
                em.op("dve", f_cp, reads=["ptr3"],
                      writes=[("g:ln2T", hh, qq) for hh in range(ND)])

            C1_SEGS = (emit_c1_s1, emit_c1_s2, emit_c1_s3, emit_c1_s4,
                       emit_c1_s5, emit_c1_s6)
            seg_sched = {}
            for k_ in range(8):
                for jseg in range(6):
                    seg_sched.setdefault(2 * k_ + 3 * jseg, []).append((k_, jseg))

            emit_oin2_load(0)
            emit_oin2_load(1)
            emit_xin2_load(0)
            emit_xin2_load(1)

            def emit_xnw_load(tt):
                j = tt % 2

                def f(g, tt=tt, j=j):
                    return g.dma_start(out=xnw[j][:, :], in_=xsp[tt * 128:(tt + 1) * 128, :])
                em.op("gp", f, reads=[("d:xsp", tt)], writes=[("xnw", j)],
                      sem=f"sxw{j}", dma=True)

            def emit_g_group(q, ff, gidx):
                j = gidx % 2
                jg = gidx % 2

                def f_gate(t, j=j, q=q):
                    r = None
                    for hh in range(ND):
                        r = t.matmul(out=psg[:, :],
                                     lhsT=w1s[j][:, hh * 128:(hh + 1) * 128],
                                     rhs=ln2T[:, hh * T + q * 512: hh * T + (q + 1) * 512],
                                     start=(hh == 0), stop=(hh == ND - 1))
                    return r
                em.op("pe", f_gate,
                      reads=[("g:w1s", j)] + [("g:ln2T", hh, i) for hh in range(ND)
                                              for i in range(q * 4, q * 4 + 4)],
                      writes=["psg"])

                em.op("act", lambda s, jg=jg: s.activation(
                    out=gsl[jg][:, :], in_=psg[:, :], func=AF.Silu),
                    reads=["psg"], writes=[("gsl", jg)])

                def f_up(t, j=j, q=q, jg=jg):
                    r = None
                    for hh in range(ND):
                        r = t.matmul(out=psu[jg][:, :],
                                     lhsT=w3s[j][:, hh * 128:(hh + 1) * 128],
                                     rhs=ln2T[:, hh * T + q * 512: hh * T + (q + 1) * 512],
                                     start=(hh == 0), stop=(hh == ND - 1))
                    return r
                em.op("pe", f_up,
                      reads=[("g:w3s", j)] + [("g:ln2T", hh, i) for hh in range(ND)
                                              for i in range(q * 4, q * 4 + 4)],
                      writes=[("psu", jg)])

                def f_h(v, ff=ff, q=q, jg=jg):
                    return v.tensor_mul(
                        out=h_q[q % 2][:, ff * 512:(ff + 1) * 512],
                        in0=gsl[jg][:, :], in1=psu[jg][:, :])
                em.op("dve", f_h, reads=[("gsl", jg), ("psu", jg)],
                      writes=[("h", q, ff)])

                if gidx + 2 < 4 * NF:
                    emit_w13_load(gidx + 2)
                if gidx in (3, 7, 11, 15):
                    emit_w2_load((gidx - 3) // 4)
                for (k_, jseg) in seg_sched.get(gidx, ()):
                    C1_SEGS[jseg](k_)
                    if jseg == 0 and k_ + 2 < 8:
                        emit_oin2_load(k_ + 2)

            def emit_d_group(tt):
                q = tt // 4
                jd = tt % 2
                j = tt % 2

                def f_down(t, tt=tt, q=q, jd=jd):
                    r = None
                    for ff in range(NF):
                        lhsT = h_q[q % 2][:, ff * 512 + (tt % 4) * 128:
                                          ff * 512 + (tt % 4 + 1) * 128]
                        t.matmul(out=psd[jd][:, 0:512], lhsT=lhsT,
                                 rhs=w2b[:, ff * D: ff * D + 512],
                                 start=(ff == 0), stop=(ff == NF - 1))
                        r = t.matmul(out=psd[jd][:, 512:1024], lhsT=lhsT,
                                     rhs=w2b[:, ff * D + 512: (ff + 1) * D],
                                     start=(ff == 0), stop=(ff == NF - 1))
                    return r
                em.op("pe", f_down,
                      reads=[("h", q, ff) for ff in range(NF)]
                      + [("w2b", p) for p in range(4)],
                      writes=[("psd", jd)])

                em.op("dve", lambda v, jd=jd, j=j: v.bn_stats(out=bn3[j][:, 0:6], in_=psd[jd][:, 0:512]),
                      reads=[("psd", jd)], writes=[("bn3", j)], inc=False)
                em.op("dve", lambda v, jd=jd, j=j: v.bn_stats(out=bn3[j][:, 6:12], in_=psd[jd][:, 512:1024]),
                      reads=[("psd", jd)], writes=[("bn3", j)])
                em.op("dve", lambda v, j=j: v.bn_aggr(out=mv3[j][:, :], in_=bn3[j][:, :]),
                      reads=[("bn3", j)], writes=[("mv3", j)])
                em.op("act", lambda s, j=j: s.activation(out=sd3[j][:, :], in_=mv3[j][:, 1:2],
                                                         func=AF.Sqrt, bias=epsb[:, :]),
                      reads=[("mv3", j), "g:epsb"], writes=[("sd3", j)])
                em.op("dve", lambda v, j=j: v.reciprocal(out=rs3[j][:, :], in_=sd3[j][:, :]),
                      reads=[("sd3", j)], writes=[("rs3", j)])
                em.op("dve", lambda v, j=j: v.tensor_scalar(
                    out=nm3[j][:, :], in0=mv3[j][:, 0:1], scalar1=rs3[j][:, :],
                    scalar2=-1.0, op0=ALU.mult, op1=ALU.mult),
                    reads=[("mv3", j), ("rs3", j)], writes=[("nm3", j)])
                em.op("act", lambda s, jd=jd, j=j: s.activation(
                    out=otp[j][:, :], in_=psd[jd][:, :], func=AF.Identity,
                    bias=nm3[j][:, :], scale=rs3[j][:, :]),
                    reads=[("psd", jd), ("nm3", j), ("rs3", j)], writes=[("otp", j)])
                em.op("dve", lambda v, j=j: v.tensor_add(
                    out=otp[j][:, :], in0=otp[j][:, :], in1=xnw[j][:, :]),
                    reads=[("otp", j), ("xnw", j)], writes=[("otp", j)])
                em.op("sync", lambda s, tt=tt, j=j: s.dma_start(
                    out=out_h[tt * 128:(tt + 1) * 128, :], in_=otp[j][:, :]),
                    reads=[("otp", j)], writes=[("d:out", tt)], sem=f"so{j}", dma=True)
                if tt + 2 < NT:
                    emit_xnw_load(tt + 2)

            emit_xnw_load(0)
            emit_xnw_load(1)

            gidx = 0
            for q in (0, 1):
                for ff in range(NF):
                    emit_g_group(q, ff, gidx)
                    gidx += 1
            for tt in range(0, 4):
                emit_d_group(tt)
            for q in (2,):
                for ff in range(NF):
                    emit_g_group(q, ff, gidx)
                    gidx += 1
            for tt in range(4, 8):
                emit_d_group(tt)
            for q in (3,):
                for ff in range(NF):
                    emit_g_group(q, ff, gidx)
                    gidx += 1
            for tt in range(8, 16):
                emit_d_group(tt)

        waited = {e: {} for e in ENGINES}
        attr_of = {"pe": "tensor", "act": "scalar", "dve": "vector",
                   "gp": "gpsimd", "sync": "sync"}

        def emit_block(prog):
            with nc.Block() as block:
                for eng_name in ENGINES:
                    ops = prog[eng_name]
                    if not ops:
                        continue

                    def section(eng, ops=ops, eng_name=eng_name):
                        w = waited[eng_name]
                        for waits, fn, mile, amt in ops:
                            for s_name, v in sorted(waits.items()):
                                if w.get(s_name, 0) < v:
                                    eng.wait_ge(sems[s_name], v)
                                    w[s_name] = v
                            inst = fn(eng)
                            if mile is not None:
                                inst.then_inc(sems[mile[0]], amt)

                    getattr(block, attr_of[eng_name])(section)

        for prog in em.blocks:
            emit_block(prog)

        f0, f1 = em.cnt["so0"], em.cnt["so1"]
        with nc.Block() as block:
            @block.sync
            def _(sync):
                sync.wait_ge(sems["so0"], f0)
                sync.wait_ge(sems["so1"], f1)

    return nc


_NC_CACHE = None


def kernel(L0_attn_inputs, L0_attn_Wq, L0_attn_bq, L0_ffn_w1, L0_ffn_w3, L0_ffn_w2):
    global _NC_CACHE
    x = np.ascontiguousarray(np.asarray(L0_attn_inputs, dtype=np.float32).reshape(S, D))
    wq = np.ascontiguousarray(np.asarray(L0_attn_Wq, dtype=np.float32))
    bq = np.ascontiguousarray(np.asarray(L0_attn_bq, dtype=np.float32).reshape(ND, 128).T)
    w1 = np.asarray(L0_ffn_w1, dtype=np.float32)
    w3 = np.asarray(L0_ffn_w3, dtype=np.float32)
    w2 = np.asarray(L0_ffn_w2, dtype=np.float32)
    ident = np.eye(128, dtype=ml_dtypes.bfloat16)

    if _NC_CACHE is None:
        _NC_CACHE = build_program()
    nc = _NC_CACHE

    in_maps = []
    for c in range(NCORES):
        in_maps.append({
            "x": np.ascontiguousarray(x[c * T:(c + 1) * T]),
            "wq": wq,
            "bq": bq,
            "w1": np.ascontiguousarray(w1[c]),
            "w3": np.ascontiguousarray(w3[c]),
            "w2": np.ascontiguousarray(w2[c]),
            "ident": ident,
        })
    res = run_bass_kernel_spmd(nc, in_maps, core_ids=list(range(NCORES)))
    out = np.concatenate([np.asarray(res.results[c]["out"]) for c in range(NCORES)], axis=0)
    return out.reshape(1, S, D).astype(np.float32)


# version 39
# speedup vs baseline: 1.0377x; 1.0080x over previous
import numpy as np
import ml_dtypes
from contextlib import ExitStack

import concourse.bass as bass
import concourse.mybir as mybir
from concourse.bass_utils import run_bass_kernel_spmd

F32 = mybir.dt.float32
BF16 = mybir.dt.bfloat16
AF = mybir.ActivationFunctionType
ALU = mybir.AluOpType

NCORES = 8
T = 2048
D = 1024
S = 16384
FF = 2816
NT = T // 128
ND = D // 128
NF = FF // 128
NG = 32
EPS = 1e-5
SM_SCALE = 1.0 / 32.0
SM_SHIFT = -32.0

ENGINES = ("pe", "act", "dve", "gp", "sync")
SEM_OF = {"pe": "s_pe", "act": "s_act", "dve": "s_dve", "gp": "s_gp", "sync": "s_sy"}
DMA_SEMS = ("s_ld", "s_gd", "s_st")
SEM_NAMES = (["s_pe", "s_act", "s_dve", "s_gp", "s_sy", "s_cc",
              "sid", "swq", "sbq", "sqf0", "sqf1", "sstf",
              "sqt0", "sqt1", "sqt2", "sqt3", "sod", "soi0", "soi1",
              "sgk0", "sgk1", "sgv0", "sgv1", "sgv2", "sod2",
              "swc1", "swc3", "swc2",
              "ssp0", "ssp1", "ssp2", "ssp3", "so0", "so1"]
             + [f"sxt{j}" for j in range(8)] + [f"sxi{j}" for j in range(2)]
             + [f"sxw{j}" for j in range(2)] + [f"skb{j}" for j in range(2)]
             + [f"svb{j}" for j in range(3)] + [f"sw1{j}" for j in range(2)]
             + [f"sw3{j}" for j in range(2)] + [f"sw2{j}" for j in range(4)])


class Emitter:

    def __init__(self):
        self.cnt = {s: 0 for s in SEM_NAMES}
        self.blocks = []
        self.prog = None
        self.last_w = {}
        self.readers = {}
        self.pending_reads = {e: [] for e in ENGINES}
        self.pending_writes = {e: [] for e in ENGINES}
        self.barrier = {}

    def new_block(self):
        self.prog = {e: [] for e in ENGINES}
        self.blocks.append(self.prog)

    def set_barrier(self, exclude=()):
        self.barrier = {s: v for s, v in self.cnt.items() if v and s not in exclude}

    def op(self, eng, fn, reads=(), writes=(), sem=None, dma=False, inc=True):
        reads = list(reads)
        writes = list(writes)
        waits = {}

        def merge(m):
            if m is not None:
                waits[m[0]] = max(waits.get(m[0], 0), m[1])

        for b in reads:
            merge(self.last_w.get(b))
        for b in writes:
            for m in self.readers.get(b, ()):
                merge(m)
            merge(self.last_w.get(b))

        def scoped(b):
            k = b[0] if isinstance(b, tuple) else b
            return not (k.startswith("g:") or k.startswith("d:"))

        if any(scoped(b) for b in reads + writes):
            for s, v in self.barrier.items():
                waits[s] = max(waits.get(s, 0), v)
        mile = None
        amt = 16 if dma else 1
        if inc:
            s = sem or SEM_OF[eng]
            self.cnt[s] += amt
            mile = (s, self.cnt[s])
        self.prog[eng].append((waits, fn, mile, amt))
        if mile is not None:
            for b in self.pending_reads[eng]:
                self.readers.setdefault(b, []).append(mile)
            self.pending_reads[eng] = []
            for b in self.pending_writes[eng]:
                self.last_w[b] = mile
                self.readers[b] = []
            self.pending_writes[eng] = []
            for b in reads:
                self.readers.setdefault(b, []).append(mile)
            for b in writes:
                self.last_w[b] = mile
                self.readers[b] = []
        else:
            self.pending_reads[eng].extend(reads)
            self.pending_writes[eng].extend(writes)
        return mile


def build_program():
    nc = bass.Bass("TRN2", target_bir_lowering=False)

    x_h = nc.dram_tensor("x", [T, D], F32, kind="ExternalInput")
    wq_h = nc.dram_tensor("wq", [D, D], F32, kind="ExternalInput")
    bq_h = nc.dram_tensor("bq", [128, ND], F32, kind="ExternalInput")
    w1_h = nc.dram_tensor("w1", [D, FF], F32, kind="ExternalInput")
    w3_h = nc.dram_tensor("w3", [D, FF], F32, kind="ExternalInput")
    w2_h = nc.dram_tensor("w2", [FF, D], F32, kind="ExternalInput")
    id_h = nc.dram_tensor("ident", [128, 128], BF16, kind="ExternalInput")
    out_h = nc.dram_tensor("out", [T, D], F32, kind="ExternalOutput")

    qq_k = [nc.dram_tensor(f"qq_k{k}", [128, 4096], BF16) for k in range(4)]
    qq_v = [nc.dram_tensor(f"qq_v{k}", [128, 4096], BF16) for k in range(4)]
    g_k = [nc.dram_tensor(f"g_k{k}", [NCORES * 128, 4096], BF16,
                          addr_space="Shared") for k in range(4)]
    g_v = [nc.dram_tensor(f"g_v{k}", [NCORES * 128, 4096], BF16,
                          addr_space="Shared") for k in range(4)]
    xsp = nc.dram_tensor("xsp", [T, D], F32)
    odr = nc.dram_tensor("odr", [16 * 128, D], F32)
    w1b = nc.dram_tensor("w1b", [128, ND * FF], BF16)
    w3b = nc.dram_tensor("w3b", [128, ND * FF], BF16)
    w2c = nc.dram_tensor("w2c", [128, NF * D], BF16)

    em = Emitter()

    with ExitStack() as G:
        eg = G.enter_context
        sems = {s: eg(nc.semaphore(s)) for s in SEM_NAMES}

        ident = eg(nc.sbuf_tensor("ident_sb", [128, 128], BF16))
        ones = eg(nc.sbuf_tensor("ones", [128, 1], BF16))
        bqs = eg(nc.sbuf_tensor("bqs", [128, ND], F32))
        lsb = eg(nc.sbuf_tensor("lsb", [128, 16], F32))
        lrcp = eg(nc.sbuf_tensor("lrcp", [128, 16], F32))
        epsb = eg(nc.sbuf_tensor("epsb", [128, 1], F32))
        shb = eg(nc.sbuf_tensor("shb", [128, 1], F32))
        ln2T = eg(nc.sbuf_tensor("ln2T", [128, ND * T], BF16))
        w1s = [eg(nc.sbuf_tensor(f"w1s{j}", [128, ND * 128], BF16)) for j in range(2)]
        w3s = [eg(nc.sbuf_tensor(f"w3s{j}", [128, ND * 128], BF16)) for j in range(2)]

        def emit_w13_cast():
            for (dst, src, key, sem) in ((w1b, w1_h, "d:w1b", "swc1"),
                                         (w3b, w3_h, "d:w3b", "swc3")):
                def f(g, dst=dst, src=src):
                    i3 = src.ap().rearrange("(hh p) F -> p hh F", p=128)
                    o3 = dst.ap().rearrange("p (hh F) -> p hh F", hh=ND)
                    return g.dma_start(out=o3, in_=i3)
                em.op("gp", f, writes=[key], sem=sem, dma=True)

            def f2(g):
                i3 = w2_h.ap().rearrange("(ff p) d -> p ff d", p=128)
                o3 = w2c.ap().rearrange("p (ff d) -> p ff d", ff=NF)
                return g.dma_start(out=o3, in_=i3)
            em.op("gp", f2, writes=["d:w2c"], sem="swc2", dma=True)

        def emit_w13_load(gidx):
            ff = gidx % NF
            j = gidx % 2
            for (dst, src, key, sem) in ((w1s, w1b, "d:w1b", "sw1"),
                                         (w3s, w3b, "d:w3b", "sw3")):
                def f(g, dst=dst, src=src, ff=ff, j=j):
                    i3 = src.ap().rearrange("p (hh F) -> p hh F", hh=ND)[
                        :, :, ff * 128:(ff + 1) * 128]
                    o3 = dst[j][:, :].rearrange("p (hh f) -> p hh f", hh=ND)
                    return g.dma_start(out=o3, in_=i3)
                em.op("sync", f, reads=[key],
                      writes=[("g:w1s" if sem == "sw1" else "g:w3s", j)],
                      sem=sem + str(j), dma=True)

        P01 = ExitStack()
        qfm = P01.enter_context(nc.sbuf_tensor("qfm", [128, ND * T], BF16))
        kb = [P01.enter_context(nc.sbuf_tensor(f"kb{j}", [128, ND * 512], BF16))
              for j in range(2)]
        vb = [P01.enter_context(nc.sbuf_tensor(f"vb{j}", [128, 4 * D], BF16))
              for j in range(3)]

        em.new_block()
        with ExitStack() as P0:
            e0 = P0.enter_context
            xt = [e0(nc.sbuf_tensor(f"xt{j}", [128, D], F32)) for j in range(6)]
            bns = [e0(nc.sbuf_tensor(f"bns{j}", [128, 12], F32)) for j in range(4)]
            mv = [e0(nc.sbuf_tensor(f"mv{j}", [128, 2], F32)) for j in range(4)]
            sd = [e0(nc.sbuf_tensor(f"sd{j}", [128, 1], F32)) for j in range(4)]
            rs = [e0(nc.sbuf_tensor(f"rs{j}", [128, 1], F32)) for j in range(4)]
            ln1 = [e0(nc.sbuf_tensor(f"ln1_{j}", [128, D], BF16)) for j in range(4)]
            ln1T = e0(nc.sbuf_tensor("ln1T", [128, ND * T], BF16))
            wqb = e0(nc.sbuf_tensor("wqb", [128, ND * D], BF16))
            qtm = [e0(nc.sbuf_tensor(f"qtm{j}", [128, D], BF16)) for j in range(4)]
            ptr = [e0(nc.psum_tensor(f"ptr{j}", [128, D], BF16)) for j in range(2)]
            psq = [e0(nc.psum_tensor(f"psq{j}", [128, 512], F32)) for j in range(2)]

            em.op("sync", lambda s: s.dma_start(out=ident[:, :], in_=id_h[:, :]),
                  writes=["g:ident"], sem="sid", dma=True)
            em.op("dve", lambda v: v.memset(ones[:, :], 1.0), writes=["g:ones"])
            em.op("dve", lambda v: v.memset(epsb[:, :], EPS), writes=["g:epsb"])
            em.op("dve", lambda v: v.memset(shb[:, :], SM_SHIFT), writes=["g:shb"])
            em.op("gp", lambda g: g.dma_start(out=bqs[:, :], in_=bq_h[:, :]),
                  writes=["g:bq"], sem="sbq", dma=True)

            def f_wq(g):
                i3 = wq_h.ap().rearrange("(hh p) d -> p hh d", p=128)
                o3 = wqb[:, :].rearrange("p (hh d) -> p hh d", hh=ND)
                return g.dma_start(out=o3, in_=i3)
            em.op("gp", f_wq, writes=["wqb"], sem="swq", dma=True)

            def emit_agk(tq, extra=()):
                def f_ag(g, tq=tq):
                    return g.collective_compute(
                        "AllGather", ALU.bypass,
                        replica_groups=[list(range(NCORES))],
                        ins=[qq_k[tq].ap().opt()], outs=[g_k[tq].ap().opt()])
                em.op("gp", f_ag, reads=[("d:qfml", tq)] + list(extra),
                      writes=[("d:gqk", tq)], sem="s_cc")

            def emit_agv(tq, extra=()):
                def f_ag(g, tq=tq):
                    return g.collective_compute(
                        "AllGather", ALU.bypass,
                        replica_groups=[list(range(NCORES))],
                        ins=[qq_v[tq].ap().opt()], outs=[g_v[tq].ap().opt()])
                em.op("gp", f_ag,
                      reads=[("d:qtml", i_) for i_ in range(tq * 4, tq * 4 + 4)]
                      + list(extra),
                      writes=[("d:gqv", tq)], sem="s_cc")

            def emit_x_load(i):
                j8 = i % 6

                def f_load(s, i=i, j8=j8):
                    return s.dma_start(out=xt[j8][:, :],
                                       in_=x_h[i * 128:(i + 1) * 128, :])
                em.op("sync" if i % 2 == 0 else "act", f_load,
                      writes=[("xt", j8)], sem=f"sxt{j8}", dma=True)

            for i in range(6):
                emit_x_load(i)
            for tq in range(4):
                ws = [tq * 4 + k for k in range(4)]
                for i in ws:
                    j = i % 4
                    j8 = i % 6
                    em.op("dve", lambda v, j=j, j8=j8: v.bn_stats(
                        out=bns[j][:, 0:6], in_=xt[j8][:, 0:512]),
                        reads=[("xt", j8)], writes=[("bns", j)], inc=False)
                    em.op("dve", lambda v, j=j, j8=j8: v.bn_stats(
                        out=bns[j][:, 6:12], in_=xt[j8][:, 512:1024]),
                        reads=[("xt", j8)], writes=[("bns", j)])
                for i in ws:
                    j = i % 4
                    em.op("dve", lambda v, j=j: v.bn_aggr(out=mv[j][:, :], in_=bns[j][:, :]),
                          reads=[("bns", j)], writes=[("mv", j)])
                for i in ws:
                    j = i % 4
                    em.op("act", lambda s, j=j: s.activation(out=sd[j][:, :], in_=mv[j][:, 1:2],
                                                             func=AF.Sqrt, bias=epsb[:, :]),
                          reads=[("mv", j), "g:epsb"], writes=[("sd", j)])
                for i in ws:
                    j = i % 4
                    em.op("dve", lambda v, j=j: v.reciprocal(out=rs[j][:, :], in_=sd[j][:, :]),
                          reads=[("sd", j)], writes=[("rs", j)])
                for i in ws:
                    j = i % 4
                    j8 = i % 6
                    em.op("dve", lambda v, j=j, j8=j8: v.tensor_scalar(
                        out=ln1[j][:, :], in0=xt[j8][:, :], scalar1=mv[j][:, 0:1],
                        scalar2=rs[j][:, :], op0=ALU.subtract, op1=ALU.mult),
                        reads=[("xt", j8), ("mv", j), ("rs", j)], writes=[("ln1", j)])
                for i in ws:
                    j = i % 4
                    jp = i % 2

                    def f_tr(t, j=j, jp=jp):
                        r = None
                        for hh in range(ND):
                            r = t.transpose(out=ptr[jp][:, hh * 128:(hh + 1) * 128],
                                            in_=ln1[j][:, hh * 128:(hh + 1) * 128],
                                            identity=ident[:, :])
                        return r
                    em.op("pe", f_tr, reads=[("ln1", j), "g:ident"], writes=[("ptr", jp)])

                    def f_cp(v, i=i, jp=jp):
                        o3 = ln1T[:, :].rearrange("p (hh t) -> p hh t", hh=ND)[:, :, i * 128:(i + 1) * 128]
                        i3 = ptr[jp][:, :].rearrange("p (hh t) -> p hh t", hh=ND)
                        return v.tensor_copy(out=o3, in_=i3)
                    em.op("dve", f_cp, reads=[("ptr", jp)],
                          writes=[("ln1T", hh, i) for hh in range(ND)])

                for i in range(4 * tq + 6, 4 * tq + 10):
                    if i < 16:
                        emit_x_load(i)

                for dd in range(ND):
                    jq = (tq * ND + dd) % 2

                    def f_mm(t, dd=dd, tq=tq, jq=jq):
                        r = None
                        for hh in range(ND):
                            r = t.matmul(out=psq[jq][:, :],
                                         lhsT=wqb[:, hh * D + dd * 128: hh * D + (dd + 1) * 128],
                                         rhs=ln1T[:, hh * T + tq * 512: hh * T + (tq + 1) * 512],
                                         start=(hh == 0), stop=(hh == ND - 1))
                        return r
                    em.op("pe", f_mm,
                          reads=[("ln1T", hh, i) for hh in range(ND)
                                 for i in range(tq * 4, tq * 4 + 4)] + ["wqb"],
                          writes=[("psq", jq)])

                    def f_bias(s, dd=dd, tq=tq, jq=jq):
                        return s.activation(
                            out=qfm[:, dd * T + tq * 512: dd * T + (tq + 1) * 512],
                            in_=psq[jq][:, :], func=AF.Identity,
                            bias=bqs[:, dd:dd + 1], scale=1.0)
                    em.op("act", f_bias, reads=[("psq", jq), "g:bq"],
                          writes=[("qfm", dd, tq)])

                for i in range(tq * 4, tq * 4 + 4):
                    j = i % 2
                    js = i % 4

                    def f_trq(t, i=i, j=j):
                        r = None
                        for dd in range(ND):
                            r = t.transpose(out=ptr[j][:, dd * 128:(dd + 1) * 128],
                                            in_=qfm[:, dd * T + i * 128: dd * T + (i + 1) * 128],
                                            identity=ident[:, :])
                        return r
                    em.op("pe", f_trq,
                          reads=[("qfm", dd, i // 4) for dd in range(ND)] + ["g:ident"],
                          writes=[("ptr", j)])
                    em.op("dve", lambda v, j=j, js=js: v.tensor_copy(
                        out=qtm[js][:, :], in_=ptr[j][:, :]),
                        reads=[("ptr", j)], writes=[("qtm", js)])

                    def f_st(g, i=i, js=js):
                        return g.dma_start(
                            out=qq_v[i // 4][:, (i % 4) * 1024:(i % 4 + 1) * 1024],
                            in_=qtm[js][:, :])
                    em.op("sync", f_st, reads=[("qtm", js)], writes=[("d:qtml", i)],
                          sem=f"sqt{js}", dma=True)

                for kt in range(4):
                    def f_stfm(g, tq=tq, kt=kt):
                        o3 = qq_k[tq].ap()[:, kt * 1024:(kt + 1) * 1024].rearrange(
                            "p (hh c) -> p hh c", hh=ND)
                        tok0 = tq * 512 + kt * 128
                        i3 = qfm[:, :].rearrange(
                            "p (dd t) -> p dd t", dd=ND)[:, :, tok0:tok0 + 128]
                        return g.dma_start(out=o3, in_=i3)
                    em.op("sync", f_stfm,
                          reads=[("qfm", dd, tq) for dd in range(ND)],
                          writes=[("d:qfml", tq)],
                          sem="sstf", dma=True)

                if tq == 0:
                    for jx in range(6):
                        em.last_w[("d:xall", jx)] = (f"sxt{jx}", em.cnt[f"sxt{jx}"])
                    emit_agk(0, [("d:xall", jx) for jx in range(6)])
                    emit_agv(0)
                else:
                    emit_agk(tq)
                    emit_agv(tq)

            for jw, rw in ((0, 0), (1, 1)):
                em.op("sync", lambda s, jw=jw, rw=rw: s.dma_start(
                    out=kb[jw][:, :],
                    in_=g_k[0].ap()[rw * 128:(rw + 1) * 128, :]),
                    reads=[("d:gqk", 0)], writes=[("kb", jw)], sem=f"sgk{jw}", dma=True)
            for jw, rw in ((0, 0), (1, 1), (2, 2)):
                em.op("sync", lambda s, jw=jw, rw=rw: s.dma_start(
                    out=vb[jw][:, :],
                    in_=g_v[0].ap()[rw * 128:(rw + 1) * 128, :]),
                    reads=[("d:gqv", 0)], writes=[("vb", jw)], sem=f"sgv{jw}", dma=True)

            em.set_barrier(exclude=("s_cc", "sstf", "sqt0", "sqt1", "sqt2",
                                    "sqt3", "sgk0", "sgk1",
                                    "sgv0", "sgv1", "sgv2"))

        em.new_block()
        with ExitStack() as P1:
            e1 = P1.enter_context
            pT = e1(nc.sbuf_tensor("pT", [128, 8 * 1024], BF16))
            osb = [e1(nc.sbuf_tensor(f"osb{q}", [128, D], F32)) for q in range(8)]
            oin = [e1(nc.sbuf_tensor(f"oin{j}", [128, D], F32)) for j in range(2)]
            at1 = [e1(nc.sbuf_tensor(f"at1_{j}", [128, D], F32)) for j in range(2)]
            xin = [e1(nc.sbuf_tensor(f"xin{j}", [128, D], F32)) for j in range(2)]
            xnu = [e1(nc.sbuf_tensor(f"xnu{j}", [128, D], F32)) for j in range(3)]
            ln2b = [e1(nc.sbuf_tensor(f"ln2b{j}", [128, D], BF16)) for j in range(2)]
            bn2 = [e1(nc.sbuf_tensor(f"bn2_{j}", [128, 12], F32)) for j in range(8)]
            mv2 = [e1(nc.sbuf_tensor(f"mv2_{j}", [128, 2], F32)) for j in range(8)]
            sd2 = [e1(nc.sbuf_tensor(f"sd2_{j}", [128, 1], F32)) for j in range(8)]
            rs2 = [e1(nc.sbuf_tensor(f"rs2_{j}", [128, 1], F32)) for j in range(8)]
            nm2 = [e1(nc.sbuf_tensor(f"nm2_{j}", [128, 1], F32)) for j in range(8)]
            pss = [e1(nc.psum_tensor(f"pss{j}", [128, 512], F32)) for j in range(2)]
            pso = [e1(nc.psum_tensor(f"pso{j}", [128, D], F32)) for j in range(2)]
            psl = e1(nc.psum_tensor("psl", [128, 8], F32))
            ptr2 = e1(nc.psum_tensor("ptr2", [128, D], BF16))

            QSEQ = [4 * r + qr for qr in range(4) for r in range(NCORES)]

            def emit_kb_load(qpos):
                m = QSEQ[qpos % 32]
                jb = qpos % 2
                r = m // 4
                qr = m % 4

                def f(s, r=r, qr=qr, jb=jb):
                    return s.dma_start(
                        out=kb[jb][:, :],
                        in_=g_k[qr].ap()[r * 128:(r + 1) * 128, :])
                em.op("sync", f, reads=[("d:gqk", qr)], writes=[("kb", jb)],
                      sem=f"skb{jb}", dma=True)

            def emit_vb_load(qpos):
                m = QSEQ[qpos % 32]
                jb = qpos % 3
                r = m // 4
                qr = m % 4

                def f(s, r=r, qr=qr, jb=jb):
                    return s.dma_start(
                        out=vb[jb][:, :],
                        in_=g_v[qr].ap()[r * 128:(r + 1) * 128, :])
                em.op("sync", f, reads=[("d:gqv", qr)], writes=[("vb", jb)],
                      sem=f"svb{jb}", dma=True)

            def emit_xin_load(qq):
                j = qq % 2

                def f(g, qq=qq, j=j):
                    return g.dma_start(out=xin[j][:, :], in_=x_h[qq * 128:(qq + 1) * 128, :])
                em.op("gp", f, writes=[("xin", j)], sem=f"sxi{j}", dma=True)

            def emit_oin_load(k):
                em.op("sync", lambda s, k=k: s.dma_start(
                    out=oin[k % 2][:, :], in_=odr[k * 128:(k + 1) * 128, :]),
                    reads=[("d:osp", q) for q in range(8)], writes=[("oin", k % 2)],
                    sem=f"soi{k % 2}", dma=True)

            def emit_epi_l(hf, qs):
                qq = hf * 8 + qs
                em.op("dve", lambda v, hf=hf, qs=qs, qq=qq: v.tensor_add(
                    out=lsb[:, qq:qq + 1], in0=lsb[:, qq:qq + 1], in1=psl[:, qs:qs + 1]),
                    reads=[("psl", qs), ("g:lsb", hf)], writes=[("g:lsbq", qq)])
                em.op("dve", lambda v, qq=qq: v.reciprocal(
                    out=lrcp[:, qq:qq + 1], in_=lsb[:, qq:qq + 1]),
                    reads=[("g:lsbq", qq)], writes=[("g:lrcpq", qq)])

            def emit_c0_s1(qs):
                em.op("act", lambda s, qs=qs: s.activation(
                    out=at1[qs % 2][:, :], in_=oin[qs % 2][:, :], func=AF.Identity,
                    scale=lrcp[:, qs:qs + 1]),
                    reads=[("oin", qs % 2), ("g:lrcpq", qs)], writes=[("at1", qs % 2)])
                em.op("dve", lambda v, qs=qs: v.bn_stats(
                    out=bn2[qs][:, 0:6], in_=at1[qs % 2][:, 0:512]),
                    reads=[("at1", qs % 2)], writes=[("bn2", qs)], inc=False)
                em.op("dve", lambda v, qs=qs: v.bn_stats(
                    out=bn2[qs][:, 6:12], in_=at1[qs % 2][:, 512:1024]),
                    reads=[("at1", qs % 2)], writes=[("bn2", qs)])
                em.op("dve", lambda v, qs=qs: v.bn_aggr(out=mv2[qs][:, :], in_=bn2[qs][:, :]),
                      reads=[("bn2", qs)], writes=[("mv2", qs)])

            def emit_c0_s2(qs):
                em.op("act", lambda s, qs=qs: s.activation(
                    out=sd2[qs][:, :], in_=mv2[qs][:, 1:2], func=AF.Sqrt, bias=epsb[:, :]),
                    reads=[("mv2", qs), "g:epsb"], writes=[("sd2", qs)])
                em.op("dve", lambda v, qs=qs: v.reciprocal(out=rs2[qs][:, :], in_=sd2[qs][:, :]),
                      reads=[("sd2", qs)], writes=[("rs2", qs)])
                em.op("dve", lambda v, qs=qs: v.tensor_scalar(
                    out=nm2[qs][:, :], in0=mv2[qs][:, 0:1], scalar1=rs2[qs][:, :],
                    scalar2=-1.0, op0=ALU.mult, op1=ALU.mult),
                    reads=[("mv2", qs), ("rs2", qs)], writes=[("nm2", qs)])

            def emit_c0_s3(qs):
                em.op("act", lambda s, qs=qs: s.activation(
                    out=xnu[qs % 3][:, :], in_=at1[qs % 2][:, :], func=AF.Identity,
                    bias=nm2[qs][:, :], scale=rs2[qs][:, :]),
                    reads=[("at1", qs % 2), ("nm2", qs), ("rs2", qs)],
                    writes=[("xnu", qs % 3)])
                em.op("gp", lambda g_, qs=qs: g_.tensor_add(
                    out=xnu[qs % 3][:, :], in0=xnu[qs % 3][:, :], in1=xin[qs % 2][:, :]),
                    reads=[("xnu", qs % 3), ("xin", qs % 2)], writes=[("xnu", qs % 3)])
                em.op("gp", lambda g_, qs=qs: g_.dma_start(
                    out=xsp[qs * 128:(qs + 1) * 128, :], in_=xnu[qs % 3][:, :]),
                    reads=[("xnu", qs % 3)], writes=[("d:xsp", qs)],
                    sem=f"ssp{qs % 4}", dma=True)
                if qs + 2 < 8:
                    emit_xin_load(qs + 2)

            def emit_c0_s4(qs):
                em.op("dve", lambda v, qs=qs: v.bn_stats(
                    out=bn2[qs][:, 0:6], in_=xnu[qs % 3][:, 0:512]),
                    reads=[("xnu", qs % 3)], writes=[("bn2", qs)], inc=False)
                em.op("dve", lambda v, qs=qs: v.bn_stats(
                    out=bn2[qs][:, 6:12], in_=xnu[qs % 3][:, 512:1024]),
                    reads=[("xnu", qs % 3)], writes=[("bn2", qs)])
                em.op("dve", lambda v, qs=qs: v.bn_aggr(out=mv2[qs][:, :], in_=bn2[qs][:, :]),
                      reads=[("bn2", qs)], writes=[("mv2", qs)])

            def emit_c0_s5(qs):
                em.op("act", lambda s, qs=qs: s.activation(
                    out=sd2[qs][:, :], in_=mv2[qs][:, 1:2], func=AF.Sqrt, bias=epsb[:, :]),
                    reads=[("mv2", qs), "g:epsb"], writes=[("sd2", qs)])
                em.op("dve", lambda v, qs=qs: v.reciprocal(out=rs2[qs][:, :], in_=sd2[qs][:, :]),
                      reads=[("sd2", qs)], writes=[("rs2", qs)])
                em.op("dve", lambda v, qs=qs: v.tensor_scalar(
                    out=ln2b[qs % 2][:, :], in0=xnu[qs % 3][:, :], scalar1=mv2[qs][:, 0:1],
                    scalar2=rs2[qs][:, :], op0=ALU.subtract, op1=ALU.mult),
                    reads=[("xnu", qs % 3), ("mv2", qs), ("rs2", qs)],
                    writes=[("ln2b", qs % 2)])

            def emit_epi_tail(hf, qs):
                qq = hf * 8 + qs
                jl = qq % 2

                def f_tr2(t, jl=jl):
                    r = None
                    for hh in range(ND):
                        r = t.transpose(out=ptr2[:, hh * 128:(hh + 1) * 128],
                                        in_=ln2b[jl][:, hh * 128:(hh + 1) * 128],
                                        identity=ident[:, :])
                    return r
                em.op("pe", f_tr2, reads=[("ln2b", jl), "g:ident"], writes=["ptr2"])

                def f_cp2(v, qq=qq):
                    o3 = ln2T[:, :].rearrange("p (hh t) -> p hh t", hh=ND)[:, :, qq * 128:(qq + 1) * 128]
                    i3 = ptr2[:, :].rearrange("p (hh t) -> p hh t", hh=ND)
                    return v.tensor_copy(out=o3, in_=i3)
                em.op("dve", f_cp2, reads=["ptr2"],
                      writes=[("g:ln2T", hh, qq) for hh in range(ND)])

            emit_xin_load(0)
            emit_xin_load(1)
            emit_w13_cast()

            for gg in range(NG):
                hf = gg // 16
                pos = gg % 16

                qp0 = hf * 32 + 2 * pos
                for t in range(8):
                    kt = t % 4
                    jb = (qp0 + t // 4) % 2
                    for qc in range(2):
                        jss = (t * 2 + qc) % 2

                        def f_sc(tt, jb=jb, kt=kt, hf=hf, qc=qc, jss=jss):
                            r = None
                            for hh in range(ND):
                                r = tt.matmul(
                                    out=pss[jss][:, :],
                                    lhsT=kb[jb][:, kt * 1024 + hh * 128: kt * 1024 + (hh + 1) * 128],
                                    rhs=qfm[:, hh * T + hf * 1024 + qc * 512:
                                            hh * T + hf * 1024 + (qc + 1) * 512],
                                    start=(hh == 0), stop=(hh == ND - 1))
                            return r
                        em.op("pe", f_sc,
                              reads=[("kb", jb)]
                              + [("qfm", d_, 2 * hf + qc) for d_ in range(ND)],
                              writes=[("pss", jss)])

                        def f_exp(s, t=t, qc=qc, jss=jss):
                            return s.activation(
                                out=pT[:, t * 1024 + qc * 512: t * 1024 + (qc + 1) * 512],
                                in_=pss[jss][:, :], func=AF.Exp,
                                bias=shb[:, :], scale=SM_SCALE)
                        em.op("act", f_exp, reads=[("pss", jss), "g:shb"], writes=[("pT", t, qc)])

                for mm_ in (qp0 + 2, qp0 + 3):
                    if mm_ < 64:
                        emit_kb_load(mm_)

                jb4a = qp0 % 3
                jb4b = (qp0 + 1) % 3
                for qs in range(8):
                    jo = (gg * 8 + qs) % 2

                    def f_pv(tt, jb4a=jb4a, jb4b=jb4b, qs=qs, jo=jo):
                        r = None
                        for t in range(8):
                            kt = t % 4
                            jb4 = jb4a if t < 4 else jb4b
                            lhsT = pT[:, t * 1024 + qs * 128: t * 1024 + (qs + 1) * 128]
                            tt.matmul(out=pso[jo][:, 0:512], lhsT=lhsT,
                                      rhs=vb[jb4][:, kt * D: kt * D + 512],
                                      start=(t == 0), stop=(t == 7))
                            tt.matmul(out=pso[jo][:, 512:1024], lhsT=lhsT,
                                      rhs=vb[jb4][:, kt * D + 512: (kt + 1) * D],
                                      start=(t == 0), stop=(t == 7))
                            r = tt.matmul(out=psl[:, qs:qs + 1], lhsT=lhsT,
                                          rhs=ones[:, 0:1],
                                          start=(t == 0), stop=(t == 7))
                        return r
                    em.op("pe", f_pv,
                          reads=[("pT", t, qs // 4) for t in range(8)]
                          + [("vb", jb4a), ("vb", jb4b), "g:ones"],
                          writes=[("pso", jo), ("psl", qs)])

                    if pos == 0:
                        em.op("dve", lambda v, qs=qs, jo=jo: v.tensor_copy(
                            out=osb[qs][:, :], in_=pso[jo][:, :]),
                            reads=[("pso", jo)]
                            + ([("d:osp", q) for q in range(8)] if hf == 1 else []),
                            writes=[("osb", qs)])
                    else:
                        em.op("dve", lambda v, qs=qs, jo=jo: v.tensor_add(
                            out=osb[qs][:, :], in0=osb[qs][:, :], in1=pso[jo][:, :]),
                            reads=[("pso", jo), ("osb", qs)], writes=[("osb", qs)])
                    if pos == 15:
                        emit_epi_l(hf, qs)
                        if hf == 0:
                            em.op("gp", lambda g_, qs=qs: g_.dma_start(
                                out=odr[qs * 128:(qs + 1) * 128, :], in_=osb[qs][:, :]),
                                reads=[("osb", qs)], writes=[("d:osp", qs)],
                                sem="sod", dma=True)
                        else:
                            em.op("sync", lambda s, qs=qs: s.dma_start(
                                out=odr[(8 + qs) * 128:(9 + qs) * 128, :],
                                in_=osb[qs][:, :]),
                                reads=[("osb", qs)], writes=[("d:osp", 8 + qs)],
                                sem="sod2", dma=True)

                for mm_ in (qp0 + 3, qp0 + 4):
                    if 2 < mm_ < 64:
                        emit_vb_load(mm_)

                if pos == 15:
                    pass
                elif pos == 0:
                    em.op("dve", lambda v, hf=hf: v.tensor_copy(
                        out=lsb[:, hf * 8:(hf + 1) * 8], in_=psl[:, :]),
                        reads=[("psl", q) for q in range(8)], writes=[("g:lsb", hf)])
                else:
                    em.op("dve", lambda v, hf=hf: v.tensor_add(
                        out=lsb[:, hf * 8:(hf + 1) * 8],
                        in0=lsb[:, hf * 8:(hf + 1) * 8], in1=psl[:, :]),
                        reads=[("psl", q) for q in range(8)] + [("g:lsb", hf)],
                        writes=[("g:lsb", hf)])

                if hf == 0 and pos == 15:
                    emit_oin_load(0)
                    emit_oin_load(1)
                if hf == 1:
                    if 0 <= pos - 5 < 8:
                        emit_epi_tail(0, pos - 5)
                    if 0 <= pos - 4 < 8:
                        emit_c0_s5(pos - 4)
                    if 0 <= pos - 3 < 8:
                        emit_c0_s4(pos - 3)
                    if 0 <= pos - 2 < 8:
                        emit_c0_s3(pos - 2)
                    if 0 <= pos - 1 < 8:
                        emit_c0_s2(pos - 1)
                    if pos < 8:
                        emit_c0_s1(pos)
                        if pos + 2 < 8:
                            emit_oin_load(pos + 2)

                if gg == 29:
                    emit_w13_load(0)
                    emit_w13_load(1)

            em.set_barrier()

        P01.close()

        em.new_block()
        with ExitStack() as P2:
            e2 = P2.enter_context
            h_q = [e2(nc.sbuf_tensor(f"h_q{j}", [128, NF * 512], BF16)) for j in range(2)]
            w2b = e2(nc.sbuf_tensor("w2b", [128, NF * D], BF16))
            gsl = [e2(nc.sbuf_tensor(f"gsl{j}", [128, 512], F32)) for j in range(2)]
            xnw = [e2(nc.sbuf_tensor(f"xnw{j}", [128, D], F32)) for j in range(2)]
            otp = [e2(nc.sbuf_tensor(f"otp{j}", [128, D], F32)) for j in range(2)]
            bn3 = [e2(nc.sbuf_tensor(f"bn3_{j}", [128, 12], F32)) for j in range(2)]
            mv3 = [e2(nc.sbuf_tensor(f"mv3_{j}", [128, 2], F32)) for j in range(2)]
            sd3 = [e2(nc.sbuf_tensor(f"sd3_{j}", [128, 1], F32)) for j in range(2)]
            rs3 = [e2(nc.sbuf_tensor(f"rs3_{j}", [128, 1], F32)) for j in range(2)]
            nm3 = [e2(nc.sbuf_tensor(f"nm3_{j}", [128, 1], F32)) for j in range(2)]
            oin2 = [e2(nc.sbuf_tensor(f"oin2_{j}", [128, D], F32)) for j in range(2)]
            at2 = [e2(nc.sbuf_tensor(f"at2_{j}", [128, D], F32)) for j in range(4)]
            xnu2 = [e2(nc.sbuf_tensor(f"xnu2_{j}", [128, D], F32)) for j in range(4)]
            xin2 = [e2(nc.sbuf_tensor(f"xin2_{j}", [128, D], F32)) for j in range(2)]
            ln2c = [e2(nc.sbuf_tensor(f"ln2c{j}", [128, D], BF16)) for j in range(2)]
            bn4 = [e2(nc.sbuf_tensor(f"bn4_{k}", [128, 12], F32)) for k in range(8)]
            mv4 = [e2(nc.sbuf_tensor(f"mv4_{k}", [128, 2], F32)) for k in range(8)]
            sd4 = [e2(nc.sbuf_tensor(f"sd4_{k}", [128, 1], F32)) for k in range(8)]
            rs4 = [e2(nc.sbuf_tensor(f"rs4_{k}", [128, 1], F32)) for k in range(8)]
            nm4 = [e2(nc.sbuf_tensor(f"nm4_{k}", [128, 1], F32)) for k in range(8)]
            psg = e2(nc.psum_tensor("psg", [128, 512], F32))
            psu = [e2(nc.psum_tensor(f"psu{j}", [128, 512], F32)) for j in range(2)]
            psd = [e2(nc.psum_tensor(f"psd{j}", [128, D], F32)) for j in range(2)]
            ptr3 = e2(nc.psum_tensor("ptr3", [128, D], BF16))

            def emit_w2_load(part):
                ranges = [(0, 6), (6, 12), (12, 18), (18, 22)]
                f0, f1 = ranges[part]

                def f_w2(g, f0=f0, f1=f1):
                    return g.dma_start(out=w2b[:, f0 * D:f1 * D],
                                       in_=w2c.ap()[:, f0 * D:f1 * D])
                em.op("gp", f_w2, reads=["d:w2c"], writes=[("w2b", part)],
                      sem=f"sw2{part}", dma=True)

            def emit_oin2_load(k):
                em.op("sync", lambda s, k=k: s.dma_start(
                    out=oin2[k % 2][:, :], in_=odr[(8 + k) * 128:(9 + k) * 128, :]),
                    reads=[("d:osp", 8 + q) for q in range(8)], writes=[("oin2", k % 2)],
                    sem=f"soi{k % 2}", dma=True)

            def emit_xin2_load(k):
                em.op("gp", lambda g, k=k: g.dma_start(
                    out=xin2[k % 2][:, :], in_=x_h[(8 + k) * 128:(9 + k) * 128, :]),
                    writes=[("xin2", k % 2)], sem=f"sxi{k % 2}", dma=True)

            def emit_c1_s1(k):
                qq = 8 + k
                em.op("act", lambda s, k=k, qq=qq: s.activation(
                    out=at2[k % 4][:, :], in_=oin2[k % 2][:, :], func=AF.Identity,
                    scale=lrcp[:, qq:qq + 1]),
                    reads=[("oin2", k % 2), ("g:lrcpq", qq)], writes=[("at2", k % 4)])
                em.op("dve", lambda v, k=k: v.bn_stats(
                    out=bn4[k][:, 0:6], in_=at2[k % 4][:, 0:512]),
                    reads=[("at2", k % 4)], writes=[("bn4", k)], inc=False)
                em.op("dve", lambda v, k=k: v.bn_stats(
                    out=bn4[k][:, 6:12], in_=at2[k % 4][:, 512:1024]),
                    reads=[("at2", k % 4)], writes=[("bn4", k)])
                em.op("dve", lambda v, k=k: v.bn_aggr(out=mv4[k][:, :], in_=bn4[k][:, :]),
                      reads=[("bn4", k)], writes=[("mv4", k)])

            def emit_c1_s2(k):
                em.op("act", lambda s, k=k: s.activation(
                    out=sd4[k][:, :], in_=mv4[k][:, 1:2], func=AF.Sqrt, bias=epsb[:, :]),
                    reads=[("mv4", k), "g:epsb"], writes=[("sd4", k)])
                em.op("dve", lambda v, k=k: v.reciprocal(out=rs4[k][:, :], in_=sd4[k][:, :]),
                      reads=[("sd4", k)], writes=[("rs4", k)])
                em.op("dve", lambda v, k=k: v.tensor_scalar(
                    out=nm4[k][:, :], in0=mv4[k][:, 0:1], scalar1=rs4[k][:, :],
                    scalar2=-1.0, op0=ALU.mult, op1=ALU.mult),
                    reads=[("mv4", k), ("rs4", k)], writes=[("nm4", k)])

            def emit_c1_s3(k):
                qq = 8 + k
                em.op("act", lambda s, k=k: s.activation(
                    out=xnu2[k % 4][:, :], in_=at2[k % 4][:, :], func=AF.Identity,
                    bias=nm4[k][:, :], scale=rs4[k][:, :]),
                    reads=[("at2", k % 4), ("nm4", k), ("rs4", k)],
                    writes=[("xnu2", k % 4)])
                em.op("gp", lambda g_, k=k: g_.tensor_add(
                    out=xnu2[k % 4][:, :], in0=xnu2[k % 4][:, :], in1=xin2[k % 2][:, :]),
                    reads=[("xnu2", k % 4), ("xin2", k % 2)], writes=[("xnu2", k % 4)])
                em.op("gp", lambda g_, k=k, qq=qq: g_.dma_start(
                    out=xsp[qq * 128:(qq + 1) * 128, :], in_=xnu2[k % 4][:, :]),
                    reads=[("xnu2", k % 4)], writes=[("d:xsp", qq)],
                    sem=f"ssp{k % 4}", dma=True)
                if k + 2 < 8:
                    emit_xin2_load(k + 2)

            def emit_c1_s4(k):
                em.op("dve", lambda v, k=k: v.bn_stats(
                    out=bn4[k][:, 0:6], in_=xnu2[k % 4][:, 0:512]),
                    reads=[("xnu2", k % 4)], writes=[("bn4", k)], inc=False)
                em.op("dve", lambda v, k=k: v.bn_stats(
                    out=bn4[k][:, 6:12], in_=xnu2[k % 4][:, 512:1024]),
                    reads=[("xnu2", k % 4)], writes=[("bn4", k)])
                em.op("dve", lambda v, k=k: v.bn_aggr(out=mv4[k][:, :], in_=bn4[k][:, :]),
                      reads=[("bn4", k)], writes=[("mv4", k)])

            def emit_c1_s5(k):
                em.op("act", lambda s, k=k: s.activation(
                    out=sd4[k][:, :], in_=mv4[k][:, 1:2], func=AF.Sqrt, bias=epsb[:, :]),
                    reads=[("mv4", k), "g:epsb"], writes=[("sd4", k)])
                em.op("dve", lambda v, k=k: v.reciprocal(out=rs4[k][:, :], in_=sd4[k][:, :]),
                      reads=[("sd4", k)], writes=[("rs4", k)])
                em.op("dve", lambda v, k=k: v.tensor_scalar(
                    out=ln2c[k % 2][:, :], in0=xnu2[k % 4][:, :], scalar1=mv4[k][:, 0:1],
                    scalar2=rs4[k][:, :], op0=ALU.subtract, op1=ALU.mult),
                    reads=[("xnu2", k % 4), ("mv4", k), ("rs4", k)],
                    writes=[("ln2c", k % 2)])

            def emit_c1_s6(k):
                qq = 8 + k

                def f_tr(t, k=k):
                    r = None
                    for hh in range(ND):
                        r = t.transpose(out=ptr3[:, hh * 128:(hh + 1) * 128],
                                        in_=ln2c[k % 2][:, hh * 128:(hh + 1) * 128],
                                        identity=ident[:, :])
                    return r
                em.op("pe", f_tr, reads=[("ln2c", k % 2), "g:ident"], writes=["ptr3"])

                def f_cp(v, qq=qq):
                    o3 = ln2T[:, :].rearrange("p (hh t) -> p hh t", hh=ND)[
                        :, :, qq * 128:(qq + 1) * 128]
                    i3 = ptr3[:, :].rearrange("p (hh t) -> p hh t", hh=ND)
                    return v.tensor_copy(out=o3, in_=i3)
                em.op("dve", f_cp, reads=["ptr3"],
                      writes=[("g:ln2T", hh, qq) for hh in range(ND)])

            C1_SEGS = (emit_c1_s1, emit_c1_s2, emit_c1_s3, emit_c1_s4,
                       emit_c1_s5, emit_c1_s6)
            seg_sched = {}
            for k_ in range(8):
                for jseg in range(6):
                    seg_sched.setdefault(2 * k_ + 3 * jseg, []).append((k_, jseg))

            emit_oin2_load(0)
            emit_oin2_load(1)
            emit_xin2_load(0)
            emit_xin2_load(1)

            def emit_xnw_load(tt):
                j = tt % 2

                def f(g, tt=tt, j=j):
                    return g.dma_start(out=xnw[j][:, :], in_=xsp[tt * 128:(tt + 1) * 128, :])
                em.op("gp", f, reads=[("d:xsp", tt)], writes=[("xnw", j)],
                      sem=f"sxw{j}", dma=True)

            def emit_g_group(q, ff, gidx):
                j = gidx % 2
                jg = gidx % 2

                def f_gate(t, j=j, q=q):
                    r = None
                    for hh in range(ND):
                        r = t.matmul(out=psg[:, :],
                                     lhsT=w1s[j][:, hh * 128:(hh + 1) * 128],
                                     rhs=ln2T[:, hh * T + q * 512: hh * T + (q + 1) * 512],
                                     start=(hh == 0), stop=(hh == ND - 1))
                    return r
                em.op("pe", f_gate,
                      reads=[("g:w1s", j)] + [("g:ln2T", hh, i) for hh in range(ND)
                                              for i in range(q * 4, q * 4 + 4)],
                      writes=["psg"])

                em.op("act", lambda s, jg=jg: s.activation(
                    out=gsl[jg][:, :], in_=psg[:, :], func=AF.Silu),
                    reads=["psg"], writes=[("gsl", jg)])

                def f_up(t, j=j, q=q, jg=jg):
                    r = None
                    for hh in range(ND):
                        r = t.matmul(out=psu[jg][:, :],
                                     lhsT=w3s[j][:, hh * 128:(hh + 1) * 128],
                                     rhs=ln2T[:, hh * T + q * 512: hh * T + (q + 1) * 512],
                                     start=(hh == 0), stop=(hh == ND - 1))
                    return r
                em.op("pe", f_up,
                      reads=[("g:w3s", j)] + [("g:ln2T", hh, i) for hh in range(ND)
                                              for i in range(q * 4, q * 4 + 4)],
                      writes=[("psu", jg)])

                def f_h(v, ff=ff, q=q, jg=jg):
                    return v.tensor_mul(
                        out=h_q[q % 2][:, ff * 512:(ff + 1) * 512],
                        in0=gsl[jg][:, :], in1=psu[jg][:, :])
                em.op("dve", f_h, reads=[("gsl", jg), ("psu", jg)],
                      writes=[("h", q, ff)])

                if gidx + 2 < 4 * NF:
                    emit_w13_load(gidx + 2)
                if gidx in (3, 7, 11, 15):
                    emit_w2_load((gidx - 3) // 4)
                for (k_, jseg) in seg_sched.get(gidx, ()):
                    C1_SEGS[jseg](k_)
                    if jseg == 0 and k_ + 2 < 8:
                        emit_oin2_load(k_ + 2)

            def emit_d_group(tt):
                q = tt // 4
                jd = tt % 2
                j = tt % 2

                def f_down(t, tt=tt, q=q, jd=jd):
                    r = None
                    for ff in range(NF):
                        lhsT = h_q[q % 2][:, ff * 512 + (tt % 4) * 128:
                                          ff * 512 + (tt % 4 + 1) * 128]
                        t.matmul(out=psd[jd][:, 0:512], lhsT=lhsT,
                                 rhs=w2b[:, ff * D: ff * D + 512],
                                 start=(ff == 0), stop=(ff == NF - 1))
                        r = t.matmul(out=psd[jd][:, 512:1024], lhsT=lhsT,
                                     rhs=w2b[:, ff * D + 512: (ff + 1) * D],
                                     start=(ff == 0), stop=(ff == NF - 1))
                    return r
                em.op("pe", f_down,
                      reads=[("h", q, ff) for ff in range(NF)]
                      + [("w2b", p) for p in range(4)],
                      writes=[("psd", jd)])

                em.op("dve", lambda v, jd=jd, j=j: v.bn_stats(out=bn3[j][:, 0:6], in_=psd[jd][:, 0:512]),
                      reads=[("psd", jd)], writes=[("bn3", j)], inc=False)
                em.op("dve", lambda v, jd=jd, j=j: v.bn_stats(out=bn3[j][:, 6:12], in_=psd[jd][:, 512:1024]),
                      reads=[("psd", jd)], writes=[("bn3", j)])
                em.op("dve", lambda v, j=j: v.bn_aggr(out=mv3[j][:, :], in_=bn3[j][:, :]),
                      reads=[("bn3", j)], writes=[("mv3", j)])
                em.op("act", lambda s, j=j: s.activation(out=sd3[j][:, :], in_=mv3[j][:, 1:2],
                                                         func=AF.Sqrt, bias=epsb[:, :]),
                      reads=[("mv3", j), "g:epsb"], writes=[("sd3", j)])
                em.op("dve", lambda v, j=j: v.reciprocal(out=rs3[j][:, :], in_=sd3[j][:, :]),
                      reads=[("sd3", j)], writes=[("rs3", j)])
                em.op("dve", lambda v, j=j: v.tensor_scalar(
                    out=nm3[j][:, :], in0=mv3[j][:, 0:1], scalar1=rs3[j][:, :],
                    scalar2=-1.0, op0=ALU.mult, op1=ALU.mult),
                    reads=[("mv3", j), ("rs3", j)], writes=[("nm3", j)])
                em.op("act", lambda s, jd=jd, j=j: s.activation(
                    out=otp[j][:, :], in_=psd[jd][:, :], func=AF.Identity,
                    bias=nm3[j][:, :], scale=rs3[j][:, :]),
                    reads=[("psd", jd), ("nm3", j), ("rs3", j)], writes=[("otp", j)])
                em.op("dve", lambda v, j=j: v.tensor_add(
                    out=otp[j][:, :], in0=otp[j][:, :], in1=xnw[j][:, :]),
                    reads=[("otp", j), ("xnw", j)], writes=[("otp", j)])
                em.op("sync", lambda s, tt=tt, j=j: s.dma_start(
                    out=out_h[tt * 128:(tt + 1) * 128, :], in_=otp[j][:, :]),
                    reads=[("otp", j)], writes=[("d:out", tt)], sem=f"so{j}", dma=True)
                if tt + 2 < NT:
                    emit_xnw_load(tt + 2)

            emit_xnw_load(0)
            emit_xnw_load(1)

            gidx = 0
            for q in (0, 1):
                for ff in range(NF):
                    emit_g_group(q, ff, gidx)
                    gidx += 1
            for tt in range(0, 4):
                emit_d_group(tt)
            for q in (2,):
                for ff in range(NF):
                    emit_g_group(q, ff, gidx)
                    gidx += 1
            for tt in range(4, 8):
                emit_d_group(tt)
            for q in (3,):
                for ff in range(NF):
                    emit_g_group(q, ff, gidx)
                    gidx += 1
            for tt in range(8, 16):
                emit_d_group(tt)

        waited = {e: {} for e in ENGINES}
        attr_of = {"pe": "tensor", "act": "scalar", "dve": "vector",
                   "gp": "gpsimd", "sync": "sync"}

        def emit_block(prog):
            with nc.Block() as block:
                for eng_name in ENGINES:
                    ops = prog[eng_name]
                    if not ops:
                        continue

                    def section(eng, ops=ops, eng_name=eng_name):
                        w = waited[eng_name]
                        for waits, fn, mile, amt in ops:
                            for s_name, v in sorted(waits.items()):
                                if w.get(s_name, 0) < v:
                                    eng.wait_ge(sems[s_name], v)
                                    w[s_name] = v
                            inst = fn(eng)
                            if mile is not None:
                                inst.then_inc(sems[mile[0]], amt)

                    getattr(block, attr_of[eng_name])(section)

        for prog in em.blocks:
            emit_block(prog)

        f0, f1 = em.cnt["so0"], em.cnt["so1"]
        with nc.Block() as block:
            @block.sync
            def _(sync):
                sync.wait_ge(sems["so0"], f0)
                sync.wait_ge(sems["so1"], f1)

    return nc


_NC_CACHE = None


def kernel(L0_attn_inputs, L0_attn_Wq, L0_attn_bq, L0_ffn_w1, L0_ffn_w3, L0_ffn_w2):
    global _NC_CACHE
    x = np.ascontiguousarray(np.asarray(L0_attn_inputs, dtype=np.float32).reshape(S, D))
    wq = np.ascontiguousarray(np.asarray(L0_attn_Wq, dtype=np.float32))
    bq = np.ascontiguousarray(np.asarray(L0_attn_bq, dtype=np.float32).reshape(ND, 128).T)
    w1 = np.asarray(L0_ffn_w1, dtype=np.float32)
    w3 = np.asarray(L0_ffn_w3, dtype=np.float32)
    w2 = np.asarray(L0_ffn_w2, dtype=np.float32)
    ident = np.eye(128, dtype=ml_dtypes.bfloat16)

    if _NC_CACHE is None:
        _NC_CACHE = build_program()
    nc = _NC_CACHE

    in_maps = []
    for c in range(NCORES):
        in_maps.append({
            "x": np.ascontiguousarray(x[c * T:(c + 1) * T]),
            "wq": wq,
            "bq": bq,
            "w1": np.ascontiguousarray(w1[c]),
            "w3": np.ascontiguousarray(w3[c]),
            "w2": np.ascontiguousarray(w2[c]),
            "ident": ident,
        })
    res = run_bass_kernel_spmd(nc, in_maps, core_ids=list(range(NCORES)))
    out = np.concatenate([np.asarray(res.results[c]["out"]) for c in range(NCORES)], axis=0)
    return out.reshape(1, S, D).astype(np.float32)
